# version 06
# baseline (speedup 1.0000x reference)
import numpy as np
import contextlib

import concourse.bass as bass
import concourse.mybir as mybir
from concourse.bass_utils import run_bass_kernel_spmd

F32 = mybir.dt.float32
BF16 = mybir.dt.bfloat16
I32 = mybir.dt.int32
AF = mybir.ActivationFunctionType
OP = mybir.AluOpType

T = 2048
TG = 512
NTG = T // TG
NTT = T // 128
D = 1024
NDC = D // 128
LN_EPS = 1e-5

L0_DO_FFN = True
L0_DO_ATTN = True
DO_LAYERS = [0, 1, 2, 3]
LAUNCH_GROUPS = None
DEBUG_TAPS = False
DEBUG_RESULTS = {}


class Prog:
    ENGS = ("t", "v", "s", "g", "y")
    ESEM = {"t": "pe", "v": "dve", "s": "act", "g": "wg", "y": "wy"}

    def __init__(self, nc, stack):
        self.nc = nc
        self.stack = stack
        self.ops = {e: [] for e in self.ENGS}
        self.sems = {}
        self.touched = {e: False for e in self.ENGS}

    def sem(self, name):
        if name not in self.sems:
            self.sems[name] = [self.stack.enter_context(self.nc.semaphore(name)), 0]

    def val(self, name):
        return self.sems[name][1]

    def op(self, eng, build, incs=()):
        assert len(incs) <= 1, "hardware: one sem update per instruction"
        pairs = []
        for s, n in incs:
            rec = self.sems[s]
            rec[1] += n
            pairs.append((rec[0], n))
        self.touched[eng] = True

        def fn(e, build=build, pairs=pairs):
            inst = build(e)
            for h, n in pairs:
                inst.then_inc(h, n)
        self.ops[eng].append(fn)

    def wait(self, eng, sem, val=None):
        h, c = self.sems[sem]
        v = c if val is None else val
        if v <= 0:
            return
        self.ops[eng].append(lambda e, h=h, v=v: e.wait_ge(h, v))

    def join(self, engs=None):
        engs = self.ENGS if engs is None else engs
        vals = [(s, self.val(s)) for s in ("pe", "dve", "act", "wg", "wy", "ws", "gp", "epi")
                if s in self.sems and self.val(s) > 0]
        for e in engs:
            own = self.ESEM.get(e)
            for s, v in vals:
                if s != own:
                    self.wait(e, s, v)

    def emit(self, block):
        for key, meth in (("t", block.tensor), ("v", block.vector),
                          ("s", block.scalar), ("g", block.gpsimd),
                          ("y", block.sync)):
            fns = self.ops[key]
            if fns:
                def body(e, fns=tuple(fns)):
                    for f in fns:
                        f(e)
                meth(body)
        self.ops = {e: [] for e in self.ENGS}


def dma(P, q, out_fn, in_fn, sem):
    P.op(q, lambda e: e.dma_start(out=out_fn(), in_=in_fn()), incs=((sem, 16),))


def _host_constants():
    c = {}
    c["c_ident"] = np.eye(128, dtype=np.float32)
    c["c_tri"] = np.tril(np.ones((128, 128), dtype=np.float32)).T.copy()
    masks = np.zeros((4, 128, 512), dtype=np.float32)
    p = np.arange(128)[:, None]
    j = np.arange(512)[None, :]
    for i, delta in enumerate((0, 128, 256, 384)):
        masks[i] = ((p + delta) <= j).astype(np.float32)
    c["c_masks"] = masks.reshape(4 * 128, 512).copy()
    inv_freq = (1.0 / 10000.0 ** (np.arange(0, 32, 2, dtype=np.float32) / 32)).astype(np.float32)
    t = np.arange(T, dtype=np.float32)
    freqs = np.outer(t, inv_freq).astype(np.float32)
    c["c_cos0"] = np.cos(freqs).astype(np.float32)
    c["c_sin0"] = np.sin(freqs).astype(np.float32)
    ids = np.arange(512, dtype=np.float32)
    with np.errstate(under="ignore"):
        theta = (np.float32(10000.0) ** (np.float32(-2.0) * ids)).astype(np.float32) / np.float32(512.0)
    r = (t[:, None] * theta[None, :]).astype(np.float32)
    c["c_cos1"] = np.cos(r).astype(np.float32)
    c["c_sin1"] = np.sin(r).astype(np.float32)
    c["c_gidx"] = (np.arange(16)[None, :] * 128 + np.arange(128)[:, None]).astype(np.float32)
    ind4 = np.zeros((128, 8), dtype=np.float32)
    ind4r = np.zeros((128, 8), dtype=np.float32)
    for cc in range(2):
        ind4[0:64, cc * 4 + 2 * cc] = 1.0
        ind4[64:128, cc * 4 + 2 * cc + 1] = 1.0
        ind4r[0:32, cc * 4 + 2 * cc] = 1.0
        ind4r[64:96, cc * 4 + 2 * cc + 1] = 1.0
    c["c_ind4"] = ind4
    c["c_ind4r"] = ind4r
    ind8c = np.zeros((128, 32), dtype=np.float32)
    for cc in range(4):
        ind8c[0:64, cc * 8 + 2 * cc] = 1.0
        ind8c[64:128, cc * 8 + 2 * cc + 1] = 1.0
    c["c_ind8c"] = ind8c
    esel = np.zeros((16, 16 * 128), dtype=np.float32)
    for e in range(16):
        esel[e, e * 128:(e + 1) * 128] = 1.0
    c["c_esel16"] = esel
    esel8 = np.zeros((8, 8 * 128), dtype=np.float32)
    for e in range(8):
        esel8[e, e * 128:(e + 1) * 128] = 1.0
    c["c_esel8"] = esel8
    return c


class Build:

    def __init__(self, layers):
        self.layers = layers
        self.nc = bass.Bass("TRN2", target_bir_lowering=False, debug=False)
        self.inputs = {}
        self.outputs = {}
        self.stack = contextlib.ExitStack()
        self.uid = 0

    def hbm_in(self, name, shape, dtype=F32):
        if name not in self.inputs:
            self.inputs[name] = self.nc.dram_tensor(name, list(shape), dtype, kind="ExternalInput").ap()
        return self.inputs[name]

    def hbm_out(self, name, shape, dtype=F32):
        if name not in self.outputs:
            self.outputs[name] = self.nc.dram_tensor(name, list(shape), dtype, kind="ExternalOutput").ap()
        return self.outputs[name]

    def tap(self, name, ap_fn, shape):
        if not DEBUG_TAPS:
            return
        P = self.P
        ap0 = ap_fn()
        out = self.hbm_out("tap_" + name, list(shape), ap0.dtype)
        for sem in ("pe", "dve", "act", "wg", "epi"):
            P.wait("y", sem, P.val(sem))
        dma(P, "y", lambda: out[:, :] if len(shape) == 2 else out[:], ap_fn, "xq")
        xv = P.val("xq")
        for e in ("t", "v", "s", "g"):
            P.wait(e, "xq", xv)

    def nm(self, base):
        self.uid += 1
        return "%s_%d" % (base, self.uid)

    def build(self):
        nc = self.nc
        st = self.stack
        with st:
            P = Prog(nc, st)
            for s in ("pe", "act", "dve", "gp", "wg", "wy", "ws", "epi", "xq"):
                P.sem(s)
            self.P = P

            def sb(name, shape, dt=F32):
                return st.enter_context(nc.sbuf_tensor(self.nm(name), list(shape), dt))

            def psm(name, shape=(128, 512), dt=F32):
                return st.enter_context(nc.psum_tensor(self.nm(name), list(shape), dt))
            self.sb_outer, self.ps_outer = sb, psm

            self.XR = sb("XR", [128, NDC * T])
            self.XN = sb("XN", [128, NDC * T], BF16)
            self.MR = sb("MR", [65, T])
            self.ONESP = sb("ONESP", [65, 128])
            self.ONES1B = sb("ONES1B", [1, 128], BF16)
            self.ONESC = sb("ONESC", [128, 1], BF16)
            self.IDENT = sb("IDENT", [128, 128], BF16)
            self.MASKS = sb("MASKS", [128, 4 * 512], BF16)
            self.SCR = sb("SCR", [128, NDC * TG + 512])
            self.CPOOL = sb("CPOOL", [128, 16])
            self._consts = {}
            for v in (1e-5, 1e-6):
                self._consts[v] = len(self._consts)
                P.op("v", lambda e, j=self._consts[v], v=v:
                     e.memset(self.CPOOL[:, j:j + 1], float(v)), incs=(("dve", 1),))
            self.PSB = [psm("PSB") for _ in range(2)]

            xin = self.hbm_in("xin", [D, T])
            dma(P, "y", lambda: self.XR[:].rearrange("p (c t) -> p c t", c=NDC),
                lambda: xin.rearrange("(c p) t -> p c t", p=128), "wy")
            ident_h = self.hbm_in("c_ident", (128, 128))
            masks_h = self.hbm_in("c_masks", (4 * 128, 512))
            dma(P, "g", lambda: self.IDENT[:], lambda: ident_h[:, :], "wg")
            dma(P, "g", lambda: self.MASKS[:].rearrange("p (i n) -> p i n", i=4),
                lambda: masks_h.rearrange("(i p) n -> p i n", p=128), "wg")
            P.op("v", lambda e: e.memset(self.ONESP[:], 0.0))
            P.op("v", lambda e: e.memset(self.ONESP[0:1, :], 1.0))
            P.op("v", lambda e: e.memset(self.ONESP[64:65, :], 1.0))
            P.op("v", lambda e: e.memset(self.ONES1B[:], 1.0))
            P.op("v", lambda e: e.memset(self.ONESC[:], 1.0), incs=(("dve", 1),))
            P.join()

            self.MU = self.MR[0:1, :]
            self.SQR = self.MR[32:33, :]
            self.RR = self.MR[64:65, :]
            for li in self.layers:
                getattr(self, "layer%d" % li)()

            xout = self.hbm_out("xout", [D, T])
            P.join(("y",))
            dma(P, "y", lambda: xout.rearrange("(c p) t -> p c t", p=128),
                lambda: self.XR[:].rearrange("p (c t) -> p c t", c=NDC), "xq")
            P.wait("y", "xq", P.val("xq"))

            with nc.Block() as block:
                P.emit(block)
        return nc

    def scr_f32(self, c0, n, parts=128):
        return self.SCR[0:parts, c0:c0 + n]

    def scr_bf16(self, c0_f32, n_bf, parts=128):
        b = self.SCR[0:parts, :].bitcast(BF16)
        return b[:, 2 * c0_f32: 2 * c0_f32 + n_bf]

    def const_col(self, value, parts=128):
        j = self._consts[value]
        return self.CPOOL[0:parts, j:j + 1]

    def xr_ap(self, c, tg=None, n=TG):
        if tg is None:
            return self.XR[:, c * T:(c + 1) * T]
        return self.XR[:, c * T + tg * n: c * T + tg * n + n]

    def xn_ap(self, c, tg=None, n=TG):
        if tg is None:
            return self.XN[:, c * T:(c + 1) * T]
        return self.XN[:, c * T + tg * n: c * T + tg * n + n]

    def layernorm(self, xn_out, eps=LN_EPS):
        P, nc = self.P, self.nc
        murow, sqrow, rrow = self.MU, self.SQR, self.RR
        for tg in range(NTG):
            xbf = self.scr_bf16(0, NDC * TG)
            xsq = self.scr_bf16(NDC * TG // 2, NDC * TG)
            xr3 = self.XR[:].rearrange("p (c t) -> p c t", c=NDC)[:, :, tg * TG:(tg + 1) * TG]
            xbf3 = xbf.rearrange("p (c t) -> p c t", c=NDC)
            xsq3 = xsq.rearrange("p (c t) -> p c t", c=NDC)
            P.wait("v", "pe", P.val("pe"))
            P.wait("s", "pe", P.val("pe"))
            P.op("v", lambda e, a=xbf3, b=xr3: e.tensor_copy(a, b), incs=(("dve", 1),))
            P.op("s", lambda e, a=xsq3, b=xr3: e.activation(a, b, AF.Square), incs=(("act", 1),))
            P.wait("t", "dve", P.val("dve"))
            P.wait("t", "act", P.val("act"))
            for which, src, dst in ((0, xbf, murow), (1, xsq, sqrow)):
                bank = self.PSB[which]
                def mms(e, src=src, bank=bank):
                    inst = None
                    for c in range(NDC):
                        inst = e.matmul(bank[0:1, 0:512], self.ONESC[:],
                                        src[:, c * TG:(c + 1) * TG],
                                        start=(c == 0), stop=(c == NDC - 1))
                    return inst
                P.op("t", mms, incs=(("pe", 1),))
                P.wait("s", "pe", P.val("pe"))
                P.op("s", lambda e, bank=bank, dst=dst, tg=tg:
                     e.activation(dst[:, tg * TG:(tg + 1) * TG], bank[0:1, 0:512],
                                  AF.Identity, scale=1.0 / 1024.0), incs=(("act", 1),))
        P.wait("v", "act", P.val("act"))
        musq = self.SCR[32:33, 0:T]
        P.op("v", lambda e: e.tensor_mul(musq, murow, murow), incs=(("dve", 1),))
        P.wait("v", "dve", P.val("dve"))
        P.op("v", lambda e: e.tensor_sub(sqrow, sqrow, musq), incs=(("dve", 1),))
        P.wait("s", "dve", P.val("dve"))
        P.op("s", lambda e: e.activation(sqrow, sqrow, AF.Ln,
                                         bias=self.CPOOL[32:33, self._consts[eps]:self._consts[eps] + 1]))
        P.op("s", lambda e: e.activation(rrow, sqrow, AF.Exp, scale=-0.5),
             incs=(("act", 1),))
        P.wait("t", "act", P.val("act"))
        for tg in range(NTG):
            self.bcast_tg(tg)
            xr3 = self.XR[:].rearrange("p (c t) -> p c t", c=NDC)[:, :, tg * TG:(tg + 1) * TG]
            scr3 = self.scr_f32(0, NDC * TG).rearrange("p (c t) -> p c t", c=NDC)
            xn3 = xn_out[:].rearrange("p (c t) -> p c t", c=NDC)[:, :, tg * TG:(tg + 1) * TG]
            mu_b = self.PSB[0][:, :].rearrange("p t -> p () t").broadcast_to([128, NDC, TG])
            r_b = self.PSB[1][:, :].rearrange("p t -> p () t").broadcast_to([128, NDC, TG])
            P.op("v", lambda e, a=scr3, b=xr3, m=mu_b: e.tensor_tensor(a, b, m, op=OP.subtract))
            P.op("v", lambda e, a=xn3, b=scr3, r=r_b: e.tensor_tensor(a, b, r, op=OP.mult),
                 incs=(("dve", 1),))
        P.join()

    def bcast_tg(self, tg):
        P = self.P
        P.wait("t", "dve", P.val("dve"))

        def k1(e, tg=tg):
            e.matmul(self.PSB[0][:, :], self.ONESP[0:1, :],
                     self.MU[:, tg * TG:(tg + 1) * TG], start=True, stop=True)
            return e.matmul(self.PSB[1][:, :], self.ONESP[64:65, :],
                            self.RR[:, tg * TG:(tg + 1) * TG], start=True, stop=True)
        P.op("t", k1, incs=(("pe", 1),))
        P.wait("v", "pe", P.val("pe"))

    def xn32_into(self, dst_ap, c, tg):
        P = self.P
        xr = self.xr_ap(c, tg)
        P.op("v", lambda e: e.tensor_tensor(dst_ap, xr, self.PSB[0][:, :], op=OP.subtract))
        P.op("v", lambda e: e.tensor_tensor(dst_ap, dst_ap, self.PSB[1][:, :], op=OP.mult),
             incs=(("dve", 1),))

    def layer0(self):
        self.layernorm(self.XN)
        if L0_DO_FFN:
            self.l0_ffn()
        if L0_DO_ATTN:
            self.l0_attn()

    def l0_ffn(self):
        P, nc = self.P, self.nc
        S = 1.0 / 127.0
        wg_h = self.hbm_in("L0_ffn_w_gate_t", [D, 3072], I32)
        wu_h = self.hbm_in("L0_ffn_w_up_t", [D, 3072], I32)
        wd_h = self.hbm_in("L0_ffn_w_down_t", [3072, D], I32)
        NH = 24
        with contextlib.ExitStack() as sc:
            H = sc.enter_context(nc.sbuf_tensor(self.nm("l0_h"), [128, NH * 1024], BF16))
            WB = [sc.enter_context(nc.sbuf_tensor(self.nm("l0_wb"), [128, NDC * 128], BF16))
                  for _ in range(4)]
            WD = [sc.enter_context(nc.sbuf_tensor(self.nm("l0_wd"), [128, NH * 128], BF16))
                  for _ in range(2)]
            PSG = [sc.enter_context(nc.psum_tensor(self.nm("l0psg"), [128, 512], F32))
                   for _ in range(2)]
            PSU = [sc.enter_context(nc.psum_tensor(self.nm("l0psu"), [128, 512], F32))
                   for _ in range(2)]
            for th in range(2):
                pe_after_fc = {}
                h_done = {}
                for fc in range(NH):
                    bsel = fc % 2
                    if fc >= 2:
                        P.wait("g", "pe", pe_after_fc[fc - 2])
                    for w_hbm, buf in ((wg_h, WB[bsel * 2]), (wu_h, WB[bsel * 2 + 1])):
                        dma(P, "g", lambda b=buf: b[:].rearrange("p (k m) -> p k m", m=128),
                            lambda w=w_hbm, fc=fc: w[:, fc * 128:(fc + 1) * 128]
                            .rearrange("(k p) m -> p k m", p=128), "wg")
                    wg_val = P.val("wg")
                    for sub in range(2):
                        tg = th * 2 + sub
                        P.wait("t", "wg", wg_val)
                        if (fc - 1, sub) in h_done:
                            P.wait("t", "dve", h_done[(fc - 1, sub)])

                        def mms(e, fc=fc, tg=tg, bsel=bsel, sub=sub):
                            for kc in range(NDC):
                                e.matmul(PSG[sub][:, :], WB[bsel * 2][:, kc * 128:(kc + 1) * 128],
                                         self.xn_ap(kc, tg), start=(kc == 0), stop=(kc == NDC - 1))
                            inst = None
                            for kc in range(NDC):
                                inst = e.matmul(PSU[sub][:, :], WB[bsel * 2 + 1][:, kc * 128:(kc + 1) * 128],
                                                self.xn_ap(kc, tg), start=(kc == 0), stop=(kc == NDC - 1))
                            return inst
                        P.op("t", mms, incs=(("pe", 1),))
                        pe_after_fc[fc] = P.val("pe")
                        P.wait("s", "pe", P.val("pe"))
                        hslice = H[:, fc * 1024 + sub * TG: fc * 1024 + sub * TG + TG]
                        P.op("s", lambda e, hs=hslice, sub=sub:
                             e.activation(hs, PSG[sub][:, :], AF.Silu, scale=S),
                             incs=(("act", 1),))
                        P.wait("v", "act", P.val("act"))
                        P.op("v", lambda e, hs=hslice, sub=sub:
                             e.tensor_tensor(hs, hs, PSU[sub][:, :], op=OP.mult),
                             incs=(("dve", 1),))
                        h_done[(fc, sub)] = P.val("dve")
                h_ready = P.val("dve")
                epi_base = P.val("epi")
                pe_after_mc = {}
                for mc in range(NDC):
                    bsel = mc % 2
                    if mc >= 2:
                        P.wait("g", "pe", pe_after_mc[mc - 2])
                    dma(P, "g", lambda b=WD[bsel]: b[:].rearrange("p (k m) -> p k m", m=128),
                        lambda mc=mc: wd_h[:, mc * 128:(mc + 1) * 128]
                        .rearrange("(k p) m -> p k m", p=128), "wg")
                    wg_val = P.val("wg")
                    for sub in range(2):
                        tg = th * 2 + sub
                        idx = mc * 2 + sub
                        P.wait("t", "wg", wg_val)
                        if idx == 0:
                            P.wait("t", "dve", h_ready)
                        if idx >= 2:
                            P.wait("t", "epi", epi_base + idx - 1)

                        def mms(e, sub=sub, bsel=bsel):
                            inst = None
                            for kc in range(NH):
                                inst = e.matmul(PSG[sub][:, :], WD[bsel][:, kc * 128:(kc + 1) * 128],
                                                H[:, kc * 1024 + sub * TG: kc * 1024 + sub * TG + TG],
                                                start=(kc == 0), stop=(kc == NH - 1))
                            return inst
                        P.op("t", mms, incs=(("pe", 1),))
                        pe_after_mc[mc] = P.val("pe")
                        P.wait("v", "pe", P.val("pe"))
                        xs = self.xr_ap(mc, tg)
                        P.op("v", lambda e, xs=xs, sub=sub:
                             e.scalar_tensor_tensor(xs, PSG[sub][:, :], S * S, xs,
                                                    op0=OP.mult, op1=OP.add),
                             incs=(("epi", 1),))
                P.wait("t", "epi", epi_base + 16)
            P.join()


    def load_w_bf16(self, sc, hbm_ap, rows, cols, row0=0, col0=0, name="w", hbm_pre=None):
        P, nc = self.P, self.nc
        nkc = rows // 128
        wsb = sc.enter_context(nc.sbuf_tensor(self.nm(name), [128, nkc * cols], BF16))

        def src():
            ap = hbm_ap if hbm_pre is None else hbm_pre(hbm_ap)
            return ap[row0:row0 + rows, col0:col0 + cols].rearrange("(k p) m -> p k m", p=128)
        dma(P, "g", lambda: wsb[:].rearrange("p (k m) -> p k m", m=cols), src, "wg")
        wv = P.val("wg")

        def tile(kc, c0=0, nc_=cols):
            return wsb[:, kc * cols + c0: kc * cols + c0 + nc_]
        return wsb, tile, wv

    def load_col_f32_shaped(self, sc, in_name, shape, name="bcol"):
        P, nc = self.P, self.nc
        hbm_ap = self.hbm_in(in_name, shape)
        csb = sc.enter_context(nc.sbuf_tensor(self.nm(name), list(shape), F32))
        dma(P, "g", lambda: csb[:], lambda: hbm_ap[:, :], "wg")
        return csb, P.val("wg")

    def load_row_bf16(self, sc, hbm_ap, n, name="brow"):
        P, nc = self.P, self.nc
        rsb = sc.enter_context(nc.sbuf_tensor(self.nm(name), [1, n], BF16))
        dma(P, "g", lambda: rsb[0:1, :], lambda: hbm_ap.rearrange("n -> () n"), "wg")
        return rsb, P.val("wg")

    def load_col_f32(self, sc, in_name, n, name="bcol"):
        P, nc = self.P, self.nc
        ncc = n // 128
        hbm_ap = self.hbm_in(in_name + "_pf", (128, ncc))
        csb = sc.enter_context(nc.sbuf_tensor(self.nm(name), [128, ncc], F32))
        dma(P, "g", lambda: csb[:], lambda: hbm_ap[:, :], "wg")
        return csb, P.val("wg")

    def attn_core(self, *, heads, parts, v_slot_fn, negm_ap_fn, scale, causal,
                  out_ap_fn, qgs, ST, PV, PT, araw, recr):
        P = self.P
        pend = [None]
        exp_done, pv_done, gate = {}, {}, {}
        jj = [0]

        def flush_pending():
            if pend[0] is not None:
                fn = pend[0]
                pend[0] = None
                fn()

        for qg in qgs:
            kts = list(range(min(NTT, 4 * qg + 4))) if causal else list(range(NTT))
            np_ = len(kts) // 2
            for h in heads:
                j0 = jj[0]

                def emit_st(lj, h=h, qg=qg, kts=kts, j0=j0):
                    j = j0 + lj
                    s2 = j % 2
                    if j >= 2:
                        P.wait("t", "act", exp_done[j - 2])
                    def st_mms(e, lj=lj, s2=s2):
                        inst = None
                        for half in range(2):
                            kt = kts[2 * lj + half]
                            for pi, (kfn, qfn, kdim) in enumerate(parts):
                                inst = e.matmul(ST[s2][:, half * 512:(half + 1) * 512],
                                                kfn(h, kt), qfn(h, qg),
                                                start=(pi == 0), stop=(pi == len(parts) - 1))
                        return inst
                    P.op("t", st_mms, incs=(("pe", 1),))
                    P.wait("s", "pe", P.val("pe"))
                    if j >= 3:
                        P.wait("s", "pe", pv_done[j - 3])
                    s3 = j % 3
                    P.op("s", lambda e, s2=s2, s3=s3, h=h:
                         e.activation(PT[s3][:, :], ST[s2][:, :], AF.Exp,
                                      bias=negm_ap_fn(h), scale=scale),
                         incs=(("act", 1),))
                    exp_done[j] = P.val("act")
                    need_mask = causal and kts[2 * lj + 1] >= 4 * qg
                    if need_mask:
                        P.wait("v", "act", exp_done[j])
                        def mask_ops(e, lj=lj, s3=s3, qg=qg, kts=kts):
                            inst = None
                            for half in range(2):
                                kt = kts[2 * lj + half]
                                if kt >= 4 * qg:
                                    mi = kt - 4 * qg
                                    inst = e.tensor_tensor(
                                        PT[s3][:, half * 512:(half + 1) * 512],
                                        PT[s3][:, half * 512:(half + 1) * 512],
                                        self.MASKS[:, mi * 512:(mi + 1) * 512],
                                        op=OP.mult)
                            return inst
                        P.op("v", mask_ops, incs=(("dve", 1),))
                        gate[j] = ("dve", P.val("dve"))
                    else:
                        gate[j] = ("act", exp_done[j])

                def emit_pv(lj, h=h, first=False, last=False, kts=kts, j0=j0):
                    j = j0 + lj
                    sem, v = gate[j]
                    P.wait("t", sem, v)
                    def pv_mms(e, lj=lj, s3=j % 3, first=first, last=last):
                        inst = None
                        for half in range(2):
                            inst = e.matmul(PV[0:65, :], v_slot_fn(h, kts[2 * lj + half]),
                                            PT[s3][:, half * 512:(half + 1) * 512],
                                            start=(first and half == 0),
                                            stop=(last and half == 1))
                        return inst
                    P.op("t", pv_mms, incs=(("pe", 1),))
                    pv_done[j] = P.val("pe")

                emit_st(0)
                flush_pending()
                for lj in range(1, np_):
                    emit_st(lj)
                    emit_pv(lj - 1, first=(lj == 1), last=False)
                if np_ == 1:
                    emit_pv(0, first=True, last=True)
                else:
                    emit_pv(np_ - 1, first=False, last=True)
                jj[0] = j0 + np_

                P.wait("s", "pe", pv_done[j0 + np_ - 1])
                P.op("s", lambda e: e.copy(araw[0:65, :], PV[0:65, :]), incs=(("act", 1),))
                if getattr(self, "_tap_araw", False):
                    self._tap_araw = False
                    self.tap("araw", lambda: araw[0:65, :], (65, 512))
                evac_act = P.val("act")
                P.wait("v", "act", evac_act)
                P.op("v", lambda e: e.reciprocal(recr[0:1, :], araw[64:65, :]),
                     incs=(("dve", 1),))
                recip_dve = P.val("dve")

                def normalize(h=h, qg=qg, recip_dve=recip_dve):
                    P.wait("t", "dve", recip_dve)
                    P.op("t", lambda e: e.matmul(self.PSB[0][0:64, :], self.ONESP[0:1, 0:64],
                                                 recr[0:1, :], start=True, stop=True),
                         incs=(("pe", 1),))
                    P.wait("v", "pe", P.val("pe"))
                    P.op("v", lambda e, h=h, qg=qg:
                         e.tensor_tensor(out_ap_fn(h, qg), araw[0:64, :],
                                         self.PSB[0][0:64, :], op=OP.mult),
                         incs=(("dve", 1),))
                    P.wait("s", "dve", P.val("dve"))
                pend[0] = normalize
        flush_pending()
        P.wait("t", "dve", P.val("dve"))

    def rope_apply_l0(self, src_f32, dst_bf, cos_ap, sin_ap, nheads, scr):
        P = self.P
        a = src_f32.rearrange("p (h f) -> p h f", f=32)[:, :, 0:16]
        b = src_f32.rearrange("p (h f) -> p h f", f=32)[:, :, 16:32]
        oa = dst_bf.rearrange("p (h f) -> p h f", f=32)[:, :, 0:16]
        ob = dst_bf.rearrange("p (h f) -> p h f", f=32)[:, :, 16:32]
        cosb = cos_ap.rearrange("p f -> p () f").broadcast_to([128, nheads, 16])
        sinb = sin_ap.rearrange("p f -> p () f").broadcast_to([128, nheads, 16])
        m1 = scr[:, 0:nheads * 16].rearrange("p (h f) -> p h f", f=16)
        m2 = scr[:, nheads * 16:nheads * 32].rearrange("p (h f) -> p h f", f=16)
        P.op("v", lambda e: e.tensor_tensor(m1, a, cosb, op=OP.mult))
        P.op("v", lambda e: e.tensor_tensor(m2, b, sinb, op=OP.mult))
        P.op("v", lambda e: e.tensor_tensor(oa, m1, m2, op=OP.subtract))
        P.op("v", lambda e: e.tensor_tensor(m1, b, cosb, op=OP.mult))
        P.op("v", lambda e: e.tensor_tensor(m2, a, sinb, op=OP.mult))
        P.op("v", lambda e: e.tensor_tensor(ob, m1, m2, op=OP.add), incs=(("dve", 1),))

    def l0_attn(self):
        P, nc = self.P, self.nc
        GH = 4
        NHG = 16 // GH
        NCH = GH // 2
        scale = 1.0 / float(np.sqrt(96.0))
        W_kv = self.hbm_in("L0_attn_W_kv", (D, 256))
        W_kdec = self.hbm_in("L0_attn_W_kdec", (256, D))
        W_vdec = self.hbm_in("L0_attn_W_vdec", (256, D))
        W_q = self.hbm_in("L0_attn_W_q", (D, 512))
        W_qdec = self.hbm_in("L0_attn_W_qdec", (512, D))
        W_krope = self.hbm_in("L0_attn_W_krope", (D, 512))
        W_qrope = self.hbm_in("L0_attn_W_qrope", (512, 512))
        W_o = self.hbm_in("L0_attn_W_o", (D, D))
        cos0_h = self.hbm_in("c_cos0", (T, 16))
        sin0_h = self.hbm_in("c_sin0", (T, 16))
        ind4_h = self.hbm_in("c_ind4", (128, 8))
        ind4r_h = self.hbm_in("c_ind4r", (128, 8))

        with contextlib.ExitStack() as sc:
            sbl = lambda n, shp, dt=F32: sc.enter_context(nc.sbuf_tensor(self.nm(n), list(shp), dt))
            KVN = sbl("kvn", [128, 2 * T], BF16)
            QCQG = sbl("qcqg", [128, 4 * TG], BF16)
            COS0 = sbl("cos0", [128, NTT * 16])
            SIN0 = sbl("sin0", [128, NTT * 16])
            IND4 = sbl("ind4", [128, 8], BF16)
            IND4R = sbl("ind4r", [128, 8], BF16)
            dma(P, "g", lambda: COS0[:].rearrange("p (t f) -> p t f", f=16),
                lambda: cos0_h.rearrange("(t p) f -> p t f", p=128), "wg")
            dma(P, "g", lambda: SIN0[:].rearrange("p (t f) -> p t f", f=16),
                lambda: sin0_h.rearrange("(t p) f -> p t f", p=128), "wg")
            dma(P, "g", lambda: IND4[:], lambda: ind4_h[:, :], "wg")
            dma(P, "g", lambda: IND4R[:], lambda: ind4r_h[:, :], "wg")
            gkv_sb, gkv_wv = self.load_col_f32(sc, "L0_attn_g_kv", 256, "gkv")

            with contextlib.ExitStack() as s2:
                wkv, wkv_t, wkv_wv = self.load_w_bf16(s2, W_kv, D, 256, name="wkv")
                kvs = s2.enter_context(nc.sbuf_tensor(self.nm("kvscr"), [128, 2 * 512], F32))
                sqb = s2.enter_context(nc.sbuf_tensor(self.nm("kvsq"), [128, 2 * 512], BF16))
                srow = s2.enter_context(nc.sbuf_tensor(self.nm("kvr"), [1, 512], F32))
                P.wait("t", "wg", wkv_wv)
                for tg in range(NTG):
                    P.wait("t", "epi", P.val("epi"))
                    for mc in range(2):
                        def mms(e, mc=mc, tg=tg):
                            inst = None
                            for kc in range(NDC):
                                inst = e.matmul(self.PSB[mc][:, :], wkv_t(kc, mc * 128, 128),
                                                self.xn_ap(kc, tg), start=(kc == 0),
                                                stop=(kc == NDC - 1))
                            return inst
                        P.op("t", mms, incs=(("pe", 1),))
                    pev = P.val("pe")
                    P.wait("s", "pe", pev)
                    P.op("s", lambda e: e.copy(kvs[:, 0:512], self.PSB[0][:, :]))
                    P.op("s", lambda e: e.copy(kvs[:, 512:1024], self.PSB[1][:, :]))
                    P.op("s", lambda e: e.activation(sqb[:, 0:512], self.PSB[0][:, :], AF.Square))
                    P.op("s", lambda e: e.activation(sqb[:, 512:1024], self.PSB[1][:, :], AF.Square),
                         incs=(("act", 1),))
                    P.wait("t", "act", P.val("act"))
                    def smm(e):
                        e.matmul(self.PSB[0][0:1, 0:512], self.ONESC[:], sqb[:, 0:512],
                                 start=True, stop=False)
                        return e.matmul(self.PSB[0][0:1, 0:512], self.ONESC[:], sqb[:, 512:1024],
                                        start=False, stop=True)
                    P.op("t", smm, incs=(("pe", 1),))
                    P.wait("s", "pe", P.val("pe"))
                    P.op("s", lambda e: e.activation(srow[0:1, :], self.PSB[0][0:1, 0:512],
                                                     AF.Identity, scale=1.0 / 256.0),
                         incs=(("act", 1),))
                    P.wait("s", "act", P.val("act"))
                    P.op("s", lambda e: e.activation(srow[0:1, :], srow[0:1, :], AF.Ln,
                                                     bias=self.const_col(1e-6, parts=1)))
                    P.op("s", lambda e: e.activation(srow[0:1, :], srow[0:1, :], AF.Exp,
                                                     scale=-0.5), incs=(("act", 1),))
                    P.wait("t", "act", P.val("act"))
                    P.op("t", lambda e: e.matmul(self.PSB[0][:, :], self.ONESP[0:1, :],
                                                 srow[0:1, :], start=True, stop=True),
                         incs=(("pe", 1),))
                    P.wait("v", "pe", P.val("pe"))
                    P.wait("v", "wg", gkv_wv)
                    for mc in range(2):
                        dst = KVN[:, mc * T + tg * TG: mc * T + (tg + 1) * TG]
                        src = kvs[:, mc * 512:(mc + 1) * 512]
                        P.op("v", lambda e, s=src: e.tensor_tensor(s, s, self.PSB[0][:, :],
                                                                   op=OP.mult))
                        P.op("v", lambda e, d=dst, s=src, mc=mc:
                             e.tensor_scalar(d, s, gkv_sb[:, mc:mc + 1], None, op0=OP.mult),
                             incs=(("epi", 1),))
                    P.wait("t", "epi", P.val("epi"))
                P.join()

            wq, wq_t, wq_wv = self.load_w_bf16(sc, W_q, D, 512, name="wq")
            P.wait("t", "wg", wq_wv)

            for hg in range(NHG):
                with contextlib.ExitStack() as s2:
                    sb2 = lambda n, shp, dt=F32: s2.enter_context(nc.sbuf_tensor(self.nm(n), list(shp), dt))
                    KC = sb2("kc", [128, NCH * T], BF16)
                    KRF = sb2("krf", [128, NCH * T], BF16)
                    VT = sb2("vt", [128, NTT * GH * 65], BF16)
                    QCF = sb2("qcf", [128, NCH * TG], BF16)
                    QRF = sb2("qrf", [128, NCH * TG], BF16)
                    SQB = self.scr_bf16(0, 2 * NCH * TG)
                    ARAW = self.scr_f32(2048, 512)
                    ROPS = self.scr_f32(2560, GH * 32)
                    ROPS2 = self.scr_f32(2816, GH * 32)
                    ROPB = self.scr_bf16(3072, GH * 32)
                    AOUT = self.scr_bf16(3328, NCH * TG)
                    KMAX = sb2("kmax", [128, 4])
                    NEGMB = sb2("negmb", [128, 1], BF16)
                    NEGM = sb2("negm", [128, GH])
                    NROW = sb2("nrow", [1, GH], BF16)
                    RECR = sb2("recr", [1, 512])
                    ST = [s2.enter_context(nc.psum_tensor(self.nm("st"), [128, 1024], F32))
                          for _ in range(2)]
                    PVP = s2.enter_context(nc.psum_tensor(self.nm("pv"), [128, 512], F32))
                    PTR = [s2.enter_context(nc.psum_tensor(self.nm("ptr"), [128, 128], BF16))
                           for _ in range(1)]
                    PT = [sb2("pt", [128, 1024], BF16) for _ in range(3)]

                    ph_a = contextlib.ExitStack()
                    wkd, wkd_t, _ = self.load_w_bf16(ph_a, W_kdec, 256, GH * 64, col0=hg * GH * 64, name="wkd")
                    wvd, wvd_t, _ = self.load_w_bf16(ph_a, W_vdec, 256, GH * 64, col0=hg * GH * 64, name="wvd")
                    wkr, wkr_t, wv_all = self.load_w_bf16(ph_a, W_krope, D, GH * 32, col0=hg * GH * 32, name="wkr")
                    P.wait("t", "wg", wv_all)

                    stpool = [ST[0][:, 0:512], ST[0][:, 512:1024],
                              ST[1][:, 0:512], ST[1][:, 512:1024]]
                    self.gemm_fm_to(nkc=2, nmc=NCH,
                                    xin_fn=lambda kc, tg: KVN[:, kc * T + tg * TG: kc * T + (tg + 1) * TG],
                                    wtile_fn=lambda kc, mc: wkd_t(kc, mc * 128, 128),
                                    dst_fn=lambda mc, tg: KC[:, mc * T + tg * TG: mc * T + (tg + 1) * TG],
                                    pool=stpool)

                    P.op("v", lambda e: e.memset(VT[:], 1.0), incs=(("dve", 1),))
                    P.wait("s", "dve", P.val("dve"))
                    for t2 in range(0, NTT, 4):
                        if t2 > 0:
                            P.wait("t", "act", P.val("act"))
                        bks = [stpool[(t2 + k) % 4] for k in range(4)]
                        def vmm(e, t2=t2, bks=bks):
                            inst = None
                            for k in range(4):
                                tt = t2 + k
                                for kc in range(2):
                                    inst = e.matmul(bks[k][:, 0:GH * 64],
                                                    KVN[:, kc * T + tt * 128: kc * T + (tt + 1) * 128],
                                                    wvd_t(kc), start=(kc == 0), stop=(kc == 1))
                            return inst
                        P.op("t", vmm, incs=(("pe", 1),))
                        P.wait("s", "pe", P.val("pe"))
                        def vevac(e, t2=t2, bks=bks):
                            inst = None
                            for k in range(4):
                                tt = t2 + k
                                vdst = VT[:, tt * GH * 65:(tt + 1) * GH * 65].rearrange(
                                    "p (h f) -> p h f", f=65)[:, :, 0:64]
                                inst = e.copy(vdst, bks[k][:, 0:GH * 64]
                                              .rearrange("p (h f) -> p h f", f=64))
                            return inst
                        P.op("s", vevac, incs=(("act", 1),))
                    P.join(("t", "v", "s"))

                    P.op("v", lambda e: e.memset(KRF[:], 0.0), incs=(("dve", 1),))
                    P.wait("t", "dve", P.val("dve"))
                    for tt in range(NTT):
                        sl = tt % 2
                        self._rope_cw = T
                        def krmm(e, tt=tt, sl=sl):
                            inst = None
                            for kc in range(NDC):
                                inst = e.matmul(self.PSB[sl][:, 0:GH * 32],
                                                self.XN[:, kc * T + tt * 128: kc * T + (tt + 1) * 128],
                                                wkr_t(kc), start=(kc == 0), stop=(kc == NDC - 1))
                            return inst
                        self.rope_tm_block(krmm, tt, sl, KRF, ROPS, ROPS2, ROPB, COS0, SIN0,
                                           PTR, nheads=GH)
                    P.join()
                    ph_a.close()
                    wqd, wqd_t, _ = self.load_w_bf16(s2, W_qdec, 512, GH * 64, col0=hg * GH * 64, name="wqd")
                    wqr, wqr_t, _ = self.load_w_bf16(s2, W_qrope, 512, GH * 32, col0=hg * GH * 32, name="wqr")
                    wob, wob_t, wv_b = self.load_w_bf16(s2, W_o, NCH * 128, D,
                                                        row0=hg * NCH * 128, name="wo")
                    P.wait("t", "wg", wv_b)

                    for qg in range(NTG):
                        P.wait("t", "act", P.val("act"))
                        self.gemm_fm_to(nkc=NDC, nmc=4,
                                        xin_fn=lambda kc, tg, qg=qg: self.xn_ap(kc, qg),
                                        wtile_fn=lambda kc, mc: wq_t(kc, mc * 128, 128),
                                        dst_fn=lambda mc, tg: QCQG[:, mc * TG:(mc + 1) * TG],
                                        tgs=(0,), pool=stpool)
                        self.gemm_fm_to(nkc=4, nmc=NCH,
                                        xin_fn=lambda kc, tg: QCQG[:, kc * TG:(kc + 1) * TG],
                                        wtile_fn=lambda kc, mc: wqd_t(kc, mc * 128, 128),
                                        dst_fn=lambda mc, tg: QCF[:, mc * TG:(mc + 1) * TG],
                                        tgs=(0,), pool=stpool)
                        P.op("v", lambda e: e.memset(QRF[:], 0.0), incs=(("dve", 1),))
                        P.wait("t", "dve", P.val("dve"))
                        for lt in range(4):
                            tt = qg * 4 + lt
                            sl = lt % 2
                            self._rope_cw = TG
                            def qrmm(e, lt=lt, sl=sl):
                                inst = None
                                for kc in range(4):
                                    inst = e.matmul(self.PSB[sl][:, 0:GH * 32],
                                                    QCQG[:, kc * TG + lt * 128: kc * TG + (lt + 1) * 128],
                                                    wqr_t(kc), start=(kc == 0), stop=(kc == 3))
                                return inst
                            self.rope_tm_block(qrmm, tt, sl, QRF, ROPS, ROPS2, ROPB, COS0, SIN0,
                                               PTR, nheads=GH, dst_tt=lt)
                        P.join(("t", "v", "s"))

                        self.l0_negm(qg, GH, NCH, KC, KRF, QCF, QRF, SQB, IND4, IND4R, KMAX,
                                     NEGM, NROW, NEGMB, ST, PTR, scale, first=(qg == 0))

                        if hg == 0 and qg == 0:
                            self.tap("negm", lambda: NEGM[:, :], (128, 4))
                            self.tap("kmax", lambda: KMAX[:, :], (128, 4))
                            self.tap("kc0", lambda: KC[:, 0:512], (128, 512))
                            self.tap("krf0", lambda: KRF[:, 0:512], (128, 512))
                            self.tap("qcf0", lambda: QCF[:, 0:512], (128, 512))
                            self.tap("qrf0", lambda: QRF[:, 0:512], (128, 512))
                            self.tap("vt0", lambda: VT[:, 0:520], (128, 520))
                            self.tap("kvn0", lambda: KVN[:, 0:512], (128, 512))

                        def kc_fn(h, kt):
                            ch, ro = h // 2, 64 * (h % 2)
                            return KC[ro:ro + 64, ch * T + kt * 128: ch * T + (kt + 1) * 128]

                        def qc_fn(h, qg_):
                            ch, ro = h // 2, 64 * (h % 2)
                            return QCF[ro:ro + 64, ch * TG:(ch + 1) * TG]

                        def kr_fn(h, kt):
                            ch, ro = h // 2, 64 * (h % 2)
                            return KRF[ro:ro + 32, ch * T + kt * 128: ch * T + (kt + 1) * 128]

                        def qr_fn(h, qg_):
                            ch, ro = h // 2, 64 * (h % 2)
                            return QRF[ro:ro + 32, ch * TG:(ch + 1) * TG]

                        def v_fn(h, kt):
                            return VT[:, kt * GH * 65 + h * 65: kt * GH * 65 + (h + 1) * 65]

                        def out_fn(h, qg_):
                            ch, ro = h // 2, 64 * (h % 2)
                            return AOUT[ro:ro + 64, ch * TG:(ch + 1) * TG]

                        self._tap_araw = DEBUG_TAPS and hg == 0 and qg == 0
                        self.attn_core(heads=range(GH),
                                       parts=[(kc_fn, qc_fn, 64), (kr_fn, qr_fn, 32)],
                                       v_slot_fn=v_fn,
                                       negm_ap_fn=lambda h: NEGM[:, h:h + 1],
                                       scale=scale, causal=True,
                                       out_ap_fn=out_fn, qgs=(qg,),
                                       ST=ST, PV=PVP, PT=PT, araw=ARAW, recr=RECR)

                        epi0 = P.val("epi")
                        for mc in range(NDC):
                            sl = mc % 2
                            if mc >= 2:
                                P.wait("t", "epi", epi0 + mc - 1)
                            def wmm(e, mc=mc, sl=sl):
                                inst = None
                                for kc in range(NCH):
                                    inst = e.matmul(self.PSB[sl][:, :], wob_t(kc, mc * 128, 128),
                                                    AOUT[:, kc * TG:(kc + 1) * TG],
                                                    start=(kc == 0), stop=(kc == NCH - 1))
                                return inst
                            P.op("t", wmm, incs=(("pe", 1),))
                            P.wait("v", "pe", P.val("pe"))
                            xs = self.xr_ap(mc, qg)
                            P.op("v", lambda e, xs=xs, sl=sl:
                                 e.tensor_add(xs, xs, self.PSB[sl][:, :]),
                                 incs=(("epi", 1),))
                        P.wait("t", "epi", epi0 + NDC)
                    P.join()
            P.join()

    def rope_tm_block(self, mm_fn, tt, sl, DSTF, ROPS, ROPS2, ROPB, COS, SIN, PTR,
                      nheads=4, dst_tt=None):
        P = self.P
        nw = nheads * 32
        dtt = tt if dst_tt is None else dst_tt
        P.wait("t", "dve", P.val("dve"))
        P.op("t", mm_fn, incs=(("pe", 1),))
        P.wait("s", "pe", P.val("pe"))
        P.op("s", lambda e, sl=sl, nw=nw: e.copy(ROPS[:, 0:nw], self.PSB[sl][:, 0:nw]),
             incs=(("act", 1),))
        P.wait("v", "act", P.val("act"))
        self.rope_apply_l0(ROPS[:, 0:nw], ROPB[:, 0:nw],
                           COS[:, tt * 16:(tt + 1) * 16], SIN[:, tt * 16:(tt + 1) * 16],
                           nheads, ROPS2)
        cw = self._rope_cw
        for bp in range(nheads // 2):
            psl = 0
            P.wait("t", "dve", P.val("dve"))
            P.op("t", lambda e, bp=bp, psl=psl:
                 e.transpose(PTR[0][0:64, :], ROPB[:, bp * 64:(bp + 1) * 64], self.IDENT[:]),
                 incs=(("pe", 1),))
            P.wait("v", "pe", P.val("pe"))
            P.op("v", lambda e, bp=bp, psl=psl, cw=cw, dtt=dtt:
                 e.tensor_copy(DSTF[0:32, bp * cw + dtt * 128: bp * cw + (dtt + 1) * 128],
                               PTR[0][0:32, :]))
            P.op("v", lambda e, bp=bp, psl=psl, cw=cw, dtt=dtt:
                 e.tensor_copy(DSTF[64:96, bp * cw + dtt * 128: bp * cw + (dtt + 1) * 128],
                               PTR[0][32:64, :]), incs=(("dve", 1),))

    def l0_negm(self, qg, GH, NCH, KC, KRF, QCF, QRF, SQB, IND, INDR, KMAX, NEGM,
                NROW, NEGMB, ST, PTR, scale, first):
        P = self.P

        def sq_and_norms(src_c, src_r, width, nt):
            sqc = SQB[:, 0:NCH * nt]
            sqr = SQB[:, NCH * nt:2 * NCH * nt]
            P.op("s", lambda e: e.activation(sqc.rearrange("p (c t) -> p c t", c=NCH),
                                             src_c, AF.Square))
            P.op("s", lambda e: e.activation(sqr.rearrange("p (c t) -> p c t", c=NCH),
                                             src_r, AF.Square), incs=(("act", 1),))
            P.wait("t", "act", P.val("act"))
            def nmm(e):
                for c in range(NCH):
                    e.matmul(ST[1][0:GH, 0:nt], IND[:, c * GH:(c + 1) * GH],
                             sqc[:, c * nt:(c + 1) * nt], start=(c == 0), stop=False)
                inst = None
                for c in range(NCH):
                    inst = e.matmul(ST[1][0:GH, 0:nt], INDR[:, c * GH:(c + 1) * GH],
                                    sqr[:, c * nt:(c + 1) * nt],
                                    start=False, stop=(c == NCH - 1))
                return inst
            P.op("t", nmm, incs=(("pe", 1),))
            P.wait("v", "pe", P.val("pe"))

        if first:
            P.op("v", lambda e: e.memset(KMAX[0:GH, 0:1], 0.0), incs=(("dve", 1),))
            P.wait("s", "dve", P.val("dve"))
            for tg in range(NTG):
                kc3 = KC[:].rearrange("p (c t) -> p c t", c=NCH)[:, :, tg * TG:(tg + 1) * TG]
                kr3 = KRF[:].rearrange("p (c t) -> p c t", c=NCH)[:, :, tg * TG:(tg + 1) * TG]
                sq_and_norms(kc3, kr3, T, TG)
                P.op("v", lambda e: e.reduce_max(KMAX[0:GH, 1:2], ST[1][0:GH, 0:TG],
                                                 axis=mybir.AxisListType.X), incs=(("dve", 1),))
                P.wait("v", "dve", P.val("dve"))
                P.op("v", lambda e: e.tensor_max(KMAX[0:GH, 0:1], KMAX[0:GH, 0:1],
                                                 KMAX[0:GH, 1:2]), incs=(("dve", 1),))
                P.wait("t", "dve", P.val("dve"))
                P.wait("s", "dve", P.val("dve"))
        qc3 = QCF[:].rearrange("p (c t) -> p c t", c=NCH)
        qr3 = QRF[:].rearrange("p (c t) -> p c t", c=NCH)
        P.wait("s", "dve", P.val("dve"))
        sq_and_norms(qc3, qr3, TG, TG)
        if DEBUG_TAPS and first and not getattr(self, "_negm_tapped", False):
            self._negm_tapped = True
            P.wait("s", "pe", P.val("pe"))
            P.op("s", lambda e: e.copy(self.scr_f32(2048, 512, parts=8)[0:8, :],
                                       ST[1][0:8, 0:512]), incs=(("act", 1),))
            self.tap("qnormps", lambda: self.scr_f32(2048, 512, parts=8)[0:8, :], (8, 512))
            self.tap("sqb", lambda: SQB[:, :], (128, 2048))
        P.op("v", lambda e: e.reduce_max(KMAX[0:GH, 2:3], ST[1][0:GH, 0:TG],
                                         axis=mybir.AxisListType.X), incs=(("dve", 1),))
        P.wait("v", "dve", P.val("dve"))
        P.op("v", lambda e: e.tensor_mul(KMAX[0:GH, 1:2], KMAX[0:GH, 2:3], KMAX[0:GH, 0:1]),
             incs=(("dve", 1),))
        P.wait("s", "dve", P.val("dve"))
        P.op("s", lambda e: e.activation(KMAX[0:GH, 2:3], KMAX[0:GH, 1:2], AF.Ln))
        P.op("s", lambda e: e.activation(KMAX[0:GH, 2:3], KMAX[0:GH, 2:3], AF.Exp, scale=0.5),
             incs=(("act", 1),))
        P.wait("v", "act", P.val("act"))
        P.op("v", lambda e, sc=-scale: e.tensor_scalar_mul(NEGMB[0:GH, 0:1],
                                                           KMAX[0:GH, 2:3], sc),
             incs=(("dve", 1),))
        P.wait("t", "dve", P.val("dve"))
        P.op("t", lambda e: e.transpose(PTR[0][0:1, 0:GH],
                                        NEGMB[0:GH, 0:1], self.IDENT[0:GH, 0:GH]),
             incs=(("pe", 1),))
        P.wait("s", "pe", P.val("pe"))
        P.op("s", lambda e: e.copy(NROW[0:1, :], PTR[0][0:1, 0:GH]), incs=(("act", 1),))
        P.wait("t", "act", P.val("act"))
        P.op("t", lambda e: e.matmul(ST[1][:, 0:GH], self.ONES1B[:], NROW[0:1, :],
                                     start=True, stop=True), incs=(("pe", 1),))
        P.wait("s", "pe", P.val("pe"))
        P.op("s", lambda e: e.copy(NEGM[:, :], ST[1][:, 0:GH]), incs=(("act", 1),))
        P.wait("t", "act", P.val("act"))
        P.wait("v", "act", P.val("act"))

    def negm_content(self, GH, NCH, k3_fn, q3, IND, KMAX, NEGM, NROW, NEGMB, PSN,
                     PTR, SQB, scale, first):
        P = self.P

        def norms(src3, nt):
            sqc = SQB[:, 0:NCH * nt]
            P.op("s", lambda e: e.activation(sqc.rearrange("p (c t) -> p c t", c=NCH),
                                             src3, AF.Square), incs=(("act", 1),))
            P.wait("t", "act", P.val("act"))
            def nmm(e):
                inst = None
                for c in range(NCH):
                    inst = e.matmul(PSN[0:GH, 0:nt], IND[:, c * GH:(c + 1) * GH],
                                    sqc[:, c * nt:(c + 1) * nt],
                                    start=(c == 0), stop=(c == NCH - 1))
                return inst
            P.op("t", nmm, incs=(("pe", 1),))
            P.wait("v", "pe", P.val("pe"))

        if first:
            P.op("v", lambda e: e.memset(KMAX[0:GH, 0:1], 0.0), incs=(("dve", 1),))
            P.wait("s", "dve", P.val("dve"))
            for tg in range(NTG):
                norms(k3_fn(tg), TG)
                P.op("v", lambda e: e.reduce_max(KMAX[0:GH, 1:2], PSN[0:GH, 0:TG],
                                                 axis=mybir.AxisListType.X), incs=(("dve", 1),))
                P.wait("v", "dve", P.val("dve"))
                P.op("v", lambda e: e.tensor_max(KMAX[0:GH, 0:1], KMAX[0:GH, 0:1],
                                                 KMAX[0:GH, 1:2]), incs=(("dve", 1),))
                P.wait("t", "dve", P.val("dve"))
                P.wait("s", "dve", P.val("dve"))
        P.wait("s", "dve", P.val("dve"))
        norms(q3, TG)
        P.op("v", lambda e: e.reduce_max(KMAX[0:GH, 2:3], PSN[0:GH, 0:TG],
                                         axis=mybir.AxisListType.X), incs=(("dve", 1),))
        P.wait("v", "dve", P.val("dve"))
        P.op("v", lambda e: e.tensor_mul(KMAX[0:GH, 1:2], KMAX[0:GH, 2:3], KMAX[0:GH, 0:1]),
             incs=(("dve", 1),))
        P.wait("s", "dve", P.val("dve"))
        P.op("s", lambda e: e.activation(KMAX[0:GH, 2:3], KMAX[0:GH, 1:2], AF.Ln))
        P.op("s", lambda e: e.activation(KMAX[0:GH, 2:3], KMAX[0:GH, 2:3], AF.Exp, scale=0.5),
             incs=(("act", 1),))
        P.wait("v", "act", P.val("act"))
        P.op("v", lambda e, sc=-scale: e.tensor_scalar_mul(NEGMB[0:GH, 0:1],
                                                           KMAX[0:GH, 2:3], sc),
             incs=(("dve", 1),))
        P.wait("t", "dve", P.val("dve"))
        P.op("t", lambda e: e.transpose(PTR[0][0:1, 0:GH],
                                        NEGMB[0:GH, 0:1], self.IDENT[0:GH, 0:GH]),
             incs=(("pe", 1),))
        P.wait("s", "pe", P.val("pe"))
        P.op("s", lambda e: e.copy(NROW[0:1, :], PTR[0][0:1, 0:GH]), incs=(("act", 1),))
        P.wait("t", "act", P.val("act"))
        P.op("t", lambda e: e.matmul(PSN[:, 0:GH], self.ONES1B[:], NROW[0:1, :],
                                     start=True, stop=True), incs=(("pe", 1),))
        P.wait("s", "pe", P.val("pe"))
        P.op("s", lambda e: e.copy(NEGM[:, :], PSN[:, 0:GH]), incs=(("act", 1),))
        P.wait("t", "act", P.val("act"))
        P.wait("v", "act", P.val("act"))

    def rope_tm_l1(self, RS, ROPB, cos_ap, sin_ap, npairs=512):
        P = self.P
        x0 = RS.rearrange("p (a two) -> p a two", two=2)[:, :, 0:1]
        x1 = RS.rearrange("p (a two) -> p a two", two=2)[:, :, 1:2]
        o0 = ROPB.rearrange("p (a two) -> p a two", two=2)[:, :, 0:1]
        o1 = ROPB.rearrange("p (a two) -> p a two", two=2)[:, :, 1:2]
        c3 = cos_ap.rearrange("p f -> p f ()")
        s3 = sin_ap.rearrange("p f -> p f ()")
        for h0 in range(0, npairs, 256):
            w = min(256, npairs - h0)
            m1 = self.scr_f32(2048, w).rearrange("p f -> p f ()")
            m2 = self.scr_f32(2304, w).rearrange("p f -> p f ()")
            m3 = self.scr_f32(2560, w).rearrange("p f -> p f ()")
            m4 = self.scr_f32(2816, w).rearrange("p f -> p f ()")
            sl = slice(h0, h0 + w)
            P.op("v", lambda e, sl=sl: e.tensor_tensor(m1, x0[:, sl], c3[:, sl], op=OP.mult))
            P.op("v", lambda e, sl=sl: e.tensor_tensor(m2, x1[:, sl], s3[:, sl], op=OP.mult))
            P.op("v", lambda e, sl=sl: e.tensor_tensor(m3, x1[:, sl], c3[:, sl], op=OP.mult))
            P.op("v", lambda e, sl=sl: e.tensor_tensor(m4, x0[:, sl], s3[:, sl], op=OP.mult))
            P.op("v", lambda e, sl=sl: e.tensor_tensor(o0[:, sl], m1, m2, op=OP.subtract))
            P.op("v", lambda e, sl=sl: e.tensor_tensor(o1[:, sl], m3, m4, op=OP.add),
                 incs=(("dve", 1),))


    def layer1(self):
        self.layernorm(self.XN)
        self.l1_ffn()
        self.l1_attn()

    def layer2(self):
        self.layernorm(self.XN)
        self.l2_attn()
        self.l2_ffn()

    def l2_attn(self):
        P, nc = self.P, self.nc
        inproj = self.hbm_in("L2_attn_in_proj_t", (D, 3072))
        outproj = self.hbm_in("L2_attn_out_proj_t", (D, D))
        with contextlib.ExitStack() as sc:
            AOUTF = sc.enter_context(nc.sbuf_tensor(self.nm("aoutf"), [128, NDC * T], BF16))
            self._attn_in = self.XN
            self.mha16(wq_src=(inproj, 0), wk_src=(inproj, 1024), wv_src=(inproj, 2048),
                       causal=True, scale=0.125, AOUTF=AOUTF)
            with contextlib.ExitStack() as s2:
                wo, wo_t, wv = self.load_w_bf16(s2, outproj, D, D, name="wo")
                P.wait("t", "wg", wv)
                epi0 = P.val("epi")
                idx = 0
                for mc in range(NDC):
                    for qg in range(NTG):
                        sl = idx % 2
                        if idx >= 2:
                            P.wait("t", "epi", epi0 + idx - 1)
                        def wmm(e, mc=mc, qg=qg, sl=sl):
                            inst = None
                            for kc in range(NDC):
                                inst = e.matmul(self.PSB[sl][:, :], wo_t(kc, mc * 128, 128),
                                                AOUTF[:, kc * T + qg * TG: kc * T + (qg + 1) * TG],
                                                start=(kc == 0), stop=(kc == NDC - 1))
                            return inst
                        P.op("t", wmm, incs=(("pe", 1),))
                        P.wait("v", "pe", P.val("pe"))
                        xs = self.xr_ap(mc, qg)
                        P.op("v", lambda e, xs=xs, sl=sl:
                             e.tensor_add(xs, xs, self.PSB[sl][:, :]), incs=(("epi", 1),))
                        idx += 1
                P.wait("t", "epi", epi0 + idx)
            P.join()

    def l2_ffn(self):
        P, nc = self.P, self.nc
        E, FF = 8, 2048
        Wq = self.hbm_in("L2_ffn_Wq", (D, D))
        Wk = self.hbm_in("L2_ffn_Wk", (D, D))
        Wv = self.hbm_in("L2_ffn_Wv", (D, D))
        Wo = self.hbm_in("L2_ffn_Wo", (D, D))
        gW = self.hbm_in("L2_ffn_gW", (D, E))
        W1 = self.hbm_in("L2_ffn_W1", (E, D, FF))
        W2 = self.hbm_in("L2_ffn_W2", (E, FF, D))
        b2_h = self.hbm_in("L2_ffn_b2", (E, D))
        esel_h = self.hbm_in("c_esel8", (8, 8 * 128))
        with contextlib.ExitStack() as sc:
            sbl = lambda n_, shp, dt=F32: sc.enter_context(nc.sbuf_tensor(self.nm(n_), list(shp), dt))
            ZB = sbl("zb", [128, NDC * TG], BF16)
            MR2 = sbl("mr2", [65, 512])
            bq, _ = self.load_col_f32(sc, "L2_ffn_bq", D, "bq")
            bk, _ = self.load_col_f32(sc, "L2_ffn_bk", D, "bk")
            bo, _ = self.load_col_f32(sc, "L2_ffn_bo", D, "bo")
            g1, _ = self.load_col_f32(sc, "L2_ffn_ln1_g", D, "g1")
            b1l, _ = self.load_col_f32(sc, "L2_ffn_ln1_b", D, "b1l")
            g2, _ = self.load_col_f32(sc, "L2_ffn_ln2_g", D, "g2")
            b2l, _ = self.load_col_f32(sc, "L2_ffn_ln2_b", D, "b2l")
            bv_r, _ = self.load_row_bf16(sc, self.hbm_in("L2_ffn_bv", (D,)), D, "bvr")

            with contextlib.ExitStack() as sa:
                AOUTF = sa.enter_context(nc.sbuf_tensor(self.nm("aoutf"), [128, NDC * T], BF16))
                self._attn_in = self.XN
                self.mha16(wq_src=(Wq, 0), wk_src=(Wk, 0), wv_src=(Wv, 0),
                           bq_col=bq, bk_col=bk, bv_row=bv_r,
                           causal=False, scale=0.125, AOUTF=AOUTF)
                X1 = self.XN
                with contextlib.ExitStack() as s2:
                    PSB2 = [s2.enter_context(nc.psum_tensor(self.nm("psb2"), [128, 512], F32))
                            for _ in range(2)]
                    self.PSB2 = PSB2
                    WOB = [s2.enter_context(nc.sbuf_tensor(self.nm("wob"), [128, NDC * 128], BF16))
                           for _ in range(2)]
                    wob_pe = {}
                    for tg in range(NTG):
                        zadd = {}
                        for dc in range(NDC):
                            sl = dc % 2
                            key = tg * NDC + dc
                            if key - 2 in wob_pe:
                                P.wait("g", "pe", wob_pe[key - 2])
                            dma(P, "g", lambda b=WOB[sl]: b[:].rearrange("p (k m) -> p k m", m=128),
                                lambda dc=dc: Wo[:, dc * 128:(dc + 1) * 128]
                                .rearrange("(k p) m -> p k m", p=128), "wg")
                            wv = P.val("wg")
                            P.wait("t", "wg", wv)
                            if sl in zadd:
                                P.wait("t", "dve", zadd[sl])
                            def wmm(e, dc=dc, tg=tg, sl=sl):
                                inst = None
                                for kc in range(NDC):
                                    inst = e.matmul(self.PSB2[sl][:, :],
                                                    WOB[sl][:, kc * 128:(kc + 1) * 128],
                                                    AOUTF[:, kc * T + tg * TG: kc * T + (tg + 1) * TG],
                                                    start=(kc == 0), stop=(kc == NDC - 1))
                                return inst
                            P.op("t", wmm, incs=(("pe", 1),))
                            wob_pe[key] = P.val("pe")
                            P.wait("v", "pe", P.val("pe"))
                            z = self.scr_f32(dc * TG, TG)
                            xnsl = self.XN[:, dc * T + tg * TG: dc * T + (tg + 1) * TG]
                            P.op("v", lambda e, z=z, sl=sl, xnsl=xnsl:
                                 e.tensor_tensor(z, xnsl, self.PSB2[sl][:, :], op=OP.add))
                            P.op("v", lambda e, z=z, dc=dc:
                                 e.tensor_scalar_add(z, z, bo[:, dc:dc + 1]),
                                 incs=(("dve", 1),))
                            zadd[sl] = P.val("dve")
                        self.inner_ln(tg, MR2, ZB, g1, b1l,
                                      out_fn=lambda dc, tg2: X1[:, dc * T + tg2 * TG:
                                                                dc * T + (tg2 + 1) * TG])
                    P.join()

            with contextlib.ExitStack() as sb_:
                sbb = lambda n_, shp, dt=F32: sb_.enter_context(nc.sbuf_tensor(self.nm(n_), list(shp), dt))
                SC_ = sbb("sc", [128, NTT * E])
                V12 = sbb("v12", [128, 4])
                IND1 = sbb("ind1", [128, NTT * E])
                WB = sbb("wb", [128, NTT * E], BF16)
                WSELF = sbb("wself", [8, T], BF16)
                ESEL = sbb("esel", [8, 8 * 128], BF16)
                YW = sbb("yw", [128, 32 + NTT * E])
                b2sb = sb_.enter_context(nc.sbuf_tensor(self.nm("b2sb"), [8, D], BF16))
                PTB = sb_.enter_context(nc.psum_tensor(self.nm("ptb"), [128, 128], BF16))
                dma(P, "g", lambda: ESEL[:, :], lambda: esel_h[:, :], "wg")
                dma(P, "g", lambda: b2sb[:, :], lambda: b2_h[:, :], "wg")
                wgt, wgt_t, _ = self.load_w_bf16(sb_, gW, D, E, name="wg")
                bgr, wv = self.load_row_bf16(sb_, self.hbm_in("L2_ffn_gb", (E,)), E, "bgr")
                P.wait("t", "wg", wv)
                for tt in range(NTT):
                    sl = tt % 2
                    if tt >= 2:
                        P.wait("t", "dve", P.val("dve"))
                    def gmm(e, tt=tt, sl=sl):
                        for kc in range(NDC):
                            e.matmul(self.PSB[sl][:, 0:E],
                                     X1[:, kc * T + tt * 128: kc * T + (tt + 1) * 128],
                                     wgt_t(kc), start=(kc == 0), stop=False)
                        return e.matmul(self.PSB[sl][:, 0:E], self.ONES1B[:],
                                        bgr[0:1, :], start=False, stop=True)
                    P.op("t", gmm, incs=(("pe", 1),))
                    P.wait("v", "pe", P.val("pe"))
                    P.op("v", lambda e, sl=sl: e.reduce_max(V12[:, 0:1], self.PSB[sl][:, 0:E],
                                                            axis=mybir.AxisListType.X),
                         incs=(("dve", 1),))
                    P.wait("v", "dve", P.val("dve"))
                    P.op("v", lambda e: e.tensor_scalar_mul(V12[:, 1:2], V12[:, 0:1], -1.0),
                         incs=(("dve", 1),))
                    P.wait("s", "dve", P.val("dve"))
                    P.op("s", lambda e, sl=sl, tt=tt:
                         e.activation(SC_[:, tt * E:(tt + 1) * E], self.PSB[sl][:, 0:E], AF.Exp,
                                      bias=V12[:, 1:2], accum_out=V12[:, 2:3]),
                         incs=(("act", 1),))
                    P.wait("v", "act", P.val("act"))
                    P.op("v", lambda e: e.reciprocal(V12[:, 3:4], V12[:, 2:3]), incs=(("dve", 1),))
                    P.wait("v", "dve", P.val("dve"))
                    P.op("v", lambda e, tt=tt:
                         e.tensor_scalar(SC_[:, tt * E:(tt + 1) * E], SC_[:, tt * E:(tt + 1) * E],
                                         V12[:, 3:4], None, op0=OP.mult), incs=(("dve", 1),))
                sc3 = SC_[:].rearrange("p (t e) -> p t e", e=E)
                P.wait("v", "dve", P.val("dve"))
                P.op("v", lambda e: e.reduce_max(YW[:, 0:16], sc3, axis=mybir.AxisListType.X),
                     incs=(("dve", 1),))
                P.wait("v", "dve", P.val("dve"))
                v1b = YW[:, 0:16].rearrange("p t -> p t ()").broadcast_to([128, 16, E])
                P.op("v", lambda e: e.tensor_tensor(IND1[:].rearrange("p (t e) -> p t e", e=E),
                                                    sc3, v1b, op=OP.is_ge))
                P.op("v", lambda e: e.scalar_tensor_tensor(YW[:, 32:160], IND1[:], -1e9, SC_[:],
                                                           op0=OP.mult, op1=OP.add),
                     incs=(("dve", 1),))
                P.wait("v", "dve", P.val("dve"))
                P.op("v", lambda e: e.reduce_max(YW[:, 16:32],
                                                 YW[:, 32:160].rearrange("p (t e) -> p t e", e=E),
                                                 axis=mybir.AxisListType.X), incs=(("dve", 1),))
                P.wait("v", "dve", P.val("dve"))
                v2b = YW[:, 16:32].rearrange("p t -> p t ()").broadcast_to([128, 16, E])
                P.op("v", lambda e: e.tensor_tensor(YW[:, 32:160].rearrange("p (t e) -> p t e", e=E),
                                                    sc3, v2b, op=OP.is_ge))
                P.op("v", lambda e: e.tensor_sub(YW[:, 32:160], YW[:, 32:160], IND1[:]))
                P.op("v", lambda e: e.tensor_tensor(YW[:, 32:160].rearrange("p (t e) -> p t e", e=E),
                                                    YW[:, 32:160].rearrange("p (t e) -> p t e", e=E),
                                                    v2b, op=OP.mult))
                P.op("v", lambda e: e.tensor_tensor(IND1[:].rearrange("p (t e) -> p t e", e=E),
                                                    IND1[:].rearrange("p (t e) -> p t e", e=E),
                                                    v1b, op=OP.mult))
                P.op("v", lambda e: e.tensor_add(WB[:], IND1[:], YW[:, 32:160]),
                     incs=(("dve", 1),))
                P.wait("t", "dve", P.val("dve"))
                for tt in range(NTT):
                    P.wait("t", "act", P.val("act"))
                    P.op("t", lambda e, tt=tt:
                         e.transpose(PTB[0:E, 0:128], WB[:, tt * E:(tt + 1) * E],
                                     self.IDENT[:]), incs=(("pe", 1),))
                    P.wait("s", "pe", P.val("pe"))
                    P.op("s", lambda e, tt=tt:
                         e.copy(WSELF[0:E, tt * 128:(tt + 1) * 128], PTB[0:E, 0:128]),
                         incs=(("act", 1),))
                P.join()

                PH = [sb_.enter_context(nc.psum_tensor(self.nm("ph"), [128, 512], F32))
                      for _ in range(2)]
                PD = [sb_.enter_context(nc.psum_tensor(self.nm("pd"), [128, 512], F32))
                      for _ in range(2)]
                W1B = [sb_.enter_context(nc.sbuf_tensor(self.nm("w1b"), [128, NDC * 256], BF16))
                       for _ in range(2)]
                W2B = [sb_.enter_context(nc.sbuf_tensor(self.nm("w2b"), [128, 16 * 128], BF16))
                       for _ in range(2)]
                HB = sb_.enter_context(nc.sbuf_tensor(self.nm("hb"), [128, 16 * TG], BF16))
                B1C, wvb = self.load_col_f32_shaped(sb_, "L2_ffn_b1_pf", (128, 16 * E), "b1c")
                P.wait("t", "wg", wvb)
                for tg in range(NTG):
                    for dc in range(NDC):
                        P.op("v", lambda e, dc=dc, tg=tg:
                             e.tensor_copy(self.scr_f32(dc * TG, TG),
                                           X1[:, dc * T + tg * TG: dc * T + (tg + 1) * TG]),
                             incs=((("dve", 1),) if dc == NDC - 1 else ()))
                    w1b_pe = {}
                    w2b_pe = {}
                    for ei in range(E):
                        gelu_done = {}
                        for fcg in range(8):
                            bsel = fcg % 2
                            key = ei * 8 + fcg
                            if key - 2 in w1b_pe:
                                P.wait("g", "pe", w1b_pe[key - 2])
                            dma(P, "g", lambda b=W1B[bsel]: b[:].rearrange("p (k m) -> p k m", m=256),
                                lambda ei=ei, fcg=fcg: W1[ei][:, fcg * 256:(fcg + 1) * 256]
                                .rearrange("(k p) m -> p k m", p=128), "wg")
                            wv1 = P.val("wg")
                            for lf in range(2):
                                fc = fcg * 2 + lf
                                sl = fc % 2
                                P.wait("t", "wg", wv1)
                                if sl in gelu_done:
                                    P.wait("t", "act", gelu_done[sl])
                                def hmm(e, fc=fc, lf=lf, sl=sl, tg=tg, bsel=bsel):
                                    inst = None
                                    for kc in range(NDC):
                                        inst = e.matmul(PH[sl][:, :],
                                                        W1B[bsel][:, kc * 256 + lf * 128: kc * 256 + (lf + 1) * 128],
                                                        X1[:, kc * T + tg * TG: kc * T + (tg + 1) * TG],
                                                        start=(kc == 0), stop=(kc == NDC - 1))
                                    return inst
                                P.op("t", hmm, incs=(("pe", 1),))
                                P.wait("s", "pe", P.val("pe"))
                                P.op("s", lambda e, fc=fc, sl=sl, ei=ei:
                                     e.activation(HB[:, fc * TG:(fc + 1) * TG], PH[sl][:, :],
                                                  AF.Relu,
                                                  bias=B1C[:, fc * E + ei: fc * E + ei + 1]),
                                     incs=(("act", 1),))
                                gelu_done[sl] = P.val("act")
                            w1b_pe[key] = P.val("pe")
                        P.wait("t", "dve", P.val("dve"))
                        P.op("t", lambda e, ei=ei, tg=tg:
                             e.matmul(PD[0][:, :], ESEL[:, ei * 128:(ei + 1) * 128],
                                      WSELF[0:E, tg * TG:(tg + 1) * TG],
                                      start=True, stop=True), incs=(("pe", 1),))
                        P.wait("v", "pe", P.val("pe"))
                        P.wait("v", "act", P.val("act"))
                        h3 = HB[:].rearrange("p (c t) -> p c t", c=16)
                        mb = PD[0][:, :].rearrange("p t -> p () t").broadcast_to([128, 16, TG])
                        P.op("v", lambda e, h3=h3, mb=mb:
                             e.tensor_tensor(h3, h3, mb, op=OP.mult), incs=(("dve", 1),))
                        hm_done = P.val("dve")
                        zadd = {}
                        for dc in range(NDC):
                            sl = dc % 2
                            bsel = dc % 2
                            key2 = ei * NDC + dc
                            if key2 - 2 in w2b_pe:
                                P.wait("g", "pe", w2b_pe[key2 - 2])
                            dma(P, "g", lambda b=W2B[bsel]: b[:].rearrange("p (k m) -> p k m", m=128),
                                lambda ei=ei, dc=dc: W2[ei][:, dc * 128:(dc + 1) * 128]
                                .rearrange("(k p) m -> p k m", p=128), "wg")
                            wv2 = P.val("wg")
                            P.wait("t", "wg", wv2)
                            if dc == 0:
                                P.wait("t", "dve", hm_done)
                            if sl in zadd:
                                P.wait("t", "dve", zadd[sl])
                            def dmm(e, dc=dc, sl=sl, tg=tg, ei=ei, bsel=bsel):
                                inst = None
                                for fc in range(16):
                                    inst = e.matmul(PD[sl][:, :], W2B[bsel][:, fc * 128:(fc + 1) * 128],
                                                    HB[:, fc * TG:(fc + 1) * TG],
                                                    start=(fc == 0),
                                                    stop=(fc == 15 and ei != 0))
                                if ei == 0:
                                    inst = e.matmul(PD[sl][:, :], b2sb[:, dc * 128:(dc + 1) * 128],
                                                    WSELF[0:E, tg * TG:(tg + 1) * TG],
                                                    start=False, stop=True)
                                return inst
                            P.op("t", dmm, incs=(("pe", 1),))
                            w2b_pe[key2] = P.val("pe")
                            P.wait("v", "pe", P.val("pe"))
                            z = self.scr_f32(dc * TG, TG)
                            P.op("v", lambda e, z=z, sl=sl:
                                 e.tensor_add(z, z, PD[sl][:, :]), incs=(("dve", 1),))
                            zadd[sl] = P.val("dve")
                    self.inner_ln(tg, MR2, ZB, g2, b2l, add_to_x=True)
                P.join()

    def inner_ln(self, tg, MR2, ZB, gcol, bcol, out_fn=None, add_to_x=False,
                 eps=LN_EPS):
        P, nc = self.P, self.nc
        P.wait("v", "dve", P.val("dve"))
        P.wait("v", "pe", P.val("pe"))
        P.op("v", lambda e: e.tensor_copy(ZB[:], self.scr_f32(0, NDC * TG)),
             incs=(("dve", 1),))
        P.wait("t", "dve", P.val("dve"))
        P.wait("t", "epi", P.val("epi"))
        def mms0(e):
            inst = None
            for c in range(NDC):
                inst = e.matmul(self.PSB[0][0:1, 0:512], self.ONESC[:],
                                ZB[:, c * TG:(c + 1) * TG],
                                start=(c == 0), stop=(c == NDC - 1))
            return inst
        P.op("t", mms0, incs=(("pe", 1),))
        P.wait("s", "pe", P.val("pe"))
        P.op("s", lambda e: e.activation(ZB[:], self.scr_f32(0, NDC * TG), AF.Square),
             incs=(("act", 1),))
        P.wait("t", "act", P.val("act"))
        def mms1(e):
            inst = None
            for c in range(NDC):
                inst = e.matmul(self.PSB[1][0:1, 0:512], self.ONESC[:],
                                ZB[:, c * TG:(c + 1) * TG],
                                start=(c == 0), stop=(c == NDC - 1))
            return inst
        P.op("t", mms1, incs=(("pe", 1),))
        P.wait("s", "pe", P.val("pe"))
        P.op("s", lambda e: e.activation(MR2[0:1, :], self.PSB[0][0:1, 0:512],
                                         AF.Identity, scale=1.0 / 1024.0))
        P.op("s", lambda e: e.activation(MR2[32:33, :], self.PSB[1][0:1, 0:512],
                                         AF.Identity, scale=1.0 / 1024.0),
             incs=(("act", 1),))
        P.wait("v", "act", P.val("act"))
        musq2 = self.SCR[32:33, 4096:4608]
        P.op("v", lambda e, m=musq2: e.tensor_mul(m, MR2[0:1, :], MR2[0:1, :]),
             incs=(("dve", 1),))
        P.wait("v", "dve", P.val("dve"))
        P.op("v", lambda e, m=musq2: e.tensor_sub(MR2[32:33, :], MR2[32:33, :], m),
             incs=(("dve", 1),))
        P.wait("s", "dve", P.val("dve"))
        P.op("s", lambda e: e.activation(MR2[32:33, :], MR2[32:33, :], AF.Ln,
                                         bias=self.CPOOL[32:33, self._consts[eps]:self._consts[eps] + 1]))
        P.op("s", lambda e: e.activation(MR2[64:65, :], MR2[32:33, :], AF.Exp,
                                         scale=-0.5), incs=(("act", 1),))
        P.wait("t", "act", P.val("act"))
        P.wait("t", "dve", P.val("dve"))
        def k1(e):
            e.matmul(self.PSB[0][:, :], self.ONESP[0:1, :], MR2[0:1, :],
                     start=True, stop=True)
            return e.matmul(self.PSB[1][:, :], self.ONESP[64:65, :], MR2[64:65, :],
                            start=True, stop=True)
        P.op("t", k1, incs=(("pe", 1),))
        P.wait("v", "pe", P.val("pe"))
        for dc in range(NDC):
            z = self.scr_f32(dc * TG, TG)
            P.op("v", lambda e, z=z: e.tensor_tensor(z, z, self.PSB[0][:, :],
                                                     op=OP.subtract))
            P.op("v", lambda e, z=z: e.tensor_tensor(z, z, self.PSB[1][:, :],
                                                     op=OP.mult))
            if add_to_x:
                xs = self.xr_ap(dc, tg)
                P.op("v", lambda e, z=z, dc=dc:
                     e.tensor_scalar(z, z, gcol[:, dc:dc + 1], bcol[:, dc:dc + 1],
                                     op0=OP.mult, op1=OP.add))
                P.op("v", lambda e, z=z, xs=xs: e.tensor_add(xs, xs, z),
                     incs=(("dve", 1),))
            else:
                P.op("v", lambda e, z=z, dc=dc, tg=tg:
                     e.tensor_scalar(out_fn(dc, tg), z, gcol[:, dc:dc + 1], bcol[:, dc:dc + 1],
                                     op0=OP.mult, op1=OP.add), incs=(("dve", 1),))
        P.wait("t", "dve", P.val("dve"))


    def layer3(self):
        self.layernorm(self.XN)
        self.l3_ffn()
        self.l3_attn()

    def l3_attn(self):
        P, nc = self.P, self.nc
        w_q = self.hbm_in("L3_attn_w_q", (D, D))
        w_k = self.hbm_in("L3_attn_w_k", (D, D))
        w_v = self.hbm_in("L3_attn_w_v", (D, D))
        w_u = self.hbm_in("L3_attn_w_u", (D, D))
        with contextlib.ExitStack() as sc:
            bu, _ = self.load_col_f32(sc, "L3_attn_b_u", D, "bu")
            AOUTF = sc.enter_context(nc.sbuf_tensor(self.nm("aoutf"), [128, NDC * T], BF16))
            self._attn_in = self.XN
            self.mha16(wq_src=(w_q, 0), wk_src=(w_k, 0), wv_src=(w_v, 0),
                       causal=True, scale=1.0 / 64.0, AOUTF=AOUTF)
            with contextlib.ExitStack() as s2:
                wo, wo_t, wv = self.load_w_bf16(s2, w_u, D, D, name="wo")
                P.wait("t", "wg", wv)
                epi0 = P.val("epi")
                idx = 0
                for mc in range(NDC):
                    for qg in range(NTG):
                        sl = idx % 2
                        if idx >= 2:
                            P.wait("t", "epi", epi0 + idx - 1)
                        def wmm(e, mc=mc, qg=qg, sl=sl):
                            inst = None
                            for kc in range(NDC):
                                inst = e.matmul(self.PSB[sl][:, :], wo_t(kc, mc * 128, 128),
                                                AOUTF[:, kc * T + qg * TG: kc * T + (qg + 1) * TG],
                                                start=(kc == 0), stop=(kc == NDC - 1))
                            return inst
                        P.op("t", wmm, incs=(("pe", 1),))
                        P.wait("v", "pe", P.val("pe"))
                        xs = self.xr_ap(mc, qg)
                        P.op("v", lambda e, xs=xs, sl=sl:
                             e.tensor_add(xs, xs, self.PSB[sl][:, :]))
                        P.op("v", lambda e, xs=xs, mc=mc:
                             e.tensor_scalar_add(xs, xs, bu[:, mc:mc + 1]),
                             incs=(("epi", 1),))
                        idx += 1
                P.wait("t", "epi", epi0 + idx)
            P.join()

    def l3_ffn(self):
        P, nc = self.P, self.nc
        E, FF = 8, 4096
        gwt_h = self.hbm_in("L3_ffn_gate_W_t", (D, E))
        Wg_h = self.hbm_in("L3_ffn_W_gate_t", (E, D, FF))
        Wu_h = self.hbm_in("L3_ffn_W_up_t", (E, D, FF))
        Wd_h = self.hbm_in("L3_ffn_W_down_t", (E, FF, D))
        esel_h = self.hbm_in("c_esel8", (8, 8 * 128))
        cc_in = self.nc.dram_tensor(self.nm("cc_in"), [128, 128], F32, kind="Internal").ap()
        cc_out = self.nc.dram_tensor(self.nm("cc_out"), [128, 128], F32, kind="Internal").ap()
        with contextlib.ExitStack() as sc:
            sbl = lambda n_, shp, dt=F32: sc.enter_context(nc.sbuf_tensor(self.nm(n_), list(shp), dt))
            LA = sbl("la", [128, NTT * E])
            ES = sbl("es", [128, NTT * E])
            TMP3 = sbl("tmp3", [128, NTT * E])
            IN1 = sbl("in1", [128, NTT * E])
            RMX = sbl("rmx", [128, 32])
            CSROW = sbl("csrow", [1, 128])
            CLROW = sbl("clrow", [1, 8])
            WB = sbl("wb", [128, NTT * E], BF16)
            WSELF = sbl("wself", [8, T], BF16)
            ESEL = sbl("esel", [8, 8 * 128], BF16)
            ONESCF = sbl("onescf", [128, 1])
            WGB = [sbl("wgb", [128, NDC * 512], BF16) for _ in range(2)]
            WUB = [sbl("wub", [128, NDC * 512], BF16) for _ in range(2)]
            WDB = [sbl("wdb", [128, 4 * D], BF16) for _ in range(2)]
            PH = self.PSB
            ptb_scope = contextlib.ExitStack()
            PTB = ptb_scope.enter_context(nc.psum_tensor(self.nm("ptb"), [128, 128], BF16))
            dma(P, "g", lambda: ESEL[:, :], lambda: esel_h[:, :], "wg")
            gwt, gwt_t, wv = self.load_w_bf16(sc, gwt_h, D, E, name="gwt")
            P.wait("t", "wg", wv)
            P.op("v", lambda e: e.memset(CSROW[0:1, :], 0.0))
            P.op("v", lambda e: e.memset(ONESCF[:], 1.0), incs=(("dve", 1),))

            for tt in range(NTT):
                sl = tt % 2
                if tt >= 2:
                    P.wait("t", "dve", P.val("dve"))
                def gmm(e, tt=tt, sl=sl):
                    inst = None
                    for kc in range(NDC):
                        inst = e.matmul(self.PSB[sl][:, 0:E],
                                        self.XN[:, kc * T + tt * 128: kc * T + (tt + 1) * 128],
                                        gwt_t(kc), start=(kc == 0), stop=(kc == NDC - 1))
                    return inst
                P.op("t", gmm, incs=(("pe", 1),))
                P.wait("v", "pe", P.val("pe"))
                P.op("v", lambda e, tt=tt, sl=sl:
                     e.tensor_scalar_mul(LA[:, tt * E:(tt + 1) * E], self.PSB[sl][:, 0:E],
                                         -20.0), incs=(("dve", 1),))
            la3 = LA[:].rearrange("p (t e) -> p t e", e=E)
            es3 = ES[:].rearrange("p (t e) -> p t e", e=E)
            for it in range(20):
                P.wait("v", "dve", P.val("dve"))
                P.wait("v", "act", P.val("act"))
                P.op("v", lambda e: e.reduce_max(RMX[:, 0:16], la3, axis=mybir.AxisListType.X),
                     incs=(("dve", 1),))
                P.wait("v", "dve", P.val("dve"))
                rb = RMX[:, 0:16].rearrange("p t -> p t ()").broadcast_to([128, 16, E])
                P.op("v", lambda e: e.tensor_tensor(es3, la3, rb, op=OP.subtract),
                     incs=(("dve", 1),))
                P.wait("s", "dve", P.val("dve"))
                P.op("s", lambda e: e.activation(ES[:], ES[:], AF.Exp), incs=(("act", 1),))
                P.wait("v", "act", P.val("act"))
                P.op("v", lambda e: e.reduce_sum(RMX[:, 16:32], es3, axis=mybir.AxisListType.X),
                     incs=(("dve", 1),))
                P.wait("s", "dve", P.val("dve"))
                P.op("s", lambda e: e.activation(RMX[:, 16:32], RMX[:, 16:32], AF.Ln),
                     incs=(("act", 1),))
                P.wait("v", "act", P.val("act"))
                P.op("v", lambda e: e.tensor_add(RMX[:, 0:16], RMX[:, 0:16], RMX[:, 16:32]),
                     incs=(("dve", 1),))
                P.wait("v", "dve", P.val("dve"))
                P.op("v", lambda e: e.tensor_tensor(la3, la3, rb, op=OP.subtract),
                     incs=(("dve", 1),))
                P.wait("s", "dve", P.val("dve"))
                P.op("s", lambda e: e.activation(ES[:], LA[:], AF.Exp), incs=(("act", 1),))
                P.wait("t", "act", P.val("act"))
                P.wait("t", "dve", P.val("dve"))
                P.op("t", lambda e: e.matmul(self.PSB[0][0:1, 0:128], ONESCF[:], ES[:],
                                             start=True, stop=True), incs=(("pe", 1),))
                P.wait("v", "pe", P.val("pe"))
                csum3 = CSROW[0:1, 0:128].rearrange("o (t e) -> o e t", e=16) if False else None
                P.op("v", lambda e: e.reduce_sum(
                    CSROW[0:1, 0:8],
                    self.PSB[0][0:1, 0:128].rearrange("o (t e) -> o e t", e=E),
                    axis=mybir.AxisListType.X), incs=(("dve", 1),))
                P.wait("y", "dve", P.val("dve"))
                dma(P, "y", lambda: cc_in[0:1, :], lambda: CSROW[0:1, :], "ws")
                P.wait("g", "ws", P.val("ws"))
                P.op("g", lambda e: e.collective_compute(
                    "AllReduce", OP.add, replica_groups=[list(range(8))],
                    ins=[cc_in[:, :].opt()], outs=[cc_out[:, :].opt()]),
                     incs=(("gp", 1),))
                P.wait("y", "gp", P.val("gp"))
                dma(P, "y", lambda: CLROW[0:1, :], lambda: cc_out[0:1, 0:8], "ws")
                P.wait("s", "ws", P.val("ws"))
                P.op("s", lambda e: e.activation(CLROW[0:1, :], CLROW[0:1, :], AF.Ln),
                     incs=(("act", 1),))
                P.wait("t", "act", P.val("act"))
                P.op("t", lambda e: e.matmul(self.PSB[1][:, 0:8], self.ONESP[0:1, :],
                                             CLROW[0:1, :], start=True, stop=True),
                     incs=(("pe", 1),))
                P.wait("v", "pe", P.val("pe"))
                clb = self.PSB[1][:, 0:8].rearrange("p e -> p () e").broadcast_to([128, 16, E])
                P.op("v", lambda e: e.tensor_tensor(la3, la3, clb, op=OP.subtract),
                     incs=(("dve", 1),))
            P.wait("s", "dve", P.val("dve"))
            P.op("s", lambda e: e.activation(ES[:], LA[:], AF.Exp), incs=(("act", 1),))
            P.wait("v", "act", P.val("act"))
            P.op("v", lambda e: e.reduce_max(RMX[:, 0:16], es3, axis=mybir.AxisListType.X),
                 incs=(("dve", 1),))
            P.wait("v", "dve", P.val("dve"))
            v1b = RMX[:, 0:16].rearrange("p t -> p t ()").broadcast_to([128, 16, E])
            in13 = IN1[:].rearrange("p (t e) -> p t e", e=E)
            tm3 = TMP3[:].rearrange("p (t e) -> p t e", e=E)
            P.op("v", lambda e: e.tensor_tensor(in13, es3, v1b, op=OP.is_ge))
            P.op("v", lambda e: e.scalar_tensor_tensor(TMP3[:], IN1[:], -1e9, ES[:],
                                                       op0=OP.mult, op1=OP.add),
                 incs=(("dve", 1),))
            P.wait("v", "dve", P.val("dve"))
            P.op("v", lambda e: e.reduce_max(RMX[:, 16:32], tm3, axis=mybir.AxisListType.X),
                 incs=(("dve", 1),))
            P.wait("v", "dve", P.val("dve"))
            v2b = RMX[:, 16:32].rearrange("p t -> p t ()").broadcast_to([128, 16, E])
            P.op("v", lambda e: e.tensor_tensor(tm3, es3, v2b, op=OP.is_ge))
            P.op("v", lambda e: e.tensor_sub(TMP3[:], TMP3[:], IN1[:]))
            pi0b = es3[:, :, 0:1].broadcast_to([128, 16, E])
            pi1b = es3[:, :, 1:2].broadcast_to([128, 16, E])
            P.op("v", lambda e: e.tensor_tensor(in13, in13, pi0b, op=OP.mult))
            P.op("v", lambda e: e.tensor_tensor(tm3, tm3, pi1b, op=OP.mult))
            P.op("v", lambda e: e.tensor_add(WB[:], IN1[:], TMP3[:]), incs=(("dve", 1),))
            P.wait("t", "dve", P.val("dve"))
            for tt in range(NTT):
                P.wait("t", "act", P.val("act"))
                P.op("t", lambda e, tt=tt:
                     e.transpose(PTB[0:E, 0:128], WB[:, tt * E:(tt + 1) * E],
                                 self.IDENT[:]), incs=(("pe", 1),))
                P.wait("s", "pe", P.val("pe"))
                P.op("s", lambda e, tt=tt:
                     e.copy(WSELF[0:E, tt * 128:(tt + 1) * 128], PTB[0:E, 0:128]),
                     incs=(("act", 1),))
            P.join()
            ptb_scope.close()
            PU = [sc.enter_context(nc.psum_tensor(self.nm("pu"), [128, 512], F32))
                  for _ in range(2)]
            PD = [sc.enter_context(nc.psum_tensor(self.nm("pd"), [128, 512], F32))
                  for _ in range(3)]
            PWB = sc.enter_context(nc.psum_tensor(self.nm("pwb"), [128, 512], F32))

            wg_pe = {}
            HBS = sc.enter_context(nc.sbuf_tensor(self.nm("hbs"), [128, 4 * TG], BF16))
            blk = 0
            for ei in range(E):
                for fh in range(2):
                    for fcg in range(4):
                        bsel = blk % 2
                        if blk - 2 in wg_pe:
                            P.wait("g", "pe", wg_pe[blk - 2])
                        f0 = fh * 2048 + fcg * 512
                        dma(P, "g", lambda b=WGB[bsel]: b[:].rearrange("p (k m) -> p k m", m=512),
                            lambda ei=ei, f0=f0: Wg_h[ei][:, f0:f0 + 512]
                            .rearrange("(k p) m -> p k m", p=128), "wg")
                        dma(P, "g", lambda b=WUB[bsel]: b[:].rearrange("p (k m) -> p k m", m=512),
                            lambda ei=ei, f0=f0: Wu_h[ei][:, f0:f0 + 512]
                            .rearrange("(k p) m -> p k m", p=128), "wg")
                        dma(P, "g", lambda b=WDB[bsel]: b[:].rearrange("p (k m) -> p k m", m=D),
                            lambda ei=ei, f0=f0: Wd_h[ei][f0:f0 + 512, :]
                            .rearrange("(k p) m -> p k m", p=128), "wg")
                        wv1 = P.val("wg")
                        for tg in range(NTG):
                            P.wait("t", "dve", P.val("dve"))
                            P.op("t", lambda e, ei=ei, tg=tg:
                                 e.matmul(PWB[:, :], ESEL[:, ei * 128:(ei + 1) * 128],
                                          WSELF[0:E, tg * TG:(tg + 1) * TG],
                                          start=True, stop=True), incs=(("pe", 1),))
                            wbc_pe = P.val("pe")
                            hdone = {}
                            for lf in range(4):
                                sl = lf % 2
                                P.wait("t", "wg", wv1)
                                if sl in hdone:
                                    P.wait("t", "dve", hdone[sl])
                                def hmm(e, lf=lf, sl=sl, tg=tg, bsel=bsel):
                                    for kc in range(NDC):
                                        e.matmul(PH[sl][:, :],
                                                 WGB[bsel][:, kc * 512 + lf * 128: kc * 512 + (lf + 1) * 128],
                                                 self.xn_ap(kc, tg),
                                                 start=(kc == 0), stop=(kc == NDC - 1))
                                    inst = None
                                    for kc in range(NDC):
                                        inst = e.matmul(PU[sl][:, :],
                                                        WUB[bsel][:, kc * 512 + lf * 128: kc * 512 + (lf + 1) * 128],
                                                        self.xn_ap(kc, tg),
                                                        start=(kc == 0), stop=(kc == NDC - 1))
                                    return inst
                                P.op("t", hmm, incs=(("pe", 1),))
                                P.wait("s", "pe", P.val("pe"))
                                hsl = HBS[:, lf * TG:(lf + 1) * TG]
                                P.op("s", lambda e, hsl=hsl, sl=sl:
                                     e.activation(hsl, PH[sl][:, :], AF.Silu),
                                     incs=(("act", 1),))
                                P.wait("v", "act", P.val("act"))
                                if lf == 0:
                                    P.wait("v", "pe", wbc_pe)
                                P.op("v", lambda e, hsl=hsl, sl=sl:
                                     e.tensor_tensor(hsl, hsl, PU[sl][:, :], op=OP.mult))
                                P.op("v", lambda e, hsl=hsl:
                                     e.tensor_tensor(hsl, hsl, PWB[:, :], op=OP.mult),
                                     incs=(("dve", 1),))
                                hdone[sl] = P.val("dve")
                            zadd = {}
                            for dc in range(NDC):
                                sl = dc % 3
                                if dc == 0:
                                    P.wait("t", "dve", hdone[1])
                                if sl in zadd:
                                    P.wait("t", "dve", zadd[sl])
                                def dmm(e, dc=dc, sl=sl, bsel=bsel):
                                    inst = None
                                    for lf in range(4):
                                        inst = e.matmul(PD[sl][:, :],
                                                        WDB[bsel][:, lf * D + dc * 128: lf * D + (dc + 1) * 128],
                                                        HBS[:, lf * TG:(lf + 1) * TG],
                                                        start=(lf == 0), stop=(lf == 3))
                                    return inst
                                P.op("t", dmm, incs=(("pe", 1),))
                                P.wait("v", "pe", P.val("pe"))
                                xs = self.xr_ap(dc, tg)
                                P.op("v", lambda e, xs=xs, sl=sl:
                                     e.tensor_add(xs, xs, PD[sl][:, :]), incs=(("dve", 1),))
                                zadd[sl] = P.val("dve")
                        wg_pe[blk] = P.val("pe")
                        blk += 1
            P.join()

    def l1_attn(self):
        P, nc = self.P, self.nc
        GH, NHG, NCH = 4, 4, 2
        scale = 1.0 / float(np.sqrt(1040.0))
        Wd_q = self.hbm_in("L1_attn_Wd_q_w", (D, 512))
        Wu_q = self.hbm_in("L1_attn_Wu_q_w", (512, D))
        Wq_r = self.hbm_in("L1_attn_Wq_r_w", (512, D))
        Wk_r = self.hbm_in("L1_attn_Wk_r_w", (D, D))
        Wd_kv = self.hbm_in("L1_attn_Wd_kv_w", (D, 512))
        Wu_k = self.hbm_in("L1_attn_Wu_k_w", (512, D))
        Wu_v = self.hbm_in("L1_attn_Wu_v_w", (512, D))
        Wo = self.hbm_in("L1_attn_Wo_w", (D, D))
        cos1_h = self.hbm_in("c_cos1", (T, 512))
        sin1_h = self.hbm_in("c_sin1", (T, 512))
        ind4_h = self.hbm_in("c_ind4", (128, 8))

        with contextlib.ExitStack() as sc:
            sbl = lambda n_, shp, dt=F32: sc.enter_context(nc.sbuf_tensor(self.nm(n_), list(shp), dt))
            CTQ = sbl("ctq", [128, 4 * T], BF16)
            CTKV = sbl("ctkv", [128, 4 * T], BF16)
            IND4 = sbl("ind4", [128, 8], BF16)
            CS = sbl("cs", [128, 2 * 2 * 128])
            RS = sbl("rs", [128, 256])
            RB = sbl("rb", [128, 256], BF16)
            dma(P, "g", lambda: IND4[:], lambda: ind4_h[:, :], "wg")
            bdq, _ = self.load_col_f32(sc, "L1_attn_Wd_q_b", 512, "bdq")
            bdkv, _ = self.load_col_f32(sc, "L1_attn_Wd_kv_b", 512, "bdkv")
            buq, _ = self.load_col_f32(sc, "L1_attn_Wu_q_b", D, "buq")
            buk, _ = self.load_col_f32(sc, "L1_attn_Wu_k_b", D, "buk")
            bo, _ = self.load_col_f32(sc, "L1_attn_Wo_b", D, "bo")
            buv_r, _ = self.load_row_bf16(sc, self.hbm_in("L1_attn_Wu_v_b", (D,)), D, "buvr")
            bqr_r, _ = self.load_row_bf16(sc, self.hbm_in("L1_attn_Wq_r_b", (D,)), D, "bqrr")
            bkr_r, wv0 = self.load_row_bf16(sc, self.hbm_in("L1_attn_Wk_r_b", (D,)), D, "bkrr")

            with contextlib.ExitStack() as s2:
                PSC = [s2.enter_context(nc.psum_tensor(self.nm("psc"), [128, 512], F32))
                       for _ in range(2)]
                pool4 = [self.PSB[0][:, :], self.PSB[1][:, :], PSC[0][:, :], PSC[1][:, :]]
                wdq, wdq_t, _ = self.load_w_bf16(s2, Wd_q, D, 512, name="wdq")
                wdkv, wdkv_t, wv1 = self.load_w_bf16(s2, Wd_kv, D, 512, name="wdkv")
                P.wait("t", "wg", wv1)
                self.gemm_fm_to(nkc=NDC, nmc=4, xin_fn=self.xn_ap,
                                wtile_fn=lambda kc, mc: wdq_t(kc, mc * 128, 128),
                                dst_fn=lambda mc, tg: CTQ[:, mc * T + tg * TG: mc * T + (tg + 1) * TG],
                                bias_fn=lambda mc: bdq[:, mc:mc + 1], pool=pool4)
                self.gemm_fm_to(nkc=NDC, nmc=4, xin_fn=self.xn_ap,
                                wtile_fn=lambda kc, mc: wdkv_t(kc, mc * 128, 128),
                                dst_fn=lambda mc, tg: CTKV[:, mc * T + tg * TG: mc * T + (tg + 1) * TG],
                                bias_fn=lambda mc: bdkv[:, mc:mc + 1], pool=pool4)
                P.join()

            for hg in range(NHG):
                with contextlib.ExitStack() as s2:
                    sb2 = lambda n_, shp, dt=F32: s2.enter_context(nc.sbuf_tensor(self.nm(n_), list(shp), dt))
                    KF = sb2("kf", [128, NCH * T], BF16)
                    VT = sb2("vt", [128, NTT * GH * 65], BF16)
                    QF = sb2("qf", [128, NCH * TG], BF16)
                    SQB = self.scr_bf16(0, 2 * NCH * TG)
                    ARAW = self.scr_f32(3072, 512)
                    AOUT = self.scr_bf16(3584, NCH * TG)
                    KMAX = sb2("kmax", [128, 4])
                    NEGMB = sb2("negmb", [128, 1], BF16)
                    NEGM = sb2("negm", [128, GH])
                    NROW = sb2("nrow", [1, GH], BF16)
                    RECR = sb2("recr", [1, 512])
                    ST = [s2.enter_context(nc.psum_tensor(self.nm("st"), [128, 1024], F32))
                          for _ in range(2)]
                    PVP = s2.enter_context(nc.psum_tensor(self.nm("pv"), [128, 512], F32))
                    PTR = [s2.enter_context(nc.psum_tensor(self.nm("ptr"), [128, 128], BF16))
                           for _ in range(1)]
                    PT = [sb2("pt", [128, 1024], BF16) for _ in range(3)]
                    hc = hg * GH * 64

                    ph_a = contextlib.ExitStack()
                    wuk, wuk_t, _ = self.load_w_bf16(ph_a, Wu_k, 512, 256, col0=hc, name="wuk")
                    wuv, wuv_t, _ = self.load_w_bf16(ph_a, Wu_v, 512, 256, col0=hc, name="wuv")
                    wkr, wkr_t, wv2 = self.load_w_bf16(ph_a, Wk_r, D, 256, col0=hc, name="wkr")
                    P.wait("t", "wg", wv2)

                    stpool = [ST[0][:, 0:512], ST[0][:, 512:1024],
                              ST[1][:, 0:512], ST[1][:, 512:1024]]
                    self.gemm_fm_to(nkc=4, nmc=NCH,
                                    xin_fn=lambda kc, tg: CTKV[:, kc * T + tg * TG: kc * T + (tg + 1) * TG],
                                    wtile_fn=lambda kc, mc: wuk_t(kc, mc * 128, 128),
                                    dst_fn=lambda mc, tg: KF[:, mc * T + tg * TG: mc * T + (tg + 1) * TG],
                                    bias_fn=lambda mc: buk[:, hg * NCH + mc: hg * NCH + mc + 1],
                                    pool=stpool)

                    P.op("v", lambda e: e.memset(VT[:], 1.0), incs=(("dve", 1),))
                    P.wait("s", "dve", P.val("dve"))
                    for t2 in range(0, NTT, 4):
                        if t2 > 0:
                            P.wait("t", "act", P.val("act"))
                        bks = [stpool[(t2 + k) % 4] for k in range(4)]
                        def vmm(e, t2=t2, hc=hc, bks=bks):
                            inst = None
                            for k in range(4):
                                tt = t2 + k
                                for kc in range(4):
                                    e.matmul(bks[k][:, 0:GH * 64],
                                             CTKV[:, kc * T + tt * 128: kc * T + (tt + 1) * 128],
                                             wuv_t(kc), start=(kc == 0), stop=False)
                                inst = e.matmul(bks[k][:, 0:GH * 64], self.ONES1B[:],
                                                buv_r[0:1, hc:hc + GH * 64], start=False, stop=True)
                            return inst
                        P.op("t", vmm, incs=(("pe", 1),))
                        P.wait("s", "pe", P.val("pe"))
                        def vevac(e, t2=t2, bks=bks):
                            inst = None
                            for k in range(4):
                                tt = t2 + k
                                vdst = VT[:, tt * GH * 65:(tt + 1) * GH * 65].rearrange(
                                    "p (h f) -> p h f", f=65)[:, :, 0:64]
                                inst = e.copy(vdst, bks[k][:, 0:GH * 64]
                                              .rearrange("p (h f) -> p h f", f=64))
                            return inst
                        P.op("s", vevac, incs=(("act", 1),))
                    P.join(("t", "v", "s"))

                    for tt in range(NTT):
                        sl = tt % 2
                        cslot = tt % 2
                        P.wait("y", "dve", P.val("dve"))
                        dma(P, "y", lambda cslot=cslot: CS[:, cslot * 256: cslot * 256 + 128],
                            lambda tt=tt, hg=hg: cos1_h[tt * 128:(tt + 1) * 128,
                                                        hg * 128:(hg + 1) * 128], "ws")
                        dma(P, "y", lambda cslot=cslot: CS[:, cslot * 256 + 128: cslot * 256 + 256],
                            lambda tt=tt, hg=hg: sin1_h[tt * 128:(tt + 1) * 128,
                                                        hg * 128:(hg + 1) * 128], "ws")
                        ws_v = P.val("ws")
                        P.wait("t", "dve", P.val("dve"))
                        def krmm(e, tt=tt, sl=sl, hc=hc):
                            for kc in range(NDC):
                                e.matmul(self.PSB[sl][:, 0:256],
                                         self.XN[:, kc * T + tt * 128: kc * T + (tt + 1) * 128],
                                         wkr_t(kc), start=(kc == 0), stop=False)
                            return e.matmul(self.PSB[sl][:, 0:256], self.ONES1B[:],
                                            bkr_r[0:1, hc:hc + 256], start=False, stop=True)
                        P.op("t", krmm, incs=(("pe", 1),))
                        P.wait("s", "pe", P.val("pe"))
                        P.op("s", lambda e, sl=sl: e.copy(RS[:, :], self.PSB[sl][:, 0:256]),
                             incs=(("act", 1),))
                        P.wait("v", "act", P.val("act"))
                        P.wait("v", "ws", ws_v)
                        self.rope_tm_l1(RS[:, :], RB[:, :],
                                        CS[:, cslot * 256: cslot * 256 + 128],
                                        CS[:, cslot * 256 + 128: cslot * 256 + 256], npairs=128)
                        for bp in range(2):
                            psl = 0
                            P.wait("t", "dve", P.val("dve"))
                            P.op("t", lambda e, bp=bp, psl=psl:
                                 e.transpose(PTR[0][:, :], RB[:, bp * 128:(bp + 1) * 128],
                                             self.IDENT[:]), incs=(("pe", 1),))
                            P.wait("v", "pe", P.val("pe"))
                            dst = KF[:, bp * T + tt * 128: bp * T + (tt + 1) * 128]
                            P.op("v", lambda e, dst=dst, psl=psl:
                                 e.tensor_tensor(dst, dst, PTR[0][:, :], op=OP.add),
                                 incs=(("dve", 1),))
                    P.join()
                    ph_a.close()
                    wuq, wuq_t, _ = self.load_w_bf16(s2, Wu_q, 512, 256, col0=hc, name="wuq")
                    wqr, wqr_t, _ = self.load_w_bf16(s2, Wq_r, 512, 256, col0=hc, name="wqr")
                    wob, wob_t, wv3 = self.load_w_bf16(s2, Wo, 256, D, row0=hc, name="wo")
                    P.wait("t", "wg", wv3)

                    for qg in range(NTG):
                        P.wait("t", "act", P.val("act"))
                        self.gemm_fm_to(nkc=4, nmc=NCH,
                                        xin_fn=lambda kc, tg, qg=qg: CTQ[:, kc * T + qg * TG: kc * T + (qg + 1) * TG],
                                        wtile_fn=lambda kc, mc: wuq_t(kc, mc * 128, 128),
                                        dst_fn=lambda mc, tg: QF[:, mc * TG:(mc + 1) * TG],
                                        bias_fn=lambda mc: buq[:, hg * NCH + mc: hg * NCH + mc + 1],
                                        tgs=(0,), pool=stpool)
                        for lt in range(4):
                            tt = qg * 4 + lt
                            sl = lt % 2
                            cslot = lt % 2
                            P.wait("y", "dve", P.val("dve"))
                            dma(P, "y", lambda cslot=cslot: CS[:, cslot * 256: cslot * 256 + 128],
                                lambda tt=tt, hg=hg: cos1_h[tt * 128:(tt + 1) * 128,
                                                            hg * 128:(hg + 1) * 128], "ws")
                            dma(P, "y", lambda cslot=cslot: CS[:, cslot * 256 + 128: cslot * 256 + 256],
                                lambda tt=tt, hg=hg: sin1_h[tt * 128:(tt + 1) * 128,
                                                            hg * 128:(hg + 1) * 128], "ws")
                            ws_v = P.val("ws")
                            P.wait("t", "dve", P.val("dve"))
                            def qrmm(e, tt=tt, sl=sl, hc=hc):
                                for kc in range(4):
                                    e.matmul(self.PSB[sl][:, 0:256],
                                             CTQ[:, kc * T + tt * 128: kc * T + (tt + 1) * 128],
                                             wqr_t(kc), start=(kc == 0), stop=False)
                                return e.matmul(self.PSB[sl][:, 0:256], self.ONES1B[:],
                                                bqr_r[0:1, hc:hc + 256], start=False, stop=True)
                            P.op("t", qrmm, incs=(("pe", 1),))
                            P.wait("s", "pe", P.val("pe"))
                            P.op("s", lambda e, sl=sl: e.copy(RS[:, :], self.PSB[sl][:, 0:256]),
                                 incs=(("act", 1),))
                            P.wait("v", "act", P.val("act"))
                            P.wait("v", "ws", ws_v)
                            self.rope_tm_l1(RS[:, :], RB[:, :],
                                            CS[:, cslot * 256: cslot * 256 + 128],
                                            CS[:, cslot * 256 + 128: cslot * 256 + 256], npairs=128)
                            for bp in range(2):
                                psl = 0
                                P.wait("t", "dve", P.val("dve"))
                                P.op("t", lambda e, bp=bp, psl=psl:
                                     e.transpose(PTR[0][:, :], RB[:, bp * 128:(bp + 1) * 128],
                                                 self.IDENT[:]), incs=(("pe", 1),))
                                P.wait("v", "pe", P.val("pe"))
                                dst = QF[:, bp * TG + lt * 128: bp * TG + (lt + 1) * 128]
                                P.op("v", lambda e, dst=dst, psl=psl:
                                     e.tensor_tensor(dst, dst, PTR[0][:, :], op=OP.add),
                                     incs=(("dve", 1),))
                        P.join(("t", "v", "s"))

                        k3_fn = lambda tg: KF[:].rearrange("p (c t) -> p c t", c=NCH)[:, :, tg * TG:(tg + 1) * TG]
                        q3 = QF[:].rearrange("p (c t) -> p c t", c=NCH)
                        self.negm_content(GH, NCH, k3_fn, q3, IND4, KMAX, NEGM, NROW,
                                          NEGMB, ST[1], PTR, SQB, scale, first=(qg == 0))

                        def kf_fn(h, kt):
                            ch, ro = h // 2, 64 * (h % 2)
                            return KF[ro:ro + 64, ch * T + kt * 128: ch * T + (kt + 1) * 128]

                        def qf_fn(h, qg_):
                            ch, ro = h // 2, 64 * (h % 2)
                            return QF[ro:ro + 64, ch * TG:(ch + 1) * TG]

                        def v_fn(h, kt):
                            return VT[:, kt * GH * 65 + h * 65: kt * GH * 65 + (h + 1) * 65]

                        def out_fn(h, qg_):
                            ch, ro = h // 2, 64 * (h % 2)
                            return AOUT[ro:ro + 64, ch * TG:(ch + 1) * TG]

                        self.attn_core(heads=range(GH), parts=[(kf_fn, qf_fn, 64)],
                                       v_slot_fn=v_fn,
                                       negm_ap_fn=lambda h: NEGM[:, h:h + 1],
                                       scale=scale, causal=False,
                                       out_ap_fn=out_fn, qgs=(qg,),
                                       ST=ST, PV=PVP, PT=PT, araw=ARAW, recr=RECR)

                        epi0 = P.val("epi")
                        for mc in range(NDC):
                            sl = mc % 2
                            if mc >= 2:
                                P.wait("t", "epi", epi0 + mc - 1)
                            def wmm(e, mc=mc, sl=sl):
                                inst = None
                                for kc in range(NCH):
                                    inst = e.matmul(self.PSB[sl][:, :], wob_t(kc, mc * 128, 128),
                                                    AOUT[:, kc * TG:(kc + 1) * TG],
                                                    start=(kc == 0), stop=(kc == NCH - 1))
                                return inst
                            P.op("t", wmm, incs=(("pe", 1),))
                            P.wait("v", "pe", P.val("pe"))
                            xs = self.xr_ap(mc, qg)
                            if hg == 0:
                                P.op("v", lambda e, xs=xs, sl=sl:
                                     e.tensor_add(xs, xs, self.PSB[sl][:, :]))
                                P.op("v", lambda e, xs=xs, mc=mc:
                                     e.tensor_scalar_add(xs, xs, bo[:, mc:mc + 1]),
                                     incs=(("epi", 1),))
                            else:
                                P.op("v", lambda e, xs=xs, sl=sl:
                                     e.tensor_add(xs, xs, self.PSB[sl][:, :]),
                                     incs=(("epi", 1),))
                        P.wait("t", "epi", epi0 + NDC)
                    P.join()
            P.join()

    def l1_ffn(self):
        P, nc = self.P, self.nc
        E, FP = 16, 256
        wg_h = self.hbm_in("L1_ffn_wg", (D, E))
        w1_h = self.hbm_in("L1_ffn_w1", (E, D, FP))
        w2_h = self.hbm_in("L1_ffn_w2", (E, FP, D))
        b2_h = self.hbm_in("L1_ffn_b2", (E, D))
        tri_h = self.hbm_in("c_tri", (128, 128))
        gidx_h = self.hbm_in("c_gidx", (128, 16))
        esel_h = self.hbm_in("c_esel16", (16, 16 * 128))
        with contextlib.ExitStack() as sc:
            sbl = lambda n_, shp, dt=F32: sc.enter_context(nc.sbuf_tensor(self.nm(n_), list(shp), dt))
            TRI = sbl("tri", [128, 128], BF16)
            GIDX = sbl("gidx", [128, 16])
            SC_ = sbl("sc", [128, 256])
            POSB = sbl("posb", [128, 256], BF16)
            CUMS = sbl("cums", [128, 256])
            SEL = sbl("sel", [128, 256])
            SELB = sbl("selb", [128, 256], BF16)
            TMP = sbl("tmp", [128, 256])
            V12 = sbl("v12", [128, 4])
            ROWS = sbl("rows", [1, 3 * 256])
            MSEL = sbl("msel", [16, T], BF16)
            ESEL = sbl("esel", [16, 16 * 128], BF16)
            MR2 = sbl("mr2", [65, 512])
            ZB = sbl("zb", [128, NDC * TG], BF16)
            ZQ = sbl("zq", [128, NDC * TG], BF16)
            HALL = sbl("hall", [128, E * 2 * TG], BF16)
            GAM, _ = self.load_col_f32(sc, "L1_ffn_gamma", D, "gam")
            BET, _ = self.load_col_f32(sc, "L1_ffn_beta", D, "bet")
            B1C, _ = self.load_col_f32_shaped(sc, "L1_ffn_b1_pf", (128, 2 * E), "b1c")
            wgt, wgt_t, _ = self.load_w_bf16(sc, wg_h, D, E, name="wg")
            bgr, _ = self.load_row_bf16(sc, self.hbm_in("L1_ffn_bg", (E,)), E, "bgr")
            b2sb = sc.enter_context(nc.sbuf_tensor(self.nm("b2sb"), [16, D], BF16))
            dma(P, "g", lambda: b2sb[:, :], lambda: b2_h[:, :], "wg")
            dma(P, "g", lambda: TRI[:], lambda: tri_h[:, :], "wg")
            dma(P, "g", lambda: ESEL[:, :], lambda: esel_h[:, :], "wg")
            dma(P, "g", lambda: GIDX[:], lambda: gidx_h[:, :], "wg")
            wv = P.val("wg")
            P.wait("t", "wg", wv)
            PH = [sc.enter_context(nc.psum_tensor(self.nm("ph"), [128, 512], F32))
                  for _ in range(2)]
            PD = [sc.enter_context(nc.psum_tensor(self.nm("pd"), [128, 512], F32))
                  for _ in range(2)]
            PTB = sc.enter_context(nc.psum_tensor(self.nm("ptb"), [128, 128], BF16))

            for tt in range(NTT):
                sl = tt % 2
                if tt >= 2:
                    P.wait("t", "dve", P.val("dve"))
                def gmm(e, tt=tt, sl=sl):
                    for kc in range(NDC):
                        e.matmul(self.PSB[sl][:, 0:E],
                                 self.XN[:, kc * T + tt * 128: kc * T + (tt + 1) * 128],
                                 wgt_t(kc), start=(kc == 0), stop=False)
                    return e.matmul(self.PSB[sl][:, 0:E], self.ONES1B[:],
                                    bgr[0:1, :], start=False, stop=True)
                P.op("t", gmm, incs=(("pe", 1),))
                P.wait("v", "pe", P.val("pe"))
                P.op("v", lambda e, sl=sl: e.reduce_max(V12[:, 0:1], self.PSB[sl][:, 0:E],
                                                        axis=mybir.AxisListType.X),
                     incs=(("dve", 1),))
                P.wait("v", "dve", P.val("dve"))
                P.op("v", lambda e: e.tensor_scalar_mul(V12[:, 1:2], V12[:, 0:1], -1.0),
                     incs=(("dve", 1),))
                P.wait("s", "dve", P.val("dve"))
                P.op("s", lambda e, sl=sl, tt=tt:
                     e.activation(SC_[:, tt * E:(tt + 1) * E], self.PSB[sl][:, 0:E], AF.Exp,
                                  bias=V12[:, 1:2], accum_out=V12[:, 2:3]),
                     incs=(("act", 1),))
                P.wait("v", "act", P.val("act"))
                P.op("v", lambda e: e.reciprocal(V12[:, 3:4], V12[:, 2:3]), incs=(("dve", 1),))
                P.wait("v", "dve", P.val("dve"))
                P.op("v", lambda e, tt=tt:
                     e.tensor_scalar(SC_[:, tt * E:(tt + 1) * E], SC_[:, tt * E:(tt + 1) * E],
                                     V12[:, 3:4], None, op0=OP.mult), incs=(("dve", 1),))
            sc3 = SC_[:].rearrange("p (t e) -> p t e", e=E)
            P.wait("v", "dve", P.val("dve"))
            P.op("v", lambda e: e.reduce_max(CUMS[:, 0:16], sc3, axis=mybir.AxisListType.X),
                 incs=(("dve", 1),))
            P.wait("v", "dve", P.val("dve"))
            v1b = CUMS[:, 0:16].rearrange("p t -> p t ()").broadcast_to([128, 16, E])
            P.op("v", lambda e: e.tensor_tensor(SEL[:].rearrange("p (t e) -> p t e", e=E),
                                                sc3, v1b, op=OP.is_ge))
            P.op("v", lambda e: e.scalar_tensor_tensor(TMP[:], SEL[:], -1e9, SC_[:],
                                                       op0=OP.mult, op1=OP.add),
                 incs=(("dve", 1),))
            P.wait("v", "dve", P.val("dve"))
            P.op("v", lambda e: e.reduce_max(CUMS[:, 16:32],
                                             TMP[:].rearrange("p (t e) -> p t e", e=E),
                                             axis=mybir.AxisListType.X), incs=(("dve", 1),))
            P.wait("v", "dve", P.val("dve"))
            v2b = CUMS[:, 16:32].rearrange("p t -> p t ()").broadcast_to([128, 16, E])
            P.op("v", lambda e: e.tensor_tensor(POSB[:].rearrange("p (t e) -> p t e", e=E),
                                                sc3, v2b, op=OP.is_ge), incs=(("dve", 1),))
            P.wait("t", "dve", P.val("dve"))
            P.op("t", lambda e: e.matmul(self.PSB[0][0:1, 0:256], self.ONESC[:], POSB[:],
                                         start=True, stop=True), incs=(("pe", 1),))
            P.wait("s", "pe", P.val("pe"))
            P.op("s", lambda e: e.copy(ROWS[0:1, 0:256], self.PSB[0][0:1, 0:256]),
                 incs=(("act", 1),))
            P.wait("v", "act", P.val("act"))
            P.op("v", lambda e: e.memset(ROWS[0:1, 256:272], 0.0), incs=(("dve", 1),))
            for tt in range(1, 16):
                P.op("v", lambda e, tt=tt:
                     e.tensor_add(ROWS[0:1, 256 + tt * 16:256 + (tt + 1) * 16],
                                  ROWS[0:1, 256 + (tt - 1) * 16:256 + tt * 16],
                                  ROWS[0:1, (tt - 1) * 16: tt * 16]), incs=(("dve", 1),))
            P.op("v", lambda e: e.tensor_add(ROWS[0:1, 512:528], ROWS[0:1, 496:512],
                                             ROWS[0:1, 240:256]))
            P.op("v", lambda e: e.tensor_scalar(ROWS[0:1, 512:528], ROWS[0:1, 512:528],
                                                -1.0, 1024.0, op0=OP.mult, op1=OP.add),
                 incs=(("dve", 1),))
            P.wait("v", "dve", P.val("dve"))
            rbud = ROWS[0:1, 512:528].rearrange("o e -> o () e").broadcast_to([1, 16, E])
            P.op("v", lambda e: e.tensor_tensor(ROWS[0:1, 256:512].rearrange("o (t e) -> o t e", e=E),
                                                ROWS[0:1, 256:512].rearrange("o (t e) -> o t e", e=E),
                                                rbud, op=OP.add), incs=(("dve", 1),))
            P.wait("t", "dve", P.val("dve"))
            def cmm(e):
                e.matmul(self.PSB[1][:, 0:256], TRI[:], POSB[:], start=True, stop=True)
                return e.matmul(self.PSB[0][:, 0:256], self.ONESP[0:1, :],
                                ROWS[0:1, 256:512], start=True, stop=True)
            P.op("t", cmm, incs=(("pe", 1),))
            P.wait("s", "pe", P.val("pe"))
            P.op("s", lambda e: e.copy(CUMS[:, :], self.PSB[1][:, 0:256]), incs=(("act", 1),))
            P.wait("v", "act", P.val("act"))
            gb = GIDX[:].rearrange("p t -> p t ()").broadcast_to([128, 16, E])
            P.op("v", lambda e: e.tensor_tensor(TMP[:].rearrange("p (t e) -> p t e", e=E), gb,
                                                CUMS[:].rearrange("p (t e) -> p t e", e=E),
                                                op=OP.subtract))
            P.op("v", lambda e: e.tensor_add(TMP[:], TMP[:], POSB[:]))
            P.op("v", lambda e: e.tensor_tensor(TMP[:], TMP[:], self.PSB[0][:, 0:256],
                                                op=OP.is_lt))
            P.op("v", lambda e: e.tensor_scalar(SEL[:], POSB[:], -1.0, 1.0,
                                                op0=OP.mult, op1=OP.add))
            P.op("v", lambda e: e.tensor_mul(SEL[:], SEL[:], TMP[:]))
            P.op("v", lambda e: e.tensor_tensor(SELB[:], SEL[:], POSB[:], op=OP.add),
                 incs=(("dve", 1),))
            P.wait("t", "dve", P.val("dve"))
            for tt in range(NTT):
                P.wait("t", "act", P.val("act"))
                P.op("t", lambda e, tt=tt:
                     e.transpose(PTB[0:16, 0:128],
                                 SELB[:, tt * 16:(tt + 1) * 16], self.IDENT[:]),
                     incs=(("pe", 1),))
                P.wait("s", "pe", P.val("pe"))
                P.op("s", lambda e, tt=tt:
                     e.copy(MSEL[0:16, tt * 128:(tt + 1) * 128],
                            PTB[0:16, 0:128]), incs=(("act", 1),))
            P.join()

            for tg in range(NTG):
                self.bcast_tg(tg)
                gelu_done = {}
                mask_done = 0
                for ei in range(E):
                    with contextlib.ExitStack() as s3:
                        w1e, w1e_t, wv1 = self.load_w_bf16(s3, w1_h, D, FP, row0=0, name="w1e",
                                                           hbm_pre=lambda a, ei=ei: a[ei])
                        P.wait("t", "wg", wv1)
                        for fc in range(2):
                            sl = fc
                            if sl in gelu_done:
                                P.wait("t", "act", gelu_done[sl])
                            def hmm(e, fc=fc, sl=sl, tg=tg):
                                inst = None
                                for kc in range(NDC):
                                    inst = e.matmul(PH[sl][:, :], w1e_t(kc, fc * 128, 128),
                                                    self.xn_ap(kc, tg),
                                                    start=(kc == 0), stop=(kc == NDC - 1))
                                return inst
                            P.op("t", hmm, incs=(("pe", 1),))
                            P.wait("s", "pe", P.val("pe"))
                            hsl = HALL[:, (ei * 2 + fc) * TG:(ei * 2 + fc + 1) * TG]
                            P.op("s", lambda e, hsl=hsl, sl=sl, fc=fc, ei=ei:
                                 e.activation(hsl, PH[sl][:, :], AF.Gelu,
                                              bias=B1C[:, fc * E + ei: fc * E + ei + 1]),
                                 incs=(("act", 1),))
                            gelu_done[sl] = P.val("act")
                        P.wait("t", "dve", mask_done)
                        P.op("t", lambda e, ei=ei, tg=tg:
                             e.matmul(PD[0][:, :], ESEL[:, ei * 128:(ei + 1) * 128],
                                      MSEL[0:16, tg * TG:(tg + 1) * TG],
                                      start=True, stop=True), incs=(("pe", 1),))
                        P.wait("v", "pe", P.val("pe"))
                        P.wait("v", "act", P.val("act"))
                        h2 = HALL[:, ei * 2 * TG:(ei + 1) * 2 * TG].rearrange(
                            "p (c t) -> p c t", c=2)
                        mb = PD[0][:, :].rearrange("p t -> p () t").broadcast_to([128, 2, TG])
                        P.op("v", lambda e, h2=h2, mb=mb:
                             e.tensor_tensor(h2, h2, mb, op=OP.mult), incs=(("dve", 1),))
                        mask_done = P.val("dve")
                h_ready = P.val("dve")
                zadd_done = {}
                for dc in range(NDC):
                    sl = dc % 2
                    with contextlib.ExitStack() as s3:
                        w2c, w2c_t, wv2 = self.load_w_bf16(s3, w2_h, E * FP, 128,
                                                           col0=dc * 128, name="w2c",
                                                           hbm_pre=lambda a: a.rearrange("e f d -> (e f) d"))
                        P.wait("t", "wg", wv2)
                        if dc == 0:
                            P.wait("t", "dve", h_ready)
                        if sl in zadd_done:
                            P.wait("t", "dve", zadd_done[sl])
                        def dmm(e, dc=dc, sl=sl, tg=tg):
                            for ef in range(E * 2):
                                e.matmul(PD[sl][:, :], w2c_t(ef),
                                         HALL[:, ef * TG:(ef + 1) * TG],
                                         start=(ef == 0), stop=False)
                            return e.matmul(PD[sl][:, :], b2sb[:, dc * 128:(dc + 1) * 128],
                                            MSEL[0:16, tg * TG:(tg + 1) * TG],
                                            start=False, stop=True)
                        P.op("t", dmm, incs=(("pe", 1),))
                        P.wait("v", "pe", P.val("pe"))
                        zdst = self.scr_f32(dc * TG, TG)
                        self.xn32_into(zdst, dc, tg)
                        P.op("v", lambda e, zdst=zdst, sl=sl:
                             e.tensor_add(zdst, zdst, PD[sl][:, :]), incs=(("dve", 1),))
                        zadd_done[sl] = P.val("dve")
                P.wait("v", "dve", P.val("dve"))
                P.op("v", lambda e: e.tensor_copy(ZB[:], self.scr_f32(0, NDC * TG)),
                     incs=(("dve", 1),))
                P.op("s", lambda e: e.activation(ZQ[:], self.scr_f32(0, NDC * TG), AF.Square),
                     incs=(("act", 1),))
                P.wait("t", "dve", P.val("dve"))
                P.wait("t", "act", P.val("act"))
                for which, src in ((0, ZB), (1, ZQ)):
                    def mms(e, src=src, which=which):
                        inst = None
                        for c in range(NDC):
                            inst = e.matmul(self.PSB[which][0:1, 0:512], self.ONESC[:],
                                            src[:, c * TG:(c + 1) * TG],
                                            start=(c == 0), stop=(c == NDC - 1))
                        return inst
                    P.op("t", mms, incs=(("pe", 1),))
                P.wait("s", "pe", P.val("pe"))
                P.op("s", lambda e: e.activation(MR2[0:1, :], self.PSB[0][0:1, 0:512],
                                                 AF.Identity, scale=1.0 / 1024.0))
                P.op("s", lambda e: e.activation(MR2[32:33, :], self.PSB[1][0:1, 0:512],
                                                 AF.Identity, scale=1.0 / 1024.0),
                     incs=(("act", 1),))
                P.wait("v", "act", P.val("act"))
                musq2 = self.SCR[32:33, 4096:4608]
                P.op("v", lambda e, m=musq2: e.tensor_mul(m, MR2[0:1, :], MR2[0:1, :]),
                     incs=(("dve", 1),))
                P.wait("v", "dve", P.val("dve"))
                P.op("v", lambda e, m=musq2: e.tensor_sub(MR2[32:33, :], MR2[32:33, :], m),
                     incs=(("dve", 1),))
                P.wait("s", "dve", P.val("dve"))
                P.op("s", lambda e: e.activation(MR2[32:33, :], MR2[32:33, :], AF.Ln,
                                                 bias=self.CPOOL[32:33, self._consts[LN_EPS]:self._consts[LN_EPS] + 1]))
                P.op("s", lambda e: e.activation(MR2[64:65, :], MR2[32:33, :], AF.Exp,
                                                 scale=-0.5), incs=(("act", 1),))
                P.wait("t", "act", P.val("act"))
                P.wait("t", "dve", P.val("dve"))
                def k1(e):
                    e.matmul(self.PSB[0][:, :], self.ONESP[0:1, :], MR2[0:1, :],
                             start=True, stop=True)
                    return e.matmul(self.PSB[1][:, :], self.ONESP[64:65, :], MR2[64:65, :],
                                    start=True, stop=True)
                P.op("t", k1, incs=(("pe", 1),))
                P.wait("v", "pe", P.val("pe"))
                for dc in range(NDC):
                    z = self.scr_f32(dc * TG, TG)
                    xs = self.xr_ap(dc, tg)
                    P.op("v", lambda e, z=z: e.tensor_tensor(z, z, self.PSB[0][:, :],
                                                             op=OP.subtract))
                    P.op("v", lambda e, z=z: e.tensor_tensor(z, z, self.PSB[1][:, :],
                                                             op=OP.mult))
                    P.op("v", lambda e, z=z, dc=dc:
                         e.tensor_scalar(z, z, GAM[:, dc:dc + 1], BET[:, dc:dc + 1],
                                         op0=OP.mult, op1=OP.add))
                    P.op("v", lambda e, z=z, xs=xs: e.tensor_add(xs, xs, z),
                         incs=(("dve", 1),))
                P.wait("t", "dve", P.val("dve"))
            P.join()

    def mha16(self, *, wq_src, wk_src, wv_src, bq_col=None, bk_col=None, bv_row=None,
              causal, scale, AOUTF, name="mha"):
        P, nc = self.P, self.nc
        GH, NHG, NCH = 4, 4, 2
        ind4_h = self.hbm_in("c_ind4", (128, 8))
        with contextlib.ExitStack() as sc:
            sb0 = lambda n_, shp, dt=F32: sc.enter_context(nc.sbuf_tensor(self.nm(n_), list(shp), dt))
            IND4 = sb0("ind4", [128, 8], BF16)
            dma(P, "g", lambda: IND4[:], lambda: ind4_h[:, :], "wg")
            for hg in range(NHG):
                hc = hg * 256
                with contextlib.ExitStack() as s2:
                    sb2 = lambda n_, shp, dt=F32: s2.enter_context(nc.sbuf_tensor(self.nm(n_), list(shp), dt))
                    KF = sb2("kf", [128, NCH * T], BF16)
                    VT = sb2("vt", [128, NTT * GH * 65], BF16)
                    QF = sb2("qf", [128, NCH * TG], BF16)
                    SQB = self.scr_bf16(0, 2 * NCH * TG)
                    ARAW = self.scr_f32(3072, 512)
                    KMAX = sb2("kmax", [128, 4])
                    NEGMB = sb2("negmb", [128, 1], BF16)
                    NEGM = sb2("negm", [128, GH])
                    NROW = sb2("nrow", [1, GH], BF16)
                    RECR = self.scr_f32(3584, 512, parts=1)
                    ST = [s2.enter_context(nc.psum_tensor(self.nm("st"), [128, 1024], F32))
                          for _ in range(2)]
                    PVP = s2.enter_context(nc.psum_tensor(self.nm("pv"), [128, 512], F32))
                    PTR = [s2.enter_context(nc.psum_tensor(self.nm("ptr"), [128, 128], BF16))
                           for _ in range(1)]
                    PT = [sb2("pt", [128, 1024], BF16) for _ in range(3)]

                    wq, wq_t, _ = self.load_w_bf16(s2, wq_src[0], D, 256,
                                                   col0=wq_src[1] + hc, name="wq")
                    wk, wk_t, _ = self.load_w_bf16(s2, wk_src[0], D, 256,
                                                   col0=wk_src[1] + hc, name="wk")
                    wv, wv_t, wva = self.load_w_bf16(s2, wv_src[0], D, 256,
                                                    col0=wv_src[1] + hc, name="wv")
                    P.wait("t", "wg", wva)

                    stpool = [ST[0][:, 0:512], ST[0][:, 512:1024],
                              ST[1][:, 0:512], ST[1][:, 512:1024]]
                    self.gemm_fm_to(nkc=NDC, nmc=NCH, xin_fn=self.xn_src,
                                    wtile_fn=lambda kc, mc: wk_t(kc, mc * 128, 128),
                                    dst_fn=lambda mc, tg: KF[:, mc * T + tg * TG: mc * T + (tg + 1) * TG],
                                    bias_fn=(None if bk_col is None else
                                             (lambda mc: bk_col[:, hg * NCH + mc: hg * NCH + mc + 1])),
                                    pool=stpool)
                    P.op("v", lambda e: e.memset(VT[:], 1.0), incs=(("dve", 1),))
                    P.wait("s", "dve", P.val("dve"))
                    for t2 in range(0, NTT, 2):
                        if t2 >= 4:
                            P.wait("t", "act", self._vt_done.get((t2 // 2) % 2, 0))
                        if t2 == 0:
                            self._vt_done = {}
                        bks = [stpool[(t2 + 0) % 4], stpool[(t2 + 1) % 4]]
                        def vmm(e, t2=t2, hc=hc, bks=bks):
                            inst = None
                            for k in range(2):
                                tt = t2 + k
                                for kc in range(NDC):
                                    inst = e.matmul(bks[k][:, 0:GH * 64],
                                                    self.xn_tile(kc, tt),
                                                    wv_t(kc), start=(kc == 0),
                                                    stop=(kc == NDC - 1 and bv_row is None))
                                if bv_row is not None:
                                    inst = e.matmul(bks[k][:, 0:GH * 64], self.ONES1B[:],
                                                    bv_row[0:1, hc:hc + GH * 64],
                                                    start=False, stop=True)
                            return inst
                        P.op("t", vmm, incs=(("pe", 1),))
                        P.wait("s", "pe", P.val("pe"))
                        def vevac(e, t2=t2, bks=bks):
                            inst = None
                            for k in range(2):
                                tt = t2 + k
                                vdst = VT[:, tt * GH * 65:(tt + 1) * GH * 65].rearrange(
                                    "p (h f) -> p h f", f=65)[:, :, 0:64]
                                inst = e.copy(vdst, bks[k][:, 0:GH * 64]
                                              .rearrange("p (h f) -> p h f", f=64))
                            return inst
                        P.op("s", vevac, incs=(("act", 1),))
                        self._vt_done[(t2 // 2) % 2] = P.val("act")
                    P.join(("t", "v", "s"))

                    for qg in range(NTG):
                        P.wait("t", "act", P.val("act"))
                        self.gemm_fm_to(nkc=NDC, nmc=NCH,
                                        xin_fn=lambda kc, tg, qg=qg: self.xn_src(kc, qg),
                                        wtile_fn=lambda kc, mc: wq_t(kc, mc * 128, 128),
                                        dst_fn=lambda mc, tg: QF[:, mc * TG:(mc + 1) * TG],
                                        bias_fn=(None if bq_col is None else
                                                 (lambda mc: bq_col[:, hg * NCH + mc: hg * NCH + mc + 1])),
                                        tgs=(0,), pool=stpool)
                        k3_fn = lambda tg: KF[:].rearrange("p (c t) -> p c t", c=NCH)[:, :, tg * TG:(tg + 1) * TG]
                        q3 = QF[:].rearrange("p (c t) -> p c t", c=NCH)
                        self.negm_content(GH, NCH, k3_fn, q3, IND4, KMAX, NEGM, NROW,
                                          NEGMB, ST[1], PTR, SQB, scale, first=(qg == 0))

                        def kf_fn(h, kt):
                            ch, ro = h // 2, 64 * (h % 2)
                            return KF[ro:ro + 64, ch * T + kt * 128: ch * T + (kt + 1) * 128]

                        def qf_fn(h, qg_):
                            ch, ro = h // 2, 64 * (h % 2)
                            return QF[ro:ro + 64, ch * TG:(ch + 1) * TG]

                        def v_fn(h, kt):
                            return VT[:, kt * GH * 65 + h * 65: kt * GH * 65 + (h + 1) * 65]

                        def out_fn(h, qg_, hg=hg, qg2=qg):
                            gch = hg * NCH + h // 2
                            ro = 64 * (h % 2)
                            return AOUTF[ro:ro + 64, gch * T + qg2 * TG:(gch * T + (qg2 + 1) * TG)]

                        self.attn_core(heads=range(GH), parts=[(kf_fn, qf_fn, 64)],
                                       v_slot_fn=v_fn,
                                       negm_ap_fn=lambda h: NEGM[:, h:h + 1],
                                       scale=scale, causal=causal,
                                       out_ap_fn=out_fn, qgs=(qg,),
                                       ST=ST, PV=PVP, PT=PT, araw=ARAW, recr=RECR)
                    P.join()
            P.join()

    def xn_src(self, kc, tg):
        return self._attn_in[:, kc * T + tg * TG: kc * T + (tg + 1) * TG]

    def xn_tile(self, kc, tt):
        return self._attn_in[:, kc * T + tt * 128: kc * T + (tt + 1) * 128]

    def gemm_fm_to(self, *, nkc, nmc, xin_fn, wtile_fn, dst_fn, tgs=None,
                   bias_fn=None, n=TG, pool=None):
        P = self.P
        tgs = list(range(NTG)) if tgs is None else list(tgs)
        tiles = [(mc, tg) for mc in range(nmc) for tg in tgs]
        if pool is None:
            pool = [self.PSB[0][:, 0:n], self.PSB[1][:, 0:n]]
        npool = len(pool)
        per = max(1, npool // 2)
        done = {}
        slot = 0
        for g0 in range(0, len(tiles), per):
            grp = tiles[g0:g0 + per]
            banks = []
            for k in range(len(grp)):
                b = (slot + k) % npool
                banks.append(pool[b])
                if b in done:
                    P.wait("t", "act", done[b])

            def mms(e, grp=grp, banks=banks):
                inst = None
                for (mc, tg), bk in zip(grp, banks):
                    for kc in range(nkc):
                        inst = e.matmul(bk, wtile_fn(kc, mc), xin_fn(kc, tg),
                                        start=(kc == 0), stop=(kc == nkc - 1))
                return inst
            P.op("t", mms, incs=(("pe", 1),))
            P.wait("s", "pe", P.val("pe"))

            def evacs(e, grp=grp, banks=banks):
                inst = None
                for (mc, tg), bk in zip(grp, banks):
                    if bias_fn is None:
                        inst = e.copy(dst_fn(mc, tg), bk)
                    else:
                        inst = e.activation(dst_fn(mc, tg), bk, AF.Identity,
                                            bias=bias_fn(mc))
                return inst
            P.op("s", evacs, incs=(("act", 1),))
            for k in range(len(grp)):
                done[(slot + k) % npool] = P.val("act")
            slot = (slot + len(grp)) % npool
        P.wait("t", "act", P.val("act"))


def _stage_inputs(inputs):
    d = dict(_host_constants())
    d["L0_ffn_w_gate_t"] = np.ascontiguousarray(inputs["L0_ffn_w_gate"].T)
    d["L0_ffn_w_up_t"] = np.ascontiguousarray(inputs["L0_ffn_w_up"].T)
    d["L0_ffn_w_down_t"] = np.ascontiguousarray(inputs["L0_ffn_w_down"].T)
    d["L2_attn_in_proj_t"] = np.ascontiguousarray(inputs["L2_attn_in_proj_weight"].T)
    d["L2_attn_out_proj_t"] = np.ascontiguousarray(inputs["L2_attn_out_proj_weight"].T)
    b1l2 = inputs["L2_ffn_b1"]
    d["L2_ffn_b1_pf"] = np.ascontiguousarray(
        b1l2.T.reshape(16, 128, 8).transpose(1, 0, 2).reshape(128, 128))
    b1 = inputs["L1_ffn_b1"]
    d["L1_ffn_b1_pf"] = np.ascontiguousarray(
        b1.T.reshape(2, 128, 16).transpose(1, 0, 2).reshape(128, 32))
    d["L3_ffn_gate_W_t"] = np.ascontiguousarray(inputs["L3_ffn_gate_W"].T)
    d["L3_ffn_W_gate_t"] = np.ascontiguousarray(inputs["L3_ffn_W_gate"].transpose(0, 2, 1))
    d["L3_ffn_W_up_t"] = np.ascontiguousarray(inputs["L3_ffn_W_up"].transpose(0, 2, 1))
    d["L3_ffn_W_down_t"] = np.ascontiguousarray(inputs["L3_ffn_W_down"].transpose(0, 2, 1))
    skip = {"L0_ffn_w_gate", "L0_ffn_w_up", "L0_ffn_w_down", "L0_attn_x",
            "L2_attn_key", "L2_attn_value", "L2_attn_in_proj_weight",
            "L2_attn_out_proj_weight", "L3_ffn_gate_W", "L3_ffn_W_gate",
            "L3_ffn_W_up", "L3_ffn_W_down"}
    for k, v in inputs.items():
        if k not in skip:
            d[k] = np.ascontiguousarray(v)
        if v.ndim == 1 and v.size % 128 == 0:
            d[k + "_pf"] = np.ascontiguousarray(v.reshape(-1, 128).T)
    return d


def kernel(**inputs):
    x = inputs["L0_attn_x"]
    staged = _stage_inputs(inputs)
    xT = [np.ascontiguousarray(x[c].T) for c in range(8)]

    groups = LAUNCH_GROUPS if LAUNCH_GROUPS is not None else [DO_LAYERS]
    cur = xT
    for g in groups:
        b = Build(g)
        nc = b.build()
        in_maps = []
        for c in range(8):
            m = {}
            for name in b.inputs:
                m[name] = cur[c] if name == "xin" else staged[name]
            in_maps.append(m)
        res = run_bass_kernel_spmd(nc, in_maps, core_ids=list(range(8)))
        cur = [np.asarray(res.results[c]["xout"]) for c in range(8)]
        for name in b.outputs:
            if name.startswith("tap_"):
                DEBUG_RESULTS[name[4:]] = [np.asarray(res.results[c][name]) for c in range(8)]
    out = np.stack([np.ascontiguousarray(cur[c].T) for c in range(8)], axis=0)
    return out.astype(np.float32)


# version 08
# speedup vs baseline: 1.0179x; 1.0179x over previous
import numpy as np
import contextlib

import concourse.bass as bass
import concourse.mybir as mybir
from concourse.bass_utils import run_bass_kernel_spmd

F32 = mybir.dt.float32
BF16 = mybir.dt.bfloat16
I32 = mybir.dt.int32
AF = mybir.ActivationFunctionType
OP = mybir.AluOpType

T = 2048
TG = 512
NTG = T // TG
NTT = T // 128
D = 1024
NDC = D // 128
LN_EPS = 1e-5

L0_DO_FFN = True
L0_DO_ATTN = True
DO_LAYERS = [0, 1, 2, 3]
LAUNCH_GROUPS = None
DEBUG_TAPS = False
DEBUG_RESULTS = {}


class Prog:
    ENGS = ("t", "v", "s", "g", "y")
    ESEM = {"t": "pe", "v": "dve", "s": "act", "g": "wg", "y": "wy"}

    def __init__(self, nc, stack):
        self.nc = nc
        self.stack = stack
        self.ops = {e: [] for e in self.ENGS}
        self.sems = {}
        self.touched = {e: False for e in self.ENGS}

    def sem(self, name):
        if name not in self.sems:
            self.sems[name] = [self.stack.enter_context(self.nc.semaphore(name)), 0]

    def val(self, name):
        return self.sems[name][1]

    def op(self, eng, build, incs=()):
        assert len(incs) <= 1, "hardware: one sem update per instruction"
        pairs = []
        for s, n in incs:
            rec = self.sems[s]
            rec[1] += n
            pairs.append((rec[0], n))
        self.touched[eng] = True

        def fn(e, build=build, pairs=pairs):
            inst = build(e)
            for h, n in pairs:
                inst.then_inc(h, n)
        self.ops[eng].append(fn)

    def wait(self, eng, sem, val=None):
        h, c = self.sems[sem]
        v = c if val is None else val
        if v <= 0:
            return
        self.ops[eng].append(lambda e, h=h, v=v: e.wait_ge(h, v))

    def join(self, engs=None):
        engs = self.ENGS if engs is None else engs
        vals = [(s, self.val(s)) for s in ("pe", "dve", "act", "wg", "wy", "ws", "gp", "epi")
                if s in self.sems and self.val(s) > 0]
        for e in engs:
            own = self.ESEM.get(e)
            for s, v in vals:
                if s != own:
                    self.wait(e, s, v)

    def emit(self, block):
        for key, meth in (("t", block.tensor), ("v", block.vector),
                          ("s", block.scalar), ("g", block.gpsimd),
                          ("y", block.sync)):
            fns = self.ops[key]
            if fns:
                def body(e, fns=tuple(fns)):
                    for f in fns:
                        f(e)
                meth(body)
        self.ops = {e: [] for e in self.ENGS}


def dma(P, q, out_fn, in_fn, sem):
    P.op(q, lambda e: e.dma_start(out=out_fn(), in_=in_fn()), incs=((sem, 16),))


def _host_constants():
    c = {}
    c["c_ident"] = np.eye(128, dtype=np.float32)
    c["c_tri"] = np.tril(np.ones((128, 128), dtype=np.float32)).T.copy()
    masks = np.zeros((4, 128, 512), dtype=np.float32)
    p = np.arange(128)[:, None]
    j = np.arange(512)[None, :]
    for i, delta in enumerate((0, 128, 256, 384)):
        masks[i] = ((p + delta) <= j).astype(np.float32)
    c["c_masks"] = masks.reshape(4 * 128, 512).copy()
    inv_freq = (1.0 / 10000.0 ** (np.arange(0, 32, 2, dtype=np.float32) / 32)).astype(np.float32)
    t = np.arange(T, dtype=np.float32)
    freqs = np.outer(t, inv_freq).astype(np.float32)
    c["c_cos0"] = np.cos(freqs).astype(np.float32)
    c["c_sin0"] = np.sin(freqs).astype(np.float32)
    ids = np.arange(512, dtype=np.float32)
    with np.errstate(under="ignore"):
        theta = (np.float32(10000.0) ** (np.float32(-2.0) * ids)).astype(np.float32) / np.float32(512.0)
    r = (t[:, None] * theta[None, :]).astype(np.float32)
    c["c_cos1"] = np.cos(r).astype(np.float32)
    c["c_sin1"] = np.sin(r).astype(np.float32)
    c["c_gidx"] = (np.arange(16)[None, :] * 128 + np.arange(128)[:, None]).astype(np.float32)
    ind4 = np.zeros((128, 8), dtype=np.float32)
    ind4r = np.zeros((128, 8), dtype=np.float32)
    for cc in range(2):
        ind4[0:64, cc * 4 + 2 * cc] = 1.0
        ind4[64:128, cc * 4 + 2 * cc + 1] = 1.0
        ind4r[0:32, cc * 4 + 2 * cc] = 1.0
        ind4r[64:96, cc * 4 + 2 * cc + 1] = 1.0
    c["c_ind4"] = ind4
    c["c_ind4r"] = ind4r
    ind8c = np.zeros((128, 32), dtype=np.float32)
    for cc in range(4):
        ind8c[0:64, cc * 8 + 2 * cc] = 1.0
        ind8c[64:128, cc * 8 + 2 * cc + 1] = 1.0
    c["c_ind8c"] = ind8c
    esel = np.zeros((16, 16 * 128), dtype=np.float32)
    for e in range(16):
        esel[e, e * 128:(e + 1) * 128] = 1.0
    c["c_esel16"] = esel
    esel8 = np.zeros((8, 8 * 128), dtype=np.float32)
    for e in range(8):
        esel8[e, e * 128:(e + 1) * 128] = 1.0
    c["c_esel8"] = esel8
    return c


class Build:

    def __init__(self, layers):
        self.layers = layers
        self.nc = bass.Bass("TRN2", target_bir_lowering=False, debug=False)
        self.inputs = {}
        self.outputs = {}
        self.stack = contextlib.ExitStack()
        self.uid = 0

    def hbm_in(self, name, shape, dtype=F32):
        if name not in self.inputs:
            self.inputs[name] = self.nc.dram_tensor(name, list(shape), dtype, kind="ExternalInput").ap()
        return self.inputs[name]

    def hbm_out(self, name, shape, dtype=F32):
        if name not in self.outputs:
            self.outputs[name] = self.nc.dram_tensor(name, list(shape), dtype, kind="ExternalOutput").ap()
        return self.outputs[name]

    def tap(self, name, ap_fn, shape):
        if not DEBUG_TAPS:
            return
        P = self.P
        ap0 = ap_fn()
        out = self.hbm_out("tap_" + name, list(shape), ap0.dtype)
        for sem in ("pe", "dve", "act", "wg", "epi"):
            P.wait("y", sem, P.val(sem))
        dma(P, "y", lambda: out[:, :] if len(shape) == 2 else out[:], ap_fn, "xq")
        xv = P.val("xq")
        for e in ("t", "v", "s", "g"):
            P.wait(e, "xq", xv)

    def nm(self, base):
        self.uid += 1
        return "%s_%d" % (base, self.uid)

    def build(self):
        nc = self.nc
        st = self.stack
        with st:
            P = Prog(nc, st)
            for s in ("pe", "act", "dve", "gp", "wg", "wy", "ws", "epi", "xq"):
                P.sem(s)
            self.P = P

            def sb(name, shape, dt=F32):
                return st.enter_context(nc.sbuf_tensor(self.nm(name), list(shape), dt))

            def psm(name, shape=(128, 512), dt=F32):
                return st.enter_context(nc.psum_tensor(self.nm(name), list(shape), dt))
            self.sb_outer, self.ps_outer = sb, psm

            self.XR = sb("XR", [128, NDC * T])
            self.XN = sb("XN", [128, NDC * T], BF16)
            self.MR = sb("MR", [65, T])
            self.ONESP = sb("ONESP", [65, 128])
            self.ONES1B = sb("ONES1B", [1, 128], BF16)
            self.ONESC = sb("ONESC", [128, 1], BF16)
            self.IDENT = sb("IDENT", [128, 128], BF16)
            self.MASKS = sb("MASKS", [128, 4 * 512], BF16)
            self.SCR = sb("SCR", [128, NDC * TG + 512])
            self.CPOOL = sb("CPOOL", [128, 16])
            self.TRS = sb("TRS", [32, 32], BF16)
            self._consts = {}
            for v in (1e-5, 1e-6):
                self._consts[v] = len(self._consts)
                P.op("v", lambda e, j=self._consts[v], v=v:
                     e.memset(self.CPOOL[:, j:j + 1], float(v)), incs=(("dve", 1),))
            self.PSB = [psm("PSB") for _ in range(2)]

            xin = self.hbm_in("xin", [D, T])
            dma(P, "y", lambda: self.XR[:].rearrange("p (c t) -> p c t", c=NDC),
                lambda: xin.rearrange("(c p) t -> p c t", p=128), "wy")
            ident_h = self.hbm_in("c_ident", (128, 128))
            masks_h = self.hbm_in("c_masks", (4 * 128, 512))
            dma(P, "g", lambda: self.IDENT[:], lambda: ident_h[:, :], "wg")
            dma(P, "g", lambda: self.MASKS[:].rearrange("p (i n) -> p i n", i=4),
                lambda: masks_h.rearrange("(i p) n -> p i n", p=128), "wg")
            P.op("v", lambda e: e.memset(self.ONESP[:], 0.0))
            P.op("v", lambda e: e.memset(self.ONESP[0:1, :], 1.0))
            P.op("v", lambda e: e.memset(self.ONESP[64:65, :], 1.0))
            P.op("v", lambda e: e.memset(self.ONES1B[:], 1.0))
            P.op("v", lambda e: e.memset(self.ONESC[:], 1.0), incs=(("dve", 1),))
            P.join()

            self.MU = self.MR[0:1, :]
            self.SQR = self.MR[32:33, :]
            self.RR = self.MR[64:65, :]
            for li in self.layers:
                getattr(self, "layer%d" % li)()

            xout = self.hbm_out("xout", [D, T])
            P.join(("y",))
            dma(P, "y", lambda: xout.rearrange("(c p) t -> p c t", p=128),
                lambda: self.XR[:].rearrange("p (c t) -> p c t", c=NDC), "xq")
            P.wait("y", "xq", P.val("xq"))

            with nc.Block() as block:
                P.emit(block)
        return nc

    def scr_f32(self, c0, n, parts=128):
        return self.SCR[0:parts, c0:c0 + n]

    def scr_bf16(self, c0_f32, n_bf, parts=128):
        b = self.SCR[0:parts, :].bitcast(BF16)
        return b[:, 2 * c0_f32: 2 * c0_f32 + n_bf]

    def const_col(self, value, parts=128):
        j = self._consts[value]
        return self.CPOOL[0:parts, j:j + 1]

    def xr_ap(self, c, tg=None, n=TG):
        if tg is None:
            return self.XR[:, c * T:(c + 1) * T]
        return self.XR[:, c * T + tg * n: c * T + tg * n + n]

    def xn_ap(self, c, tg=None, n=TG):
        if tg is None:
            return self.XN[:, c * T:(c + 1) * T]
        return self.XN[:, c * T + tg * n: c * T + tg * n + n]

    def layernorm(self, xn_out, eps=LN_EPS):
        P, nc = self.P, self.nc
        murow, sqrow, rrow = self.MU, self.SQR, self.RR
        for tg in range(NTG):
            xbf = self.scr_bf16(0, NDC * TG)
            xsq = self.scr_bf16(NDC * TG // 2, NDC * TG)
            xr3 = self.XR[:].rearrange("p (c t) -> p c t", c=NDC)[:, :, tg * TG:(tg + 1) * TG]
            xbf3 = xbf.rearrange("p (c t) -> p c t", c=NDC)
            xsq3 = xsq.rearrange("p (c t) -> p c t", c=NDC)
            P.wait("v", "pe", P.val("pe"))
            P.wait("s", "pe", P.val("pe"))
            P.op("v", lambda e, a=xbf3, b=xr3: e.tensor_copy(a, b), incs=(("dve", 1),))
            P.op("s", lambda e, a=xsq3, b=xr3: e.activation(a, b, AF.Square), incs=(("act", 1),))
            P.wait("t", "dve", P.val("dve"))
            P.wait("t", "act", P.val("act"))
            for which, src, dst in ((0, xbf, murow), (1, xsq, sqrow)):
                bank = self.PSB[which]
                def mms(e, src=src, bank=bank):
                    inst = None
                    for c in range(NDC):
                        inst = e.matmul(bank[0:1, 0:512], self.ONESC[:],
                                        src[:, c * TG:(c + 1) * TG],
                                        start=(c == 0), stop=(c == NDC - 1))
                    return inst
                P.op("t", mms, incs=(("pe", 1),))
                P.wait("s", "pe", P.val("pe"))
                P.op("s", lambda e, bank=bank, dst=dst, tg=tg:
                     e.activation(dst[:, tg * TG:(tg + 1) * TG], bank[0:1, 0:512],
                                  AF.Identity, scale=1.0 / 1024.0), incs=(("act", 1),))
        P.wait("v", "act", P.val("act"))
        musq = self.SCR[32:33, 0:T]
        P.op("v", lambda e: e.tensor_mul(musq, murow, murow), incs=(("dve", 1),))
        P.wait("v", "dve", P.val("dve"))
        P.op("v", lambda e: e.tensor_sub(sqrow, sqrow, musq), incs=(("dve", 1),))
        P.wait("s", "dve", P.val("dve"))
        P.op("s", lambda e: e.activation(sqrow, sqrow, AF.Ln,
                                         bias=self.CPOOL[32:33, self._consts[eps]:self._consts[eps] + 1]))
        P.op("s", lambda e: e.activation(rrow, sqrow, AF.Exp, scale=-0.5),
             incs=(("act", 1),))
        P.wait("t", "act", P.val("act"))
        for tg in range(NTG):
            self.bcast_tg(tg)
            xr3 = self.XR[:].rearrange("p (c t) -> p c t", c=NDC)[:, :, tg * TG:(tg + 1) * TG]
            scr3 = self.scr_f32(0, NDC * TG).rearrange("p (c t) -> p c t", c=NDC)
            xn3 = xn_out[:].rearrange("p (c t) -> p c t", c=NDC)[:, :, tg * TG:(tg + 1) * TG]
            mu_b = self.PSB[0][:, :].rearrange("p t -> p () t").broadcast_to([128, NDC, TG])
            r_b = self.PSB[1][:, :].rearrange("p t -> p () t").broadcast_to([128, NDC, TG])
            P.op("v", lambda e, a=scr3, b=xr3, m=mu_b: e.tensor_tensor(a, b, m, op=OP.subtract))
            P.op("v", lambda e, a=xn3, b=scr3, r=r_b: e.tensor_tensor(a, b, r, op=OP.mult),
                 incs=(("dve", 1),))
        P.join()

    def bcast_tg(self, tg):
        P = self.P
        P.wait("t", "dve", P.val("dve"))

        def k1(e, tg=tg):
            e.matmul(self.PSB[0][:, :], self.ONESP[0:1, :],
                     self.MU[:, tg * TG:(tg + 1) * TG], start=True, stop=True)
            return e.matmul(self.PSB[1][:, :], self.ONESP[64:65, :],
                            self.RR[:, tg * TG:(tg + 1) * TG], start=True, stop=True)
        P.op("t", k1, incs=(("pe", 1),))
        P.wait("v", "pe", P.val("pe"))

    def xn32_into(self, dst_ap, c, tg):
        P = self.P
        xr = self.xr_ap(c, tg)
        P.op("v", lambda e: e.tensor_tensor(dst_ap, xr, self.PSB[0][:, :], op=OP.subtract))
        P.op("v", lambda e: e.tensor_tensor(dst_ap, dst_ap, self.PSB[1][:, :], op=OP.mult),
             incs=(("dve", 1),))

    def layer0(self):
        self.layernorm(self.XN)
        if L0_DO_FFN:
            self.l0_ffn()
        if L0_DO_ATTN:
            self.l0_attn()

    def l0_ffn(self):
        P, nc = self.P, self.nc
        S = 1.0 / 127.0
        wg_h = self.hbm_in("L0_ffn_w_gate_t", [D, 3072], I32)
        wu_h = self.hbm_in("L0_ffn_w_up_t", [D, 3072], I32)
        wd_h = self.hbm_in("L0_ffn_w_down_t", [3072, D], I32)
        NH = 24
        with contextlib.ExitStack() as sc:
            H = sc.enter_context(nc.sbuf_tensor(self.nm("l0_h"), [128, NH * 1024], BF16))
            WB = [sc.enter_context(nc.sbuf_tensor(self.nm("l0_wb"), [128, NDC * 128], BF16))
                  for _ in range(4)]
            WD = [sc.enter_context(nc.sbuf_tensor(self.nm("l0_wd"), [128, NH * 128], BF16))
                  for _ in range(2)]
            PSG = [sc.enter_context(nc.psum_tensor(self.nm("l0psg"), [128, 512], F32))
                   for _ in range(2)]
            PSU = [sc.enter_context(nc.psum_tensor(self.nm("l0psu"), [128, 512], F32))
                   for _ in range(2)]
            for th in range(2):
                pe_after_fc = {}
                h_done = {}
                for fc in range(NH):
                    bsel = fc % 2
                    if fc >= 2:
                        P.wait("g", "pe", pe_after_fc[fc - 2])
                    for w_hbm, buf in ((wg_h, WB[bsel * 2]), (wu_h, WB[bsel * 2 + 1])):
                        dma(P, "g", lambda b=buf: b[:].rearrange("p (k m) -> p k m", m=128),
                            lambda w=w_hbm, fc=fc: w[:, fc * 128:(fc + 1) * 128]
                            .rearrange("(k p) m -> p k m", p=128), "wg")
                    wg_val = P.val("wg")
                    for sub in range(2):
                        tg = th * 2 + sub
                        P.wait("t", "wg", wg_val)
                        if (fc - 1, sub) in h_done:
                            P.wait("t", "dve", h_done[(fc - 1, sub)])

                        def mms(e, fc=fc, tg=tg, bsel=bsel, sub=sub):
                            for kc in range(NDC):
                                e.matmul(PSG[sub][:, :], WB[bsel * 2][:, kc * 128:(kc + 1) * 128],
                                         self.xn_ap(kc, tg), start=(kc == 0), stop=(kc == NDC - 1))
                            inst = None
                            for kc in range(NDC):
                                inst = e.matmul(PSU[sub][:, :], WB[bsel * 2 + 1][:, kc * 128:(kc + 1) * 128],
                                                self.xn_ap(kc, tg), start=(kc == 0), stop=(kc == NDC - 1))
                            return inst
                        P.op("t", mms, incs=(("pe", 1),))
                        pe_after_fc[fc] = P.val("pe")
                        P.wait("s", "pe", P.val("pe"))
                        hslice = H[:, fc * 1024 + sub * TG: fc * 1024 + sub * TG + TG]
                        P.op("s", lambda e, hs=hslice, sub=sub:
                             e.activation(hs, PSG[sub][:, :], AF.Silu, scale=S),
                             incs=(("act", 1),))
                        P.wait("v", "act", P.val("act"))
                        P.op("v", lambda e, hs=hslice, sub=sub:
                             e.tensor_tensor(hs, hs, PSU[sub][:, :], op=OP.mult),
                             incs=(("dve", 1),))
                        h_done[(fc, sub)] = P.val("dve")
                h_ready = P.val("dve")
                epi_base = P.val("epi")
                pe_after_mc = {}
                for mc in range(NDC):
                    bsel = mc % 2
                    if mc >= 2:
                        P.wait("g", "pe", pe_after_mc[mc - 2])
                    dma(P, "g", lambda b=WD[bsel]: b[:].rearrange("p (k m) -> p k m", m=128),
                        lambda mc=mc: wd_h[:, mc * 128:(mc + 1) * 128]
                        .rearrange("(k p) m -> p k m", p=128), "wg")
                    wg_val = P.val("wg")
                    for sub in range(2):
                        tg = th * 2 + sub
                        idx = mc * 2 + sub
                        P.wait("t", "wg", wg_val)
                        if idx == 0:
                            P.wait("t", "dve", h_ready)
                        if idx >= 2:
                            P.wait("t", "epi", epi_base + idx - 1)

                        def mms(e, sub=sub, bsel=bsel):
                            inst = None
                            for kc in range(NH):
                                inst = e.matmul(PSG[sub][:, :], WD[bsel][:, kc * 128:(kc + 1) * 128],
                                                H[:, kc * 1024 + sub * TG: kc * 1024 + sub * TG + TG],
                                                start=(kc == 0), stop=(kc == NH - 1))
                            return inst
                        P.op("t", mms, incs=(("pe", 1),))
                        pe_after_mc[mc] = P.val("pe")
                        P.wait("v", "pe", P.val("pe"))
                        xs = self.xr_ap(mc, tg)
                        P.op("v", lambda e, xs=xs, sub=sub:
                             e.scalar_tensor_tensor(xs, PSG[sub][:, :], S * S, xs,
                                                    op0=OP.mult, op1=OP.add),
                             incs=(("epi", 1),))
                P.wait("t", "epi", epi_base + 16)
            P.join()


    def load_w_bf16(self, sc, hbm_ap, rows, cols, row0=0, col0=0, name="w", hbm_pre=None):
        P, nc = self.P, self.nc
        nkc = rows // 128
        wsb = sc.enter_context(nc.sbuf_tensor(self.nm(name), [128, nkc * cols], BF16))

        def src():
            ap = hbm_ap if hbm_pre is None else hbm_pre(hbm_ap)
            return ap[row0:row0 + rows, col0:col0 + cols].rearrange("(k p) m -> p k m", p=128)
        dma(P, "g", lambda: wsb[:].rearrange("p (k m) -> p k m", m=cols), src, "wg")
        wv = P.val("wg")

        def tile(kc, c0=0, nc_=cols):
            return wsb[:, kc * cols + c0: kc * cols + c0 + nc_]
        return wsb, tile, wv

    def load_col_f32_shaped(self, sc, in_name, shape, name="bcol"):
        P, nc = self.P, self.nc
        hbm_ap = self.hbm_in(in_name, shape)
        csb = sc.enter_context(nc.sbuf_tensor(self.nm(name), list(shape), F32))
        dma(P, "g", lambda: csb[:], lambda: hbm_ap[:, :], "wg")
        return csb, P.val("wg")

    def load_row_bf16(self, sc, hbm_ap, n, name="brow"):
        P, nc = self.P, self.nc
        rsb = sc.enter_context(nc.sbuf_tensor(self.nm(name), [1, n], BF16))
        dma(P, "g", lambda: rsb[0:1, :], lambda: hbm_ap.rearrange("n -> () n"), "wg")
        return rsb, P.val("wg")

    def load_col_f32(self, sc, in_name, n, name="bcol"):
        P, nc = self.P, self.nc
        ncc = n // 128
        hbm_ap = self.hbm_in(in_name + "_pf", (128, ncc))
        csb = sc.enter_context(nc.sbuf_tensor(self.nm(name), [128, ncc], F32))
        dma(P, "g", lambda: csb[:], lambda: hbm_ap[:, :], "wg")
        return csb, P.val("wg")

    def attn_core(self, *, heads, parts, v_slot_fn, negm_ap_fn, scale, causal,
                  out_ap_fn, qgs, ST, PV, PT, araw, recr):
        P = self.P
        pend = [None]
        exp_done, pv_done, gate = {}, {}, {}
        jj = [0]

        def flush_pending():
            if pend[0] is not None:
                fn = pend[0]
                pend[0] = None
                fn()

        for qg in qgs:
            kts = list(range(min(NTT, 4 * qg + 4))) if causal else list(range(NTT))
            np_ = len(kts) // 2
            for h in heads:
                j0 = jj[0]

                def emit_st(lj, h=h, qg=qg, kts=kts, j0=j0):
                    j = j0 + lj
                    s2 = j % 2
                    if j >= 2:
                        P.wait("t", "act", exp_done[j - 2])
                    def st_mms(e, lj=lj, s2=s2):
                        inst = None
                        for half in range(2):
                            kt = kts[2 * lj + half]
                            for pi, (kfn, qfn, kdim) in enumerate(parts):
                                inst = e.matmul(ST[s2][:, half * 512:(half + 1) * 512],
                                                kfn(h, kt), qfn(h, qg),
                                                start=(pi == 0), stop=(pi == len(parts) - 1))
                        return inst
                    P.op("t", st_mms, incs=(("pe", 1),))
                    P.wait("s", "pe", P.val("pe"))
                    if j >= 3:
                        P.wait("s", "pe", pv_done[j - 3])
                    s3 = j % 3
                    P.op("s", lambda e, s2=s2, s3=s3, h=h:
                         e.activation(PT[s3][:, :], ST[s2][:, :], AF.Exp,
                                      bias=negm_ap_fn(h), scale=scale),
                         incs=(("act", 1),))
                    exp_done[j] = P.val("act")
                    need_mask = causal and kts[2 * lj + 1] >= 4 * qg
                    if need_mask:
                        P.wait("v", "act", exp_done[j])
                        def mask_ops(e, lj=lj, s3=s3, qg=qg, kts=kts):
                            inst = None
                            for half in range(2):
                                kt = kts[2 * lj + half]
                                if kt >= 4 * qg:
                                    mi = kt - 4 * qg
                                    inst = e.tensor_tensor(
                                        PT[s3][:, half * 512:(half + 1) * 512],
                                        PT[s3][:, half * 512:(half + 1) * 512],
                                        self.MASKS[:, mi * 512:(mi + 1) * 512],
                                        op=OP.mult)
                            return inst
                        P.op("v", mask_ops, incs=(("dve", 1),))
                        gate[j] = ("dve", P.val("dve"))
                    else:
                        gate[j] = ("act", exp_done[j])

                def emit_pv(lj, h=h, first=False, last=False, kts=kts, j0=j0):
                    j = j0 + lj
                    sem, v = gate[j]
                    P.wait("t", sem, v)
                    def pv_mms(e, lj=lj, s3=j % 3, first=first, last=last):
                        inst = None
                        for half in range(2):
                            inst = e.matmul(PV[0:65, :], v_slot_fn(h, kts[2 * lj + half]),
                                            PT[s3][:, half * 512:(half + 1) * 512],
                                            start=(first and half == 0),
                                            stop=(last and half == 1))
                        return inst
                    P.op("t", pv_mms, incs=(("pe", 1),))
                    pv_done[j] = P.val("pe")

                emit_st(0)
                flush_pending()
                for lj in range(1, np_):
                    emit_st(lj)
                    emit_pv(lj - 1, first=(lj == 1), last=False)
                if np_ == 1:
                    emit_pv(0, first=True, last=True)
                else:
                    emit_pv(np_ - 1, first=False, last=True)
                jj[0] = j0 + np_

                P.wait("s", "pe", pv_done[j0 + np_ - 1])
                P.op("s", lambda e: e.copy(araw[0:65, :], PV[0:65, :]), incs=(("act", 1),))
                if getattr(self, "_tap_araw", False):
                    self._tap_araw = False
                    self.tap("araw", lambda: araw[0:65, :], (65, 512))
                evac_act = P.val("act")
                P.wait("v", "act", evac_act)
                P.op("v", lambda e: e.reciprocal(recr[0:1, :], araw[64:65, :]),
                     incs=(("dve", 1),))
                recip_dve = P.val("dve")

                def normalize(h=h, qg=qg, recip_dve=recip_dve):
                    P.wait("t", "dve", recip_dve)
                    P.op("t", lambda e: e.matmul(self.PSB[0][0:64, :], self.ONESP[0:1, 0:64],
                                                 recr[0:1, :], start=True, stop=True),
                         incs=(("pe", 1),))
                    P.wait("v", "pe", P.val("pe"))
                    P.op("v", lambda e, h=h, qg=qg:
                         e.tensor_tensor(out_ap_fn(h, qg), araw[0:64, :],
                                         self.PSB[0][0:64, :], op=OP.mult),
                         incs=(("dve", 1),))
                    P.wait("s", "dve", P.val("dve"))
                pend[0] = normalize
        flush_pending()
        P.wait("t", "dve", P.val("dve"))

    def attn_core_paired(self, *, npairs, kf_even, kf_odd, qf_even, qf_odd,
                         v_slot_fn, negmp_ap_fn, scale, causal, out_ap_fn, qg,
                         ST, PVE, PVO, PT, araw_e, araw_o, recr_e, recr_o):
        P = self.P
        pend = [None]
        exp_done, pv_done, gate = {}, {}, {}
        jj = [0]

        def flush_pending():
            if pend[0] is not None:
                fn = pend[0]
                pend[0] = None
                fn()

        kts = list(range(min(NTT, 4 * qg + 4))) if causal else list(range(NTT))
        n = len(kts)
        for hp in range(npairs):
            j0 = jj[0]

            def emit_st(i, hp=hp, j0=j0):
                j = j0 + i
                kt = kts[i]
                s2 = j % 2
                if j >= 2:
                    P.wait("t", "act", exp_done[j - 2])
                def st_mms(e, kt=kt, hp=hp, s2=s2):
                    e.matmul(ST[s2][:, 0:512], kf_even(hp, kt), qf_even(hp),
                             start=True, stop=True)
                    return e.matmul(ST[s2][:, 512:1024], kf_odd(hp, kt), qf_odd(hp),
                                    start=True, stop=True)
                P.op("t", st_mms, incs=(("pe", 1),))
                P.wait("s", "pe", P.val("pe"))
                if j >= 3:
                    P.wait("s", "pe", pv_done[j - 3])
                s3 = j % 3
                P.op("s", lambda e, s2=s2, s3=s3, hp=hp:
                     e.activation(PT[s3][:, :], ST[s2][:, :], AF.Exp,
                                  bias=negmp_ap_fn(hp), scale=scale),
                     incs=(("act", 1),))
                exp_done[j] = P.val("act")
                if causal and kt >= 4 * qg:
                    mi = kt - 4 * qg
                    P.wait("v", "act", exp_done[j])
                    pt3 = PT[s3][:, :].rearrange("p (h t) -> p h t", h=2)
                    mb = self.MASKS[:, mi * 512:(mi + 1) * 512].rearrange(
                        "p t -> p () t").broadcast_to([128, 2, 512])
                    P.op("v", lambda e, pt3=pt3, mb=mb:
                         e.tensor_tensor(pt3, pt3, mb, op=OP.mult), incs=(("dve", 1),))
                    gate[j] = ("dve", P.val("dve"))
                else:
                    gate[j] = ("act", exp_done[j])

            def emit_pv(i, hp=hp, first=False, last=False, j0=j0):
                j = j0 + i
                kt = kts[i]
                sem, v = gate[j]
                P.wait("t", sem, v)
                def pv_mms(e, kt=kt, hp=hp, s3=j % 3, first=first, last=last):
                    e.matmul(PVE[0:65, :], v_slot_fn(2 * hp, kt), PT[s3][:, 0:512],
                             start=first, stop=last)
                    return e.matmul(PVO[0:65, :], v_slot_fn(2 * hp + 1, kt),
                                    PT[s3][:, 512:1024], start=first, stop=last)
                P.op("t", pv_mms, incs=(("pe", 1),))
                pv_done[j] = P.val("pe")

            emit_st(0)
            flush_pending()
            for i in range(1, n):
                emit_st(i)
                emit_pv(i - 1, first=(i == 1), last=False)
            if n == 1:
                emit_pv(0, first=True, last=True)
            else:
                emit_pv(n - 1, first=False, last=True)
            jj[0] = j0 + n

            P.wait("s", "pe", pv_done[j0 + n - 1])
            P.op("s", lambda e: e.copy(araw_e[0:65, :], PVE[0:65, :]))
            P.op("s", lambda e: e.copy(araw_o[0:65, :], PVO[0:65, :]), incs=(("act", 1),))
            P.wait("v", "act", P.val("act"))
            P.op("v", lambda e: e.reciprocal(recr_e[0:1, :], araw_e[64:65, :]))
            P.op("v", lambda e: e.reciprocal(recr_o[0:1, :], araw_o[64:65, :]),
                 incs=(("dve", 1),))
            recip_dve = P.val("dve")

            def normalize(hp=hp, qg=qg, recip_dve=recip_dve):
                P.wait("t", "dve", recip_dve)
                def bc(e):
                    e.matmul(self.PSB[0][0:64, :], self.ONESP[0:1, 0:64],
                             recr_e[0:1, :], start=True, stop=True)
                    return e.matmul(self.PSB[1][0:64, :], self.ONESP[0:1, 0:64],
                                    recr_o[0:1, :], start=True, stop=True)
                P.op("t", bc, incs=(("pe", 1),))
                P.wait("v", "pe", P.val("pe"))
                P.op("v", lambda e, hp=hp, qg=qg:
                     e.tensor_tensor(out_ap_fn(2 * hp, qg), araw_e[0:64, :],
                                     self.PSB[0][0:64, :], op=OP.mult))
                P.op("v", lambda e, hp=hp, qg=qg:
                     e.tensor_tensor(out_ap_fn(2 * hp + 1, qg), araw_o[0:64, :],
                                     self.PSB[1][0:64, :], op=OP.mult),
                     incs=(("dve", 1),))
                P.wait("s", "dve", P.val("dve"))
            pend[0] = normalize
        flush_pending()
        P.wait("t", "dve", P.val("dve"))

    def rope_apply_l0(self, src_f32, dst_bf, cos_ap, sin_ap, nheads, scr):
        P = self.P
        a = src_f32.rearrange("p (h f) -> p h f", f=32)[:, :, 0:16]
        b = src_f32.rearrange("p (h f) -> p h f", f=32)[:, :, 16:32]
        oa = dst_bf.rearrange("p (h f) -> p h f", f=32)[:, :, 0:16]
        ob = dst_bf.rearrange("p (h f) -> p h f", f=32)[:, :, 16:32]
        cosb = cos_ap.rearrange("p f -> p () f").broadcast_to([128, nheads, 16])
        sinb = sin_ap.rearrange("p f -> p () f").broadcast_to([128, nheads, 16])
        m1 = scr[:, 0:nheads * 16].rearrange("p (h f) -> p h f", f=16)
        m2 = scr[:, nheads * 16:nheads * 32].rearrange("p (h f) -> p h f", f=16)
        P.op("v", lambda e: e.tensor_tensor(m1, a, cosb, op=OP.mult))
        P.op("v", lambda e: e.tensor_tensor(m2, b, sinb, op=OP.mult))
        P.op("v", lambda e: e.tensor_tensor(oa, m1, m2, op=OP.subtract))
        P.op("v", lambda e: e.tensor_tensor(m1, b, cosb, op=OP.mult))
        P.op("v", lambda e: e.tensor_tensor(m2, a, sinb, op=OP.mult))
        P.op("v", lambda e: e.tensor_tensor(ob, m1, m2, op=OP.add), incs=(("dve", 1),))

    def l0_attn(self):
        P, nc = self.P, self.nc
        GH = 4
        NHG = 16 // GH
        NCH = GH // 2
        scale = 1.0 / float(np.sqrt(96.0))
        W_kv = self.hbm_in("L0_attn_W_kv", (D, 256))
        W_kdec = self.hbm_in("L0_attn_W_kdec", (256, D))
        W_vdec = self.hbm_in("L0_attn_W_vdec", (256, D))
        W_q = self.hbm_in("L0_attn_W_q", (D, 512))
        W_qdec = self.hbm_in("L0_attn_W_qdec", (512, D))
        W_krope = self.hbm_in("L0_attn_W_krope", (D, 512))
        W_qrope = self.hbm_in("L0_attn_W_qrope", (512, 512))
        W_o = self.hbm_in("L0_attn_W_o", (D, D))
        cos0_h = self.hbm_in("c_cos0", (T, 16))
        sin0_h = self.hbm_in("c_sin0", (T, 16))
        ind4_h = self.hbm_in("c_ind4", (128, 8))
        ind4r_h = self.hbm_in("c_ind4r", (128, 8))

        with contextlib.ExitStack() as sc:
            sbl = lambda n, shp, dt=F32: sc.enter_context(nc.sbuf_tensor(self.nm(n), list(shp), dt))
            KVN = sbl("kvn", [128, 2 * T], BF16)
            QCQG = sbl("qcqg", [128, 4 * TG], BF16)
            COS0 = sbl("cos0", [128, NTT * 16])
            SIN0 = sbl("sin0", [128, NTT * 16])
            IND4 = sbl("ind4", [128, 8], BF16)
            IND4R = sbl("ind4r", [128, 8], BF16)
            dma(P, "g", lambda: COS0[:].rearrange("p (t f) -> p t f", f=16),
                lambda: cos0_h.rearrange("(t p) f -> p t f", p=128), "wg")
            dma(P, "g", lambda: SIN0[:].rearrange("p (t f) -> p t f", f=16),
                lambda: sin0_h.rearrange("(t p) f -> p t f", p=128), "wg")
            dma(P, "g", lambda: IND4[:], lambda: ind4_h[:, :], "wg")
            dma(P, "g", lambda: IND4R[:], lambda: ind4r_h[:, :], "wg")
            gkv_sb, gkv_wv = self.load_col_f32(sc, "L0_attn_g_kv", 256, "gkv")

            with contextlib.ExitStack() as s2:
                wkv, wkv_t, wkv_wv = self.load_w_bf16(s2, W_kv, D, 256, name="wkv")
                kvs = s2.enter_context(nc.sbuf_tensor(self.nm("kvscr"), [128, 2 * 512], F32))
                sqb = s2.enter_context(nc.sbuf_tensor(self.nm("kvsq"), [128, 2 * 512], BF16))
                srow = s2.enter_context(nc.sbuf_tensor(self.nm("kvr"), [1, 512], F32))
                P.wait("t", "wg", wkv_wv)
                for tg in range(NTG):
                    P.wait("t", "epi", P.val("epi"))
                    for mc in range(2):
                        def mms(e, mc=mc, tg=tg):
                            inst = None
                            for kc in range(NDC):
                                inst = e.matmul(self.PSB[mc][:, :], wkv_t(kc, mc * 128, 128),
                                                self.xn_ap(kc, tg), start=(kc == 0),
                                                stop=(kc == NDC - 1))
                            return inst
                        P.op("t", mms, incs=(("pe", 1),))
                    pev = P.val("pe")
                    P.wait("s", "pe", pev)
                    P.op("s", lambda e: e.copy(kvs[:, 0:512], self.PSB[0][:, :]))
                    P.op("s", lambda e: e.copy(kvs[:, 512:1024], self.PSB[1][:, :]))
                    P.op("s", lambda e: e.activation(sqb[:, 0:512], self.PSB[0][:, :], AF.Square))
                    P.op("s", lambda e: e.activation(sqb[:, 512:1024], self.PSB[1][:, :], AF.Square),
                         incs=(("act", 1),))
                    P.wait("t", "act", P.val("act"))
                    def smm(e):
                        e.matmul(self.PSB[0][0:1, 0:512], self.ONESC[:], sqb[:, 0:512],
                                 start=True, stop=False)
                        return e.matmul(self.PSB[0][0:1, 0:512], self.ONESC[:], sqb[:, 512:1024],
                                        start=False, stop=True)
                    P.op("t", smm, incs=(("pe", 1),))
                    P.wait("s", "pe", P.val("pe"))
                    P.op("s", lambda e: e.activation(srow[0:1, :], self.PSB[0][0:1, 0:512],
                                                     AF.Identity, scale=1.0 / 256.0),
                         incs=(("act", 1),))
                    P.wait("s", "act", P.val("act"))
                    P.op("s", lambda e: e.activation(srow[0:1, :], srow[0:1, :], AF.Ln,
                                                     bias=self.const_col(1e-6, parts=1)))
                    P.op("s", lambda e: e.activation(srow[0:1, :], srow[0:1, :], AF.Exp,
                                                     scale=-0.5), incs=(("act", 1),))
                    P.wait("t", "act", P.val("act"))
                    P.op("t", lambda e: e.matmul(self.PSB[0][:, :], self.ONESP[0:1, :],
                                                 srow[0:1, :], start=True, stop=True),
                         incs=(("pe", 1),))
                    P.wait("v", "pe", P.val("pe"))
                    P.wait("v", "wg", gkv_wv)
                    for mc in range(2):
                        dst = KVN[:, mc * T + tg * TG: mc * T + (tg + 1) * TG]
                        src = kvs[:, mc * 512:(mc + 1) * 512]
                        P.op("v", lambda e, s=src: e.tensor_tensor(s, s, self.PSB[0][:, :],
                                                                   op=OP.mult))
                        P.op("v", lambda e, d=dst, s=src, mc=mc:
                             e.tensor_scalar(d, s, gkv_sb[:, mc:mc + 1], None, op0=OP.mult),
                             incs=(("epi", 1),))
                    P.wait("t", "epi", P.val("epi"))
                P.join()

            wq, wq_t, wq_wv = self.load_w_bf16(sc, W_q, D, 512, name="wq")
            P.wait("t", "wg", wq_wv)

            for hg in range(NHG):
                with contextlib.ExitStack() as s2:
                    sb2 = lambda n, shp, dt=F32: s2.enter_context(nc.sbuf_tensor(self.nm(n), list(shp), dt))
                    KC = sb2("kc", [128, NCH * T], BF16)
                    KRF = sb2("krf", [128, NCH * T], BF16)
                    VT = sb2("vt", [128, NTT * GH * 65], BF16)
                    QCF = sb2("qcf", [128, NCH * TG], BF16)
                    QRF = sb2("qrf", [128, NCH * TG], BF16)
                    SQB = self.scr_bf16(0, 2 * NCH * TG)
                    ARAW = self.scr_f32(2048, 512)
                    ROPS = self.scr_f32(2560, GH * 32)
                    ROPS2 = self.scr_f32(2816, GH * 32)
                    ROPB = self.scr_bf16(3072, GH * 32)
                    AOUT = self.scr_bf16(3328, NCH * TG)
                    KMAX = sb2("kmax", [128, 4])
                    NEGMB = sb2("negmb", [128, 1], BF16)
                    NEGM = sb2("negm", [128, GH])
                    NROW = sb2("nrow", [1, GH], BF16)
                    RECR = sb2("recr", [1, 512])
                    ST = [s2.enter_context(nc.psum_tensor(self.nm("st"), [128, 1024], F32))
                          for _ in range(2)]
                    PVP = s2.enter_context(nc.psum_tensor(self.nm("pv"), [128, 512], F32))
                    PTR = [s2.enter_context(nc.psum_tensor(self.nm("ptr"), [128, 128], BF16))
                           for _ in range(1)]
                    PT = [sb2("pt", [128, 1024], BF16) for _ in range(3)]

                    ph_a = contextlib.ExitStack()
                    wkd, wkd_t, _ = self.load_w_bf16(ph_a, W_kdec, 256, GH * 64, col0=hg * GH * 64, name="wkd")
                    wvd, wvd_t, _ = self.load_w_bf16(ph_a, W_vdec, 256, GH * 64, col0=hg * GH * 64, name="wvd")
                    wkr, wkr_t, wv_all = self.load_w_bf16(ph_a, W_krope, D, GH * 32, col0=hg * GH * 32, name="wkr")
                    P.wait("t", "wg", wv_all)

                    stpool = [ST[0][:, 0:512], ST[0][:, 512:1024],
                              ST[1][:, 0:512], ST[1][:, 512:1024]]
                    self.gemm_fm_to(nkc=2, nmc=NCH,
                                    xin_fn=lambda kc, tg: KVN[:, kc * T + tg * TG: kc * T + (tg + 1) * TG],
                                    wtile_fn=lambda kc, mc: wkd_t(kc, mc * 128, 128),
                                    dst_fn=lambda mc, tg: KC[:, mc * T + tg * TG: mc * T + (tg + 1) * TG],
                                    pool=stpool)

                    P.op("v", lambda e: e.memset(VT[:], 1.0), incs=(("dve", 1),))
                    P.wait("s", "dve", P.val("dve"))
                    for t2 in range(0, NTT, 4):
                        if t2 > 0:
                            P.wait("t", "act", P.val("act"))
                        bks = [stpool[(t2 + k) % 4] for k in range(4)]
                        def vmm(e, t2=t2, bks=bks):
                            inst = None
                            for k in range(4):
                                tt = t2 + k
                                for kc in range(2):
                                    inst = e.matmul(bks[k][:, 0:GH * 64],
                                                    KVN[:, kc * T + tt * 128: kc * T + (tt + 1) * 128],
                                                    wvd_t(kc), start=(kc == 0), stop=(kc == 1))
                            return inst
                        P.op("t", vmm, incs=(("pe", 1),))
                        P.wait("s", "pe", P.val("pe"))
                        def vevac(e, t2=t2, bks=bks):
                            inst = None
                            for k in range(4):
                                tt = t2 + k
                                vdst = VT[:, tt * GH * 65:(tt + 1) * GH * 65].rearrange(
                                    "p (h f) -> p h f", f=65)[:, :, 0:64]
                                inst = e.copy(vdst, bks[k][:, 0:GH * 64]
                                              .rearrange("p (h f) -> p h f", f=64))
                            return inst
                        P.op("s", vevac, incs=(("act", 1),))
                    P.join(("t", "v", "s"))

                    P.op("v", lambda e: e.memset(KRF[:], 0.0), incs=(("dve", 1),))
                    P.wait("t", "dve", P.val("dve"))
                    for tt in range(NTT):
                        sl = tt % 2
                        self._rope_cw = T
                        def krmm(e, tt=tt, sl=sl):
                            inst = None
                            for kc in range(NDC):
                                inst = e.matmul(self.PSB[sl][:, 0:GH * 32],
                                                self.XN[:, kc * T + tt * 128: kc * T + (tt + 1) * 128],
                                                wkr_t(kc), start=(kc == 0), stop=(kc == NDC - 1))
                            return inst
                        self.rope_tm_block(krmm, tt, sl, KRF, ROPS, ROPS2, ROPB, COS0, SIN0,
                                           PTR, nheads=GH)
                    P.join()
                    ph_a.close()
                    wqd, wqd_t, _ = self.load_w_bf16(s2, W_qdec, 512, GH * 64, col0=hg * GH * 64, name="wqd")
                    wqr, wqr_t, _ = self.load_w_bf16(s2, W_qrope, 512, GH * 32, col0=hg * GH * 32, name="wqr")
                    wob, wob_t, wv_b = self.load_w_bf16(s2, W_o, NCH * 128, D,
                                                        row0=hg * NCH * 128, name="wo")
                    P.wait("t", "wg", wv_b)

                    for qg in range(NTG):
                        P.wait("t", "act", P.val("act"))
                        self.gemm_fm_to(nkc=NDC, nmc=4,
                                        xin_fn=lambda kc, tg, qg=qg: self.xn_ap(kc, qg),
                                        wtile_fn=lambda kc, mc: wq_t(kc, mc * 128, 128),
                                        dst_fn=lambda mc, tg: QCQG[:, mc * TG:(mc + 1) * TG],
                                        tgs=(0,), pool=stpool)
                        self.gemm_fm_to(nkc=4, nmc=NCH,
                                        xin_fn=lambda kc, tg: QCQG[:, kc * TG:(kc + 1) * TG],
                                        wtile_fn=lambda kc, mc: wqd_t(kc, mc * 128, 128),
                                        dst_fn=lambda mc, tg: QCF[:, mc * TG:(mc + 1) * TG],
                                        tgs=(0,), pool=stpool)
                        P.op("v", lambda e: e.memset(QRF[:], 0.0), incs=(("dve", 1),))
                        P.wait("t", "dve", P.val("dve"))
                        for lt in range(4):
                            tt = qg * 4 + lt
                            sl = lt % 2
                            self._rope_cw = TG
                            def qrmm(e, lt=lt, sl=sl):
                                inst = None
                                for kc in range(4):
                                    inst = e.matmul(self.PSB[sl][:, 0:GH * 32],
                                                    QCQG[:, kc * TG + lt * 128: kc * TG + (lt + 1) * 128],
                                                    wqr_t(kc), start=(kc == 0), stop=(kc == 3))
                                return inst
                            self.rope_tm_block(qrmm, tt, sl, QRF, ROPS, ROPS2, ROPB, COS0, SIN0,
                                               PTR, nheads=GH, dst_tt=lt)
                        P.join(("t", "v", "s"))

                        self.l0_negm(qg, GH, NCH, KC, KRF, QCF, QRF, SQB, IND4, IND4R, KMAX,
                                     NEGM, NROW, NEGMB, ST, PTR, scale, first=(qg == 0))

                        if hg == 0 and qg == 0:
                            self.tap("negm", lambda: NEGM[:, :], (128, 4))
                            self.tap("kmax", lambda: KMAX[:, :], (128, 4))
                            self.tap("kc0", lambda: KC[:, 0:512], (128, 512))
                            self.tap("krf0", lambda: KRF[:, 0:512], (128, 512))
                            self.tap("qcf0", lambda: QCF[:, 0:512], (128, 512))
                            self.tap("qrf0", lambda: QRF[:, 0:512], (128, 512))
                            self.tap("vt0", lambda: VT[:, 0:520], (128, 520))
                            self.tap("kvn0", lambda: KVN[:, 0:512], (128, 512))

                        def kc_fn(h, kt):
                            ch, ro = h // 2, 64 * (h % 2)
                            return KC[ro:ro + 64, ch * T + kt * 128: ch * T + (kt + 1) * 128]

                        def qc_fn(h, qg_):
                            ch, ro = h // 2, 64 * (h % 2)
                            return QCF[ro:ro + 64, ch * TG:(ch + 1) * TG]

                        def kr_fn(h, kt):
                            ch, ro = h // 2, 64 * (h % 2)
                            return KRF[ro:ro + 32, ch * T + kt * 128: ch * T + (kt + 1) * 128]

                        def qr_fn(h, qg_):
                            ch, ro = h // 2, 64 * (h % 2)
                            return QRF[ro:ro + 32, ch * TG:(ch + 1) * TG]

                        def v_fn(h, kt):
                            return VT[:, kt * GH * 65 + h * 65: kt * GH * 65 + (h + 1) * 65]

                        def out_fn(h, qg_):
                            ch, ro = h // 2, 64 * (h % 2)
                            return AOUT[ro:ro + 64, ch * TG:(ch + 1) * TG]

                        self._tap_araw = DEBUG_TAPS and hg == 0 and qg == 0
                        self.attn_core(heads=range(GH),
                                       parts=[(kc_fn, qc_fn, 64), (kr_fn, qr_fn, 32)],
                                       v_slot_fn=v_fn,
                                       negm_ap_fn=lambda h: NEGM[:, h:h + 1],
                                       scale=scale, causal=True,
                                       out_ap_fn=out_fn, qgs=(qg,),
                                       ST=ST, PV=PVP, PT=PT, araw=ARAW, recr=RECR)

                        epi0 = P.val("epi")
                        for mc in range(NDC):
                            sl = mc % 2
                            if mc >= 2:
                                P.wait("t", "epi", epi0 + mc - 1)
                            def wmm(e, mc=mc, sl=sl):
                                inst = None
                                for kc in range(NCH):
                                    inst = e.matmul(self.PSB[sl][:, :], wob_t(kc, mc * 128, 128),
                                                    AOUT[:, kc * TG:(kc + 1) * TG],
                                                    start=(kc == 0), stop=(kc == NCH - 1))
                                return inst
                            P.op("t", wmm, incs=(("pe", 1),))
                            P.wait("v", "pe", P.val("pe"))
                            xs = self.xr_ap(mc, qg)
                            P.op("v", lambda e, xs=xs, sl=sl:
                                 e.tensor_add(xs, xs, self.PSB[sl][:, :]),
                                 incs=(("epi", 1),))
                        P.wait("t", "epi", epi0 + NDC)
                    P.join()
            P.join()

    def rope_tm_block(self, mm_fn, tt, sl, DSTF, ROPS, ROPS2, ROPB, COS, SIN, PTR,
                      nheads=4, dst_tt=None):
        P = self.P
        nw = nheads * 32
        dtt = tt if dst_tt is None else dst_tt
        P.wait("t", "dve", P.val("dve"))
        P.op("t", mm_fn, incs=(("pe", 1),))
        P.wait("s", "pe", P.val("pe"))
        P.op("s", lambda e, sl=sl, nw=nw: e.copy(ROPS[:, 0:nw], self.PSB[sl][:, 0:nw]),
             incs=(("act", 1),))
        P.wait("v", "act", P.val("act"))
        self.rope_apply_l0(ROPS[:, 0:nw], ROPB[:, 0:nw],
                           COS[:, tt * 16:(tt + 1) * 16], SIN[:, tt * 16:(tt + 1) * 16],
                           nheads, ROPS2)
        cw = self._rope_cw
        for bp in range(nheads // 2):
            psl = 0
            P.wait("t", "dve", P.val("dve"))
            P.op("t", lambda e, bp=bp, psl=psl:
                 e.transpose(PTR[0][0:64, :], ROPB[:, bp * 64:(bp + 1) * 64], self.IDENT[:]),
                 incs=(("pe", 1),))
            P.wait("v", "pe", P.val("pe"))
            P.op("v", lambda e, bp=bp, psl=psl, cw=cw, dtt=dtt:
                 e.tensor_copy(DSTF[0:32, bp * cw + dtt * 128: bp * cw + (dtt + 1) * 128],
                               PTR[0][0:32, :]))
            P.op("v", lambda e, bp=bp, psl=psl, cw=cw, dtt=dtt:
                 e.tensor_copy(DSTF[64:96, bp * cw + dtt * 128: bp * cw + (dtt + 1) * 128],
                               PTR[0][32:64, :]), incs=(("dve", 1),))

    def l0_negm(self, qg, GH, NCH, KC, KRF, QCF, QRF, SQB, IND, INDR, KMAX, NEGM,
                NROW, NEGMB, ST, PTR, scale, first):
        P = self.P

        def sq_and_norms(src_c, src_r, width, nt):
            sqc = SQB[:, 0:NCH * nt]
            sqr = SQB[:, NCH * nt:2 * NCH * nt]
            P.op("s", lambda e: e.activation(sqc.rearrange("p (c t) -> p c t", c=NCH),
                                             src_c, AF.Square))
            P.op("s", lambda e: e.activation(sqr.rearrange("p (c t) -> p c t", c=NCH),
                                             src_r, AF.Square), incs=(("act", 1),))
            P.wait("t", "act", P.val("act"))
            def nmm(e):
                for c in range(NCH):
                    e.matmul(ST[1][0:GH, 0:nt], IND[:, c * GH:(c + 1) * GH],
                             sqc[:, c * nt:(c + 1) * nt], start=(c == 0), stop=False)
                inst = None
                for c in range(NCH):
                    inst = e.matmul(ST[1][0:GH, 0:nt], INDR[:, c * GH:(c + 1) * GH],
                                    sqr[:, c * nt:(c + 1) * nt],
                                    start=False, stop=(c == NCH - 1))
                return inst
            P.op("t", nmm, incs=(("pe", 1),))
            P.wait("v", "pe", P.val("pe"))

        if first:
            P.op("v", lambda e: e.memset(KMAX[0:GH, 0:1], 0.0), incs=(("dve", 1),))
            P.wait("s", "dve", P.val("dve"))
            for tg in range(NTG):
                kc3 = KC[:].rearrange("p (c t) -> p c t", c=NCH)[:, :, tg * TG:(tg + 1) * TG]
                kr3 = KRF[:].rearrange("p (c t) -> p c t", c=NCH)[:, :, tg * TG:(tg + 1) * TG]
                sq_and_norms(kc3, kr3, T, TG)
                P.op("v", lambda e: e.reduce_max(KMAX[0:GH, 1:2], ST[1][0:GH, 0:TG],
                                                 axis=mybir.AxisListType.X), incs=(("dve", 1),))
                P.wait("v", "dve", P.val("dve"))
                P.op("v", lambda e: e.tensor_max(KMAX[0:GH, 0:1], KMAX[0:GH, 0:1],
                                                 KMAX[0:GH, 1:2]), incs=(("dve", 1),))
                P.wait("t", "dve", P.val("dve"))
                P.wait("s", "dve", P.val("dve"))
        qc3 = QCF[:].rearrange("p (c t) -> p c t", c=NCH)
        qr3 = QRF[:].rearrange("p (c t) -> p c t", c=NCH)
        P.wait("s", "dve", P.val("dve"))
        sq_and_norms(qc3, qr3, TG, TG)
        if DEBUG_TAPS and first and not getattr(self, "_negm_tapped", False):
            self._negm_tapped = True
            P.wait("s", "pe", P.val("pe"))
            P.op("s", lambda e: e.copy(self.scr_f32(2048, 512, parts=8)[0:8, :],
                                       ST[1][0:8, 0:512]), incs=(("act", 1),))
            self.tap("qnormps", lambda: self.scr_f32(2048, 512, parts=8)[0:8, :], (8, 512))
            self.tap("sqb", lambda: SQB[:, :], (128, 2048))
        P.op("v", lambda e: e.reduce_max(KMAX[0:GH, 2:3], ST[1][0:GH, 0:TG],
                                         axis=mybir.AxisListType.X), incs=(("dve", 1),))
        P.wait("v", "dve", P.val("dve"))
        P.op("v", lambda e: e.tensor_mul(KMAX[0:GH, 1:2], KMAX[0:GH, 2:3], KMAX[0:GH, 0:1]),
             incs=(("dve", 1),))
        P.wait("s", "dve", P.val("dve"))
        P.op("s", lambda e: e.activation(KMAX[0:GH, 2:3], KMAX[0:GH, 1:2], AF.Ln))
        P.op("s", lambda e: e.activation(KMAX[0:GH, 2:3], KMAX[0:GH, 2:3], AF.Exp, scale=0.5),
             incs=(("act", 1),))
        P.wait("v", "act", P.val("act"))
        P.op("v", lambda e, sc=-scale: e.tensor_scalar_mul(NEGMB[0:GH, 0:1],
                                                           KMAX[0:GH, 2:3], sc),
             incs=(("dve", 1),))
        P.wait("t", "dve", P.val("dve"))
        P.op("t", lambda e: e.transpose(PTR[0][0:1, 0:GH],
                                        NEGMB[0:GH, 0:1], self.IDENT[0:GH, 0:GH]),
             incs=(("pe", 1),))
        P.wait("s", "pe", P.val("pe"))
        P.op("s", lambda e: e.copy(NROW[0:1, :], PTR[0][0:1, 0:GH]), incs=(("act", 1),))
        P.wait("t", "act", P.val("act"))
        P.op("t", lambda e: e.matmul(ST[1][:, 0:GH], self.ONES1B[:], NROW[0:1, :],
                                     start=True, stop=True), incs=(("pe", 1),))
        P.wait("s", "pe", P.val("pe"))
        P.op("s", lambda e: e.copy(NEGM[:, :], ST[1][:, 0:GH]), incs=(("act", 1),))
        P.wait("t", "act", P.val("act"))
        P.wait("v", "act", P.val("act"))

    def negm_content(self, GH, NCH, k3_fn, q3, IND, KMAX, NEGM, NROW, NEGMB, PSN,
                     PTR, SQB, scale, first):
        P = self.P

        def norms(src3, nt):
            sqc = SQB[:, 0:NCH * nt]
            P.op("s", lambda e: e.activation(sqc.rearrange("p (c t) -> p c t", c=NCH),
                                             src3, AF.Square), incs=(("act", 1),))
            P.wait("t", "act", P.val("act"))
            def nmm(e):
                inst = None
                for c in range(NCH):
                    inst = e.matmul(PSN[0:GH, 0:nt], IND[:, c * GH:(c + 1) * GH],
                                    sqc[:, c * nt:(c + 1) * nt],
                                    start=(c == 0), stop=(c == NCH - 1))
                return inst
            P.op("t", nmm, incs=(("pe", 1),))
            P.wait("v", "pe", P.val("pe"))

        if first:
            P.op("v", lambda e: e.memset(KMAX[0:GH, 0:1], 0.0), incs=(("dve", 1),))
            P.wait("s", "dve", P.val("dve"))
            for tg in range(NTG):
                norms(k3_fn(tg), TG)
                P.op("v", lambda e: e.reduce_max(KMAX[0:GH, 1:2], PSN[0:GH, 0:TG],
                                                 axis=mybir.AxisListType.X), incs=(("dve", 1),))
                P.wait("v", "dve", P.val("dve"))
                P.op("v", lambda e: e.tensor_max(KMAX[0:GH, 0:1], KMAX[0:GH, 0:1],
                                                 KMAX[0:GH, 1:2]), incs=(("dve", 1),))
                P.wait("t", "dve", P.val("dve"))
                P.wait("s", "dve", P.val("dve"))
        P.wait("s", "dve", P.val("dve"))
        norms(q3, TG)
        P.op("v", lambda e: e.reduce_max(KMAX[0:GH, 2:3], PSN[0:GH, 0:TG],
                                         axis=mybir.AxisListType.X), incs=(("dve", 1),))
        P.wait("v", "dve", P.val("dve"))
        P.op("v", lambda e: e.tensor_mul(KMAX[0:GH, 1:2], KMAX[0:GH, 2:3], KMAX[0:GH, 0:1]),
             incs=(("dve", 1),))
        P.wait("s", "dve", P.val("dve"))
        P.op("s", lambda e: e.activation(KMAX[0:GH, 2:3], KMAX[0:GH, 1:2], AF.Ln))
        P.op("s", lambda e: e.activation(KMAX[0:GH, 2:3], KMAX[0:GH, 2:3], AF.Exp, scale=0.5),
             incs=(("act", 1),))
        P.wait("v", "act", P.val("act"))
        P.op("v", lambda e, sc=-scale: e.tensor_scalar_mul(NEGMB[0:GH, 0:1],
                                                           KMAX[0:GH, 2:3], sc),
             incs=(("dve", 1),))
        if PTR is None:
            P.wait("v", "dve", P.val("dve"))
            P.op("v", lambda e: e.transpose(self.TRS[0:32, 0:32], NEGMB[0:32, 0:32]),
                 incs=(("dve", 1),))
            P.wait("v", "dve", P.val("dve"))
            P.op("v", lambda e: e.tensor_copy(NROW[0:1, 0:GH], self.TRS[0:1, 0:GH]),
                 incs=(("dve", 1),))
            P.wait("t", "dve", P.val("dve"))
        else:
            P.wait("t", "dve", P.val("dve"))
            P.op("t", lambda e: e.transpose(PTR[0][0:1, 0:GH],
                                            NEGMB[0:GH, 0:1], self.IDENT[0:GH, 0:GH]),
                 incs=(("pe", 1),))
            P.wait("s", "pe", P.val("pe"))
            P.op("s", lambda e: e.copy(NROW[0:1, :], PTR[0][0:1, 0:GH]), incs=(("act", 1),))
            P.wait("t", "act", P.val("act"))
        P.op("t", lambda e: e.matmul(PSN[:, 0:GH], self.ONES1B[:], NROW[0:1, 0:GH],
                                     start=True, stop=True), incs=(("pe", 1),))
        P.wait("s", "pe", P.val("pe"))
        P.op("s", lambda e: e.copy(NEGM[:, :], PSN[:, 0:GH]), incs=(("act", 1),))
        P.wait("t", "act", P.val("act"))
        P.wait("v", "act", P.val("act"))

    def rope_tm_l1(self, RS, ROPB, cos_ap, sin_ap, npairs=512):
        P = self.P
        x0 = RS.rearrange("p (a two) -> p a two", two=2)[:, :, 0:1]
        x1 = RS.rearrange("p (a two) -> p a two", two=2)[:, :, 1:2]
        o0 = ROPB.rearrange("p (a two) -> p a two", two=2)[:, :, 0:1]
        o1 = ROPB.rearrange("p (a two) -> p a two", two=2)[:, :, 1:2]
        c3 = cos_ap.rearrange("p f -> p f ()")
        s3 = sin_ap.rearrange("p f -> p f ()")
        for h0 in range(0, npairs, 256):
            w = min(256, npairs - h0)
            m1 = self.scr_f32(2048, w).rearrange("p f -> p f ()")
            m2 = self.scr_f32(2304, w).rearrange("p f -> p f ()")
            m3 = self.scr_f32(2560, w).rearrange("p f -> p f ()")
            m4 = self.scr_f32(2816, w).rearrange("p f -> p f ()")
            sl = slice(h0, h0 + w)
            P.op("v", lambda e, sl=sl: e.tensor_tensor(m1, x0[:, sl], c3[:, sl], op=OP.mult))
            P.op("v", lambda e, sl=sl: e.tensor_tensor(m2, x1[:, sl], s3[:, sl], op=OP.mult))
            P.op("v", lambda e, sl=sl: e.tensor_tensor(m3, x1[:, sl], c3[:, sl], op=OP.mult))
            P.op("v", lambda e, sl=sl: e.tensor_tensor(m4, x0[:, sl], s3[:, sl], op=OP.mult))
            P.op("v", lambda e, sl=sl: e.tensor_tensor(o0[:, sl], m1, m2, op=OP.subtract))
            P.op("v", lambda e, sl=sl: e.tensor_tensor(o1[:, sl], m3, m4, op=OP.add),
                 incs=(("dve", 1),))


    def layer1(self):
        self.layernorm(self.XN)
        self.l1_ffn()
        self.l1_attn()

    def layer2(self):
        self.layernorm(self.XN)
        self.l2_attn()
        self.l2_ffn()

    def l2_attn(self):
        P, nc = self.P, self.nc
        inproj = self.hbm_in("L2_attn_in_proj_t", (D, 3072))
        outproj = self.hbm_in("L2_attn_out_proj_t", (D, D))
        with contextlib.ExitStack() as sc:
            AOUTF = sc.enter_context(nc.sbuf_tensor(self.nm("aoutf"), [128, NDC * T], BF16))
            self._attn_in = self.XN
            self.mha16(wq_src=(inproj, 0), wk_src=(inproj, 1024), wv_src=(inproj, 2048),
                       causal=True, scale=0.125, AOUTF=AOUTF)
            with contextlib.ExitStack() as s2:
                wo, wo_t, wv = self.load_w_bf16(s2, outproj, D, D, name="wo")
                P.wait("t", "wg", wv)
                epi0 = P.val("epi")
                idx = 0
                for mc in range(NDC):
                    for qg in range(NTG):
                        sl = idx % 2
                        if idx >= 2:
                            P.wait("t", "epi", epi0 + idx - 1)
                        def wmm(e, mc=mc, qg=qg, sl=sl):
                            inst = None
                            for kc in range(NDC):
                                inst = e.matmul(self.PSB[sl][:, :], wo_t(kc, mc * 128, 128),
                                                AOUTF[:, kc * T + qg * TG: kc * T + (qg + 1) * TG],
                                                start=(kc == 0), stop=(kc == NDC - 1))
                            return inst
                        P.op("t", wmm, incs=(("pe", 1),))
                        P.wait("v", "pe", P.val("pe"))
                        xs = self.xr_ap(mc, qg)
                        P.op("v", lambda e, xs=xs, sl=sl:
                             e.tensor_add(xs, xs, self.PSB[sl][:, :]), incs=(("epi", 1),))
                        idx += 1
                P.wait("t", "epi", epi0 + idx)
            P.join()

    def l2_ffn(self):
        P, nc = self.P, self.nc
        E, FF = 8, 2048
        Wq = self.hbm_in("L2_ffn_Wq", (D, D))
        Wk = self.hbm_in("L2_ffn_Wk", (D, D))
        Wv = self.hbm_in("L2_ffn_Wv", (D, D))
        Wo = self.hbm_in("L2_ffn_Wo", (D, D))
        gW = self.hbm_in("L2_ffn_gW", (D, E))
        W1 = self.hbm_in("L2_ffn_W1", (E, D, FF))
        W2 = self.hbm_in("L2_ffn_W2", (E, FF, D))
        b2_h = self.hbm_in("L2_ffn_b2", (E, D))
        esel_h = self.hbm_in("c_esel8", (8, 8 * 128))
        with contextlib.ExitStack() as sc:
            sbl = lambda n_, shp, dt=F32: sc.enter_context(nc.sbuf_tensor(self.nm(n_), list(shp), dt))
            ZB = sbl("zb", [128, NDC * TG], BF16)
            MR2 = sbl("mr2", [65, 512])
            bq, _ = self.load_col_f32(sc, "L2_ffn_bq", D, "bq")
            bk, _ = self.load_col_f32(sc, "L2_ffn_bk", D, "bk")
            bo, _ = self.load_col_f32(sc, "L2_ffn_bo", D, "bo")
            g1, _ = self.load_col_f32(sc, "L2_ffn_ln1_g", D, "g1")
            b1l, _ = self.load_col_f32(sc, "L2_ffn_ln1_b", D, "b1l")
            g2, _ = self.load_col_f32(sc, "L2_ffn_ln2_g", D, "g2")
            b2l, _ = self.load_col_f32(sc, "L2_ffn_ln2_b", D, "b2l")
            bv_r, _ = self.load_row_bf16(sc, self.hbm_in("L2_ffn_bv", (D,)), D, "bvr")

            with contextlib.ExitStack() as sa:
                AOUTF = sa.enter_context(nc.sbuf_tensor(self.nm("aoutf"), [128, NDC * T], BF16))
                self._attn_in = self.XN
                self.mha16(wq_src=(Wq, 0), wk_src=(Wk, 0), wv_src=(Wv, 0),
                           bq_col=bq, bk_col=bk, bv_row=bv_r,
                           causal=False, scale=0.125, AOUTF=AOUTF)
                X1 = self.XN
                with contextlib.ExitStack() as s2:
                    PSB2 = [s2.enter_context(nc.psum_tensor(self.nm("psb2"), [128, 512], F32))
                            for _ in range(2)]
                    self.PSB2 = PSB2
                    WOB = [s2.enter_context(nc.sbuf_tensor(self.nm("wob"), [128, NDC * 128], BF16))
                           for _ in range(2)]
                    wob_pe = {}
                    for tg in range(NTG):
                        zadd = {}
                        for dc in range(NDC):
                            sl = dc % 2
                            key = tg * NDC + dc
                            if key - 2 in wob_pe:
                                P.wait("g", "pe", wob_pe[key - 2])
                            dma(P, "g", lambda b=WOB[sl]: b[:].rearrange("p (k m) -> p k m", m=128),
                                lambda dc=dc: Wo[:, dc * 128:(dc + 1) * 128]
                                .rearrange("(k p) m -> p k m", p=128), "wg")
                            wv = P.val("wg")
                            P.wait("t", "wg", wv)
                            if sl in zadd:
                                P.wait("t", "dve", zadd[sl])
                            def wmm(e, dc=dc, tg=tg, sl=sl):
                                inst = None
                                for kc in range(NDC):
                                    inst = e.matmul(self.PSB2[sl][:, :],
                                                    WOB[sl][:, kc * 128:(kc + 1) * 128],
                                                    AOUTF[:, kc * T + tg * TG: kc * T + (tg + 1) * TG],
                                                    start=(kc == 0), stop=(kc == NDC - 1))
                                return inst
                            P.op("t", wmm, incs=(("pe", 1),))
                            wob_pe[key] = P.val("pe")
                            P.wait("v", "pe", P.val("pe"))
                            z = self.scr_f32(dc * TG, TG)
                            xnsl = self.XN[:, dc * T + tg * TG: dc * T + (tg + 1) * TG]
                            P.op("v", lambda e, z=z, sl=sl, xnsl=xnsl:
                                 e.tensor_tensor(z, xnsl, self.PSB2[sl][:, :], op=OP.add))
                            P.op("v", lambda e, z=z, dc=dc:
                                 e.tensor_scalar_add(z, z, bo[:, dc:dc + 1]),
                                 incs=(("dve", 1),))
                            zadd[sl] = P.val("dve")
                        self.inner_ln(tg, MR2, ZB, g1, b1l,
                                      out_fn=lambda dc, tg2: X1[:, dc * T + tg2 * TG:
                                                                dc * T + (tg2 + 1) * TG])
                    P.join()

            with contextlib.ExitStack() as sb_:
                sbb = lambda n_, shp, dt=F32: sb_.enter_context(nc.sbuf_tensor(self.nm(n_), list(shp), dt))
                SC_ = sbb("sc", [128, NTT * E])
                V12 = sbb("v12", [128, 4])
                IND1 = sbb("ind1", [128, NTT * E])
                WB = sbb("wb", [128, NTT * E], BF16)
                WSELF = sbb("wself", [8, T], BF16)
                ESEL = sbb("esel", [8, 8 * 128], BF16)
                YW = sbb("yw", [128, 32 + NTT * E])
                b2sb = sb_.enter_context(nc.sbuf_tensor(self.nm("b2sb"), [8, D], BF16))
                PTB = sb_.enter_context(nc.psum_tensor(self.nm("ptb"), [128, 128], BF16))
                dma(P, "g", lambda: ESEL[:, :], lambda: esel_h[:, :], "wg")
                dma(P, "g", lambda: b2sb[:, :], lambda: b2_h[:, :], "wg")
                wgt, wgt_t, _ = self.load_w_bf16(sb_, gW, D, E, name="wg")
                bgr, wv = self.load_row_bf16(sb_, self.hbm_in("L2_ffn_gb", (E,)), E, "bgr")
                P.wait("t", "wg", wv)
                for tt in range(NTT):
                    sl = tt % 2
                    if tt >= 2:
                        P.wait("t", "dve", P.val("dve"))
                    def gmm(e, tt=tt, sl=sl):
                        for kc in range(NDC):
                            e.matmul(self.PSB[sl][:, 0:E],
                                     X1[:, kc * T + tt * 128: kc * T + (tt + 1) * 128],
                                     wgt_t(kc), start=(kc == 0), stop=False)
                        return e.matmul(self.PSB[sl][:, 0:E], self.ONES1B[:],
                                        bgr[0:1, :], start=False, stop=True)
                    P.op("t", gmm, incs=(("pe", 1),))
                    P.wait("v", "pe", P.val("pe"))
                    P.op("v", lambda e, sl=sl: e.reduce_max(V12[:, 0:1], self.PSB[sl][:, 0:E],
                                                            axis=mybir.AxisListType.X),
                         incs=(("dve", 1),))
                    P.wait("v", "dve", P.val("dve"))
                    P.op("v", lambda e: e.tensor_scalar_mul(V12[:, 1:2], V12[:, 0:1], -1.0),
                         incs=(("dve", 1),))
                    P.wait("s", "dve", P.val("dve"))
                    P.op("s", lambda e, sl=sl, tt=tt:
                         e.activation(SC_[:, tt * E:(tt + 1) * E], self.PSB[sl][:, 0:E], AF.Exp,
                                      bias=V12[:, 1:2], accum_out=V12[:, 2:3]),
                         incs=(("act", 1),))
                    P.wait("v", "act", P.val("act"))
                    P.op("v", lambda e: e.reciprocal(V12[:, 3:4], V12[:, 2:3]), incs=(("dve", 1),))
                    P.wait("v", "dve", P.val("dve"))
                    P.op("v", lambda e, tt=tt:
                         e.tensor_scalar(SC_[:, tt * E:(tt + 1) * E], SC_[:, tt * E:(tt + 1) * E],
                                         V12[:, 3:4], None, op0=OP.mult), incs=(("dve", 1),))
                sc3 = SC_[:].rearrange("p (t e) -> p t e", e=E)
                P.wait("v", "dve", P.val("dve"))
                P.op("v", lambda e: e.reduce_max(YW[:, 0:16], sc3, axis=mybir.AxisListType.X),
                     incs=(("dve", 1),))
                P.wait("v", "dve", P.val("dve"))
                v1b = YW[:, 0:16].rearrange("p t -> p t ()").broadcast_to([128, 16, E])
                P.op("v", lambda e: e.tensor_tensor(IND1[:].rearrange("p (t e) -> p t e", e=E),
                                                    sc3, v1b, op=OP.is_ge))
                P.op("v", lambda e: e.scalar_tensor_tensor(YW[:, 32:160], IND1[:], -1e9, SC_[:],
                                                           op0=OP.mult, op1=OP.add),
                     incs=(("dve", 1),))
                P.wait("v", "dve", P.val("dve"))
                P.op("v", lambda e: e.reduce_max(YW[:, 16:32],
                                                 YW[:, 32:160].rearrange("p (t e) -> p t e", e=E),
                                                 axis=mybir.AxisListType.X), incs=(("dve", 1),))
                P.wait("v", "dve", P.val("dve"))
                v2b = YW[:, 16:32].rearrange("p t -> p t ()").broadcast_to([128, 16, E])
                P.op("v", lambda e: e.tensor_tensor(YW[:, 32:160].rearrange("p (t e) -> p t e", e=E),
                                                    sc3, v2b, op=OP.is_ge))
                P.op("v", lambda e: e.tensor_sub(YW[:, 32:160], YW[:, 32:160], IND1[:]))
                P.op("v", lambda e: e.tensor_tensor(YW[:, 32:160].rearrange("p (t e) -> p t e", e=E),
                                                    YW[:, 32:160].rearrange("p (t e) -> p t e", e=E),
                                                    v2b, op=OP.mult))
                P.op("v", lambda e: e.tensor_tensor(IND1[:].rearrange("p (t e) -> p t e", e=E),
                                                    IND1[:].rearrange("p (t e) -> p t e", e=E),
                                                    v1b, op=OP.mult))
                P.op("v", lambda e: e.tensor_add(WB[:], IND1[:], YW[:, 32:160]),
                     incs=(("dve", 1),))
                P.wait("t", "dve", P.val("dve"))
                for tt in range(NTT):
                    P.wait("t", "act", P.val("act"))
                    P.op("t", lambda e, tt=tt:
                         e.transpose(PTB[0:E, 0:128], WB[:, tt * E:(tt + 1) * E],
                                     self.IDENT[:]), incs=(("pe", 1),))
                    P.wait("s", "pe", P.val("pe"))
                    P.op("s", lambda e, tt=tt:
                         e.copy(WSELF[0:E, tt * 128:(tt + 1) * 128], PTB[0:E, 0:128]),
                         incs=(("act", 1),))
                P.join()

                PH = [sb_.enter_context(nc.psum_tensor(self.nm("ph"), [128, 512], F32))
                      for _ in range(2)]
                PD = [sb_.enter_context(nc.psum_tensor(self.nm("pd"), [128, 512], F32))
                      for _ in range(2)]
                W1B = [sb_.enter_context(nc.sbuf_tensor(self.nm("w1b"), [128, NDC * 256], BF16))
                       for _ in range(2)]
                W2B = [sb_.enter_context(nc.sbuf_tensor(self.nm("w2b"), [128, 16 * 128], BF16))
                       for _ in range(2)]
                HB = sb_.enter_context(nc.sbuf_tensor(self.nm("hb"), [128, 16 * TG], BF16))
                B1C, wvb = self.load_col_f32_shaped(sb_, "L2_ffn_b1_pf", (128, 16 * E), "b1c")
                P.wait("t", "wg", wvb)
                for tg in range(NTG):
                    for dc in range(NDC):
                        P.op("v", lambda e, dc=dc, tg=tg:
                             e.tensor_copy(self.scr_f32(dc * TG, TG),
                                           X1[:, dc * T + tg * TG: dc * T + (tg + 1) * TG]),
                             incs=((("dve", 1),) if dc == NDC - 1 else ()))
                    w1b_pe = {}
                    w2b_pe = {}
                    for ei in range(E):
                        gelu_done = {}
                        for fcg in range(8):
                            bsel = fcg % 2
                            key = ei * 8 + fcg
                            if key - 2 in w1b_pe:
                                P.wait("g", "pe", w1b_pe[key - 2])
                            dma(P, "g", lambda b=W1B[bsel]: b[:].rearrange("p (k m) -> p k m", m=256),
                                lambda ei=ei, fcg=fcg: W1[ei][:, fcg * 256:(fcg + 1) * 256]
                                .rearrange("(k p) m -> p k m", p=128), "wg")
                            wv1 = P.val("wg")
                            for lf in range(2):
                                fc = fcg * 2 + lf
                                sl = fc % 2
                                P.wait("t", "wg", wv1)
                                if sl in gelu_done:
                                    P.wait("t", "act", gelu_done[sl])
                                def hmm(e, fc=fc, lf=lf, sl=sl, tg=tg, bsel=bsel):
                                    inst = None
                                    for kc in range(NDC):
                                        inst = e.matmul(PH[sl][:, :],
                                                        W1B[bsel][:, kc * 256 + lf * 128: kc * 256 + (lf + 1) * 128],
                                                        X1[:, kc * T + tg * TG: kc * T + (tg + 1) * TG],
                                                        start=(kc == 0), stop=(kc == NDC - 1))
                                    return inst
                                P.op("t", hmm, incs=(("pe", 1),))
                                P.wait("s", "pe", P.val("pe"))
                                P.op("s", lambda e, fc=fc, sl=sl, ei=ei:
                                     e.activation(HB[:, fc * TG:(fc + 1) * TG], PH[sl][:, :],
                                                  AF.Relu,
                                                  bias=B1C[:, fc * E + ei: fc * E + ei + 1]),
                                     incs=(("act", 1),))
                                gelu_done[sl] = P.val("act")
                            w1b_pe[key] = P.val("pe")
                        P.wait("t", "dve", P.val("dve"))
                        P.op("t", lambda e, ei=ei, tg=tg:
                             e.matmul(PD[0][:, :], ESEL[:, ei * 128:(ei + 1) * 128],
                                      WSELF[0:E, tg * TG:(tg + 1) * TG],
                                      start=True, stop=True), incs=(("pe", 1),))
                        P.wait("v", "pe", P.val("pe"))
                        P.wait("v", "act", P.val("act"))
                        h3 = HB[:].rearrange("p (c t) -> p c t", c=16)
                        mb = PD[0][:, :].rearrange("p t -> p () t").broadcast_to([128, 16, TG])
                        P.op("v", lambda e, h3=h3, mb=mb:
                             e.tensor_tensor(h3, h3, mb, op=OP.mult), incs=(("dve", 1),))
                        hm_done = P.val("dve")
                        zadd = {}
                        for dc in range(NDC):
                            sl = dc % 2
                            bsel = dc % 2
                            key2 = ei * NDC + dc
                            if key2 - 2 in w2b_pe:
                                P.wait("g", "pe", w2b_pe[key2 - 2])
                            dma(P, "g", lambda b=W2B[bsel]: b[:].rearrange("p (k m) -> p k m", m=128),
                                lambda ei=ei, dc=dc: W2[ei][:, dc * 128:(dc + 1) * 128]
                                .rearrange("(k p) m -> p k m", p=128), "wg")
                            wv2 = P.val("wg")
                            P.wait("t", "wg", wv2)
                            if dc == 0:
                                P.wait("t", "dve", hm_done)
                            if sl in zadd:
                                P.wait("t", "dve", zadd[sl])
                            def dmm(e, dc=dc, sl=sl, tg=tg, ei=ei, bsel=bsel):
                                inst = None
                                for fc in range(16):
                                    inst = e.matmul(PD[sl][:, :], W2B[bsel][:, fc * 128:(fc + 1) * 128],
                                                    HB[:, fc * TG:(fc + 1) * TG],
                                                    start=(fc == 0),
                                                    stop=(fc == 15 and ei != 0))
                                if ei == 0:
                                    inst = e.matmul(PD[sl][:, :], b2sb[:, dc * 128:(dc + 1) * 128],
                                                    WSELF[0:E, tg * TG:(tg + 1) * TG],
                                                    start=False, stop=True)
                                return inst
                            P.op("t", dmm, incs=(("pe", 1),))
                            w2b_pe[key2] = P.val("pe")
                            P.wait("v", "pe", P.val("pe"))
                            z = self.scr_f32(dc * TG, TG)
                            P.op("v", lambda e, z=z, sl=sl:
                                 e.tensor_add(z, z, PD[sl][:, :]), incs=(("dve", 1),))
                            zadd[sl] = P.val("dve")
                    self.inner_ln(tg, MR2, ZB, g2, b2l, add_to_x=True)
                P.join()

    def inner_ln(self, tg, MR2, ZB, gcol, bcol, out_fn=None, add_to_x=False,
                 eps=LN_EPS):
        P, nc = self.P, self.nc
        P.wait("v", "dve", P.val("dve"))
        P.wait("v", "pe", P.val("pe"))
        P.op("v", lambda e: e.tensor_copy(ZB[:], self.scr_f32(0, NDC * TG)),
             incs=(("dve", 1),))
        P.wait("t", "dve", P.val("dve"))
        P.wait("t", "epi", P.val("epi"))
        def mms0(e):
            inst = None
            for c in range(NDC):
                inst = e.matmul(self.PSB[0][0:1, 0:512], self.ONESC[:],
                                ZB[:, c * TG:(c + 1) * TG],
                                start=(c == 0), stop=(c == NDC - 1))
            return inst
        P.op("t", mms0, incs=(("pe", 1),))
        P.wait("s", "pe", P.val("pe"))
        P.op("s", lambda e: e.activation(ZB[:], self.scr_f32(0, NDC * TG), AF.Square),
             incs=(("act", 1),))
        P.wait("t", "act", P.val("act"))
        def mms1(e):
            inst = None
            for c in range(NDC):
                inst = e.matmul(self.PSB[1][0:1, 0:512], self.ONESC[:],
                                ZB[:, c * TG:(c + 1) * TG],
                                start=(c == 0), stop=(c == NDC - 1))
            return inst
        P.op("t", mms1, incs=(("pe", 1),))
        P.wait("s", "pe", P.val("pe"))
        P.op("s", lambda e: e.activation(MR2[0:1, :], self.PSB[0][0:1, 0:512],
                                         AF.Identity, scale=1.0 / 1024.0))
        P.op("s", lambda e: e.activation(MR2[32:33, :], self.PSB[1][0:1, 0:512],
                                         AF.Identity, scale=1.0 / 1024.0),
             incs=(("act", 1),))
        P.wait("v", "act", P.val("act"))
        musq2 = self.SCR[32:33, 4096:4608]
        P.op("v", lambda e, m=musq2: e.tensor_mul(m, MR2[0:1, :], MR2[0:1, :]),
             incs=(("dve", 1),))
        P.wait("v", "dve", P.val("dve"))
        P.op("v", lambda e, m=musq2: e.tensor_sub(MR2[32:33, :], MR2[32:33, :], m),
             incs=(("dve", 1),))
        P.wait("s", "dve", P.val("dve"))
        P.op("s", lambda e: e.activation(MR2[32:33, :], MR2[32:33, :], AF.Ln,
                                         bias=self.CPOOL[32:33, self._consts[eps]:self._consts[eps] + 1]))
        P.op("s", lambda e: e.activation(MR2[64:65, :], MR2[32:33, :], AF.Exp,
                                         scale=-0.5), incs=(("act", 1),))
        P.wait("t", "act", P.val("act"))
        P.wait("t", "dve", P.val("dve"))
        def k1(e):
            e.matmul(self.PSB[0][:, :], self.ONESP[0:1, :], MR2[0:1, :],
                     start=True, stop=True)
            return e.matmul(self.PSB[1][:, :], self.ONESP[64:65, :], MR2[64:65, :],
                            start=True, stop=True)
        P.op("t", k1, incs=(("pe", 1),))
        P.wait("v", "pe", P.val("pe"))
        for dc in range(NDC):
            z = self.scr_f32(dc * TG, TG)
            P.op("v", lambda e, z=z: e.tensor_tensor(z, z, self.PSB[0][:, :],
                                                     op=OP.subtract))
            P.op("v", lambda e, z=z: e.tensor_tensor(z, z, self.PSB[1][:, :],
                                                     op=OP.mult))
            if add_to_x:
                xs = self.xr_ap(dc, tg)
                P.op("v", lambda e, z=z, dc=dc:
                     e.tensor_scalar(z, z, gcol[:, dc:dc + 1], bcol[:, dc:dc + 1],
                                     op0=OP.mult, op1=OP.add))
                P.op("v", lambda e, z=z, xs=xs: e.tensor_add(xs, xs, z),
                     incs=(("dve", 1),))
            else:
                P.op("v", lambda e, z=z, dc=dc, tg=tg:
                     e.tensor_scalar(out_fn(dc, tg), z, gcol[:, dc:dc + 1], bcol[:, dc:dc + 1],
                                     op0=OP.mult, op1=OP.add), incs=(("dve", 1),))
        P.wait("t", "dve", P.val("dve"))


    def layer3(self):
        self.layernorm(self.XN)
        self.l3_ffn()
        self.l3_attn()

    def l3_attn(self):
        P, nc = self.P, self.nc
        w_q = self.hbm_in("L3_attn_w_q", (D, D))
        w_k = self.hbm_in("L3_attn_w_k", (D, D))
        w_v = self.hbm_in("L3_attn_w_v", (D, D))
        w_u = self.hbm_in("L3_attn_w_u", (D, D))
        with contextlib.ExitStack() as sc:
            bu, _ = self.load_col_f32(sc, "L3_attn_b_u", D, "bu")
            AOUTF = sc.enter_context(nc.sbuf_tensor(self.nm("aoutf"), [128, NDC * T], BF16))
            self._attn_in = self.XN
            self.mha16(wq_src=(w_q, 0), wk_src=(w_k, 0), wv_src=(w_v, 0),
                       causal=True, scale=1.0 / 64.0, AOUTF=AOUTF)
            with contextlib.ExitStack() as s2:
                wo, wo_t, wv = self.load_w_bf16(s2, w_u, D, D, name="wo")
                P.wait("t", "wg", wv)
                epi0 = P.val("epi")
                idx = 0
                for mc in range(NDC):
                    for qg in range(NTG):
                        sl = idx % 2
                        if idx >= 2:
                            P.wait("t", "epi", epi0 + idx - 1)
                        def wmm(e, mc=mc, qg=qg, sl=sl):
                            inst = None
                            for kc in range(NDC):
                                inst = e.matmul(self.PSB[sl][:, :], wo_t(kc, mc * 128, 128),
                                                AOUTF[:, kc * T + qg * TG: kc * T + (qg + 1) * TG],
                                                start=(kc == 0), stop=(kc == NDC - 1))
                            return inst
                        P.op("t", wmm, incs=(("pe", 1),))
                        P.wait("v", "pe", P.val("pe"))
                        xs = self.xr_ap(mc, qg)
                        P.op("v", lambda e, xs=xs, sl=sl:
                             e.tensor_add(xs, xs, self.PSB[sl][:, :]))
                        P.op("v", lambda e, xs=xs, mc=mc:
                             e.tensor_scalar_add(xs, xs, bu[:, mc:mc + 1]),
                             incs=(("epi", 1),))
                        idx += 1
                P.wait("t", "epi", epi0 + idx)
            P.join()

    def l3_ffn(self):
        P, nc = self.P, self.nc
        E, FF = 8, 4096
        gwt_h = self.hbm_in("L3_ffn_gate_W_t", (D, E))
        Wg_h = self.hbm_in("L3_ffn_W_gate_t", (E, D, FF))
        Wu_h = self.hbm_in("L3_ffn_W_up_t", (E, D, FF))
        Wd_h = self.hbm_in("L3_ffn_W_down_t", (E, FF, D))
        esel_h = self.hbm_in("c_esel8", (8, 8 * 128))
        cc_in = self.nc.dram_tensor(self.nm("cc_in"), [128, 128], F32, kind="Internal").ap()
        cc_out = self.nc.dram_tensor(self.nm("cc_out"), [128, 128], F32, kind="Internal").ap()
        with contextlib.ExitStack() as sc:
            sbl = lambda n_, shp, dt=F32: sc.enter_context(nc.sbuf_tensor(self.nm(n_), list(shp), dt))
            LA = sbl("la", [128, NTT * E])
            ES = sbl("es", [128, NTT * E])
            TMP3 = sbl("tmp3", [128, NTT * E])
            IN1 = sbl("in1", [128, NTT * E])
            RMX = sbl("rmx", [128, 32])
            CSROW = sbl("csrow", [1, 128])
            CLROW = sbl("clrow", [1, 8])
            WB = sbl("wb", [128, NTT * E], BF16)
            WSELF = sbl("wself", [8, T], BF16)
            ESEL = sbl("esel", [8, 8 * 128], BF16)
            ONESCF = sbl("onescf", [128, 1])
            WGB = [sbl("wgb", [128, NDC * 512], BF16) for _ in range(2)]
            WUB = [sbl("wub", [128, NDC * 512], BF16) for _ in range(2)]
            WDB = [sbl("wdb", [128, 4 * D], BF16) for _ in range(2)]
            PH = self.PSB
            ptb_scope = contextlib.ExitStack()
            PTB = ptb_scope.enter_context(nc.psum_tensor(self.nm("ptb"), [128, 128], BF16))
            dma(P, "g", lambda: ESEL[:, :], lambda: esel_h[:, :], "wg")
            gwt, gwt_t, wv = self.load_w_bf16(sc, gwt_h, D, E, name="gwt")
            P.wait("t", "wg", wv)
            P.op("v", lambda e: e.memset(CSROW[0:1, :], 0.0))
            P.op("v", lambda e: e.memset(ONESCF[:], 1.0), incs=(("dve", 1),))

            for tt in range(NTT):
                sl = tt % 2
                if tt >= 2:
                    P.wait("t", "dve", P.val("dve"))
                def gmm(e, tt=tt, sl=sl):
                    inst = None
                    for kc in range(NDC):
                        inst = e.matmul(self.PSB[sl][:, 0:E],
                                        self.XN[:, kc * T + tt * 128: kc * T + (tt + 1) * 128],
                                        gwt_t(kc), start=(kc == 0), stop=(kc == NDC - 1))
                    return inst
                P.op("t", gmm, incs=(("pe", 1),))
                P.wait("v", "pe", P.val("pe"))
                P.op("v", lambda e, tt=tt, sl=sl:
                     e.tensor_scalar_mul(LA[:, tt * E:(tt + 1) * E], self.PSB[sl][:, 0:E],
                                         -20.0), incs=(("dve", 1),))
            la3 = LA[:].rearrange("p (t e) -> p t e", e=E)
            es3 = ES[:].rearrange("p (t e) -> p t e", e=E)
            for it in range(20):
                P.wait("v", "dve", P.val("dve"))
                P.wait("v", "act", P.val("act"))
                P.op("v", lambda e: e.reduce_max(RMX[:, 0:16], la3, axis=mybir.AxisListType.X),
                     incs=(("dve", 1),))
                P.wait("v", "dve", P.val("dve"))
                rb = RMX[:, 0:16].rearrange("p t -> p t ()").broadcast_to([128, 16, E])
                P.op("v", lambda e: e.tensor_tensor(es3, la3, rb, op=OP.subtract),
                     incs=(("dve", 1),))
                P.wait("s", "dve", P.val("dve"))
                P.op("s", lambda e: e.activation(ES[:], ES[:], AF.Exp), incs=(("act", 1),))
                P.wait("v", "act", P.val("act"))
                P.op("v", lambda e: e.reduce_sum(RMX[:, 16:32], es3, axis=mybir.AxisListType.X),
                     incs=(("dve", 1),))
                P.wait("s", "dve", P.val("dve"))
                P.op("s", lambda e: e.activation(RMX[:, 16:32], RMX[:, 16:32], AF.Ln),
                     incs=(("act", 1),))
                P.wait("v", "act", P.val("act"))
                P.op("v", lambda e: e.tensor_add(RMX[:, 0:16], RMX[:, 0:16], RMX[:, 16:32]),
                     incs=(("dve", 1),))
                P.wait("v", "dve", P.val("dve"))
                P.op("v", lambda e: e.tensor_tensor(la3, la3, rb, op=OP.subtract),
                     incs=(("dve", 1),))
                P.wait("s", "dve", P.val("dve"))
                P.op("s", lambda e: e.activation(ES[:], LA[:], AF.Exp), incs=(("act", 1),))
                P.wait("t", "act", P.val("act"))
                P.wait("t", "dve", P.val("dve"))
                P.op("t", lambda e: e.matmul(self.PSB[0][0:1, 0:128], ONESCF[:], ES[:],
                                             start=True, stop=True), incs=(("pe", 1),))
                P.wait("v", "pe", P.val("pe"))
                csum3 = CSROW[0:1, 0:128].rearrange("o (t e) -> o e t", e=16) if False else None
                P.op("v", lambda e: e.reduce_sum(
                    CSROW[0:1, 0:8],
                    self.PSB[0][0:1, 0:128].rearrange("o (t e) -> o e t", e=E),
                    axis=mybir.AxisListType.X), incs=(("dve", 1),))
                P.wait("y", "dve", P.val("dve"))
                dma(P, "y", lambda: cc_in[0:1, :], lambda: CSROW[0:1, :], "ws")
                P.wait("g", "ws", P.val("ws"))
                P.op("g", lambda e: e.collective_compute(
                    "AllReduce", OP.add, replica_groups=[list(range(8))],
                    ins=[cc_in[:, :].opt()], outs=[cc_out[:, :].opt()]),
                     incs=(("gp", 1),))
                P.wait("y", "gp", P.val("gp"))
                dma(P, "y", lambda: CLROW[0:1, :], lambda: cc_out[0:1, 0:8], "ws")
                P.wait("s", "ws", P.val("ws"))
                P.op("s", lambda e: e.activation(CLROW[0:1, :], CLROW[0:1, :], AF.Ln),
                     incs=(("act", 1),))
                P.wait("t", "act", P.val("act"))
                P.op("t", lambda e: e.matmul(self.PSB[1][:, 0:8], self.ONESP[0:1, :],
                                             CLROW[0:1, :], start=True, stop=True),
                     incs=(("pe", 1),))
                P.wait("v", "pe", P.val("pe"))
                clb = self.PSB[1][:, 0:8].rearrange("p e -> p () e").broadcast_to([128, 16, E])
                P.op("v", lambda e: e.tensor_tensor(la3, la3, clb, op=OP.subtract),
                     incs=(("dve", 1),))
            P.wait("s", "dve", P.val("dve"))
            P.op("s", lambda e: e.activation(ES[:], LA[:], AF.Exp), incs=(("act", 1),))
            P.wait("v", "act", P.val("act"))
            P.op("v", lambda e: e.reduce_max(RMX[:, 0:16], es3, axis=mybir.AxisListType.X),
                 incs=(("dve", 1),))
            P.wait("v", "dve", P.val("dve"))
            v1b = RMX[:, 0:16].rearrange("p t -> p t ()").broadcast_to([128, 16, E])
            in13 = IN1[:].rearrange("p (t e) -> p t e", e=E)
            tm3 = TMP3[:].rearrange("p (t e) -> p t e", e=E)
            P.op("v", lambda e: e.tensor_tensor(in13, es3, v1b, op=OP.is_ge))
            P.op("v", lambda e: e.scalar_tensor_tensor(TMP3[:], IN1[:], -1e9, ES[:],
                                                       op0=OP.mult, op1=OP.add),
                 incs=(("dve", 1),))
            P.wait("v", "dve", P.val("dve"))
            P.op("v", lambda e: e.reduce_max(RMX[:, 16:32], tm3, axis=mybir.AxisListType.X),
                 incs=(("dve", 1),))
            P.wait("v", "dve", P.val("dve"))
            v2b = RMX[:, 16:32].rearrange("p t -> p t ()").broadcast_to([128, 16, E])
            P.op("v", lambda e: e.tensor_tensor(tm3, es3, v2b, op=OP.is_ge))
            P.op("v", lambda e: e.tensor_sub(TMP3[:], TMP3[:], IN1[:]))
            pi0b = es3[:, :, 0:1].broadcast_to([128, 16, E])
            pi1b = es3[:, :, 1:2].broadcast_to([128, 16, E])
            P.op("v", lambda e: e.tensor_tensor(in13, in13, pi0b, op=OP.mult))
            P.op("v", lambda e: e.tensor_tensor(tm3, tm3, pi1b, op=OP.mult))
            P.op("v", lambda e: e.tensor_add(WB[:], IN1[:], TMP3[:]), incs=(("dve", 1),))
            P.wait("t", "dve", P.val("dve"))
            for tt in range(NTT):
                P.wait("t", "act", P.val("act"))
                P.op("t", lambda e, tt=tt:
                     e.transpose(PTB[0:E, 0:128], WB[:, tt * E:(tt + 1) * E],
                                 self.IDENT[:]), incs=(("pe", 1),))
                P.wait("s", "pe", P.val("pe"))
                P.op("s", lambda e, tt=tt:
                     e.copy(WSELF[0:E, tt * 128:(tt + 1) * 128], PTB[0:E, 0:128]),
                     incs=(("act", 1),))
            P.join()
            ptb_scope.close()
            PU = [sc.enter_context(nc.psum_tensor(self.nm("pu"), [128, 512], F32))
                  for _ in range(2)]
            PD = [sc.enter_context(nc.psum_tensor(self.nm("pd"), [128, 512], F32))
                  for _ in range(3)]
            PWB = sc.enter_context(nc.psum_tensor(self.nm("pwb"), [128, 512], F32))

            wg_pe = {}
            HBS = sc.enter_context(nc.sbuf_tensor(self.nm("hbs"), [128, 4 * TG], BF16))
            blk = 0
            for ei in range(E):
                for fh in range(2):
                    for fcg in range(4):
                        bsel = blk % 2
                        if blk - 2 in wg_pe:
                            P.wait("g", "pe", wg_pe[blk - 2])
                        f0 = fh * 2048 + fcg * 512
                        dma(P, "g", lambda b=WGB[bsel]: b[:].rearrange("p (k m) -> p k m", m=512),
                            lambda ei=ei, f0=f0: Wg_h[ei][:, f0:f0 + 512]
                            .rearrange("(k p) m -> p k m", p=128), "wg")
                        dma(P, "g", lambda b=WUB[bsel]: b[:].rearrange("p (k m) -> p k m", m=512),
                            lambda ei=ei, f0=f0: Wu_h[ei][:, f0:f0 + 512]
                            .rearrange("(k p) m -> p k m", p=128), "wg")
                        dma(P, "g", lambda b=WDB[bsel]: b[:].rearrange("p (k m) -> p k m", m=D),
                            lambda ei=ei, f0=f0: Wd_h[ei][f0:f0 + 512, :]
                            .rearrange("(k p) m -> p k m", p=128), "wg")
                        wv1 = P.val("wg")
                        for tg in range(NTG):
                            P.wait("t", "dve", P.val("dve"))
                            P.op("t", lambda e, ei=ei, tg=tg:
                                 e.matmul(PWB[:, :], ESEL[:, ei * 128:(ei + 1) * 128],
                                          WSELF[0:E, tg * TG:(tg + 1) * TG],
                                          start=True, stop=True), incs=(("pe", 1),))
                            wbc_pe = P.val("pe")
                            hdone = {}
                            for lf in range(4):
                                sl = lf % 2
                                P.wait("t", "wg", wv1)
                                if sl in hdone:
                                    P.wait("t", "dve", hdone[sl])
                                def hmm(e, lf=lf, sl=sl, tg=tg, bsel=bsel):
                                    for kc in range(NDC):
                                        e.matmul(PH[sl][:, :],
                                                 WGB[bsel][:, kc * 512 + lf * 128: kc * 512 + (lf + 1) * 128],
                                                 self.xn_ap(kc, tg),
                                                 start=(kc == 0), stop=(kc == NDC - 1))
                                    inst = None
                                    for kc in range(NDC):
                                        inst = e.matmul(PU[sl][:, :],
                                                        WUB[bsel][:, kc * 512 + lf * 128: kc * 512 + (lf + 1) * 128],
                                                        self.xn_ap(kc, tg),
                                                        start=(kc == 0), stop=(kc == NDC - 1))
                                    return inst
                                P.op("t", hmm, incs=(("pe", 1),))
                                P.wait("s", "pe", P.val("pe"))
                                hsl = HBS[:, lf * TG:(lf + 1) * TG]
                                P.op("s", lambda e, hsl=hsl, sl=sl:
                                     e.activation(hsl, PH[sl][:, :], AF.Silu),
                                     incs=(("act", 1),))
                                P.wait("v", "act", P.val("act"))
                                if lf == 0:
                                    P.wait("v", "pe", wbc_pe)
                                P.op("v", lambda e, hsl=hsl, sl=sl:
                                     e.tensor_tensor(hsl, hsl, PU[sl][:, :], op=OP.mult))
                                P.op("v", lambda e, hsl=hsl:
                                     e.tensor_tensor(hsl, hsl, PWB[:, :], op=OP.mult),
                                     incs=(("dve", 1),))
                                hdone[sl] = P.val("dve")
                            zadd = {}
                            for dc in range(NDC):
                                sl = dc % 3
                                if dc == 0:
                                    P.wait("t", "dve", hdone[1])
                                if sl in zadd:
                                    P.wait("t", "dve", zadd[sl])
                                def dmm(e, dc=dc, sl=sl, bsel=bsel):
                                    inst = None
                                    for lf in range(4):
                                        inst = e.matmul(PD[sl][:, :],
                                                        WDB[bsel][:, lf * D + dc * 128: lf * D + (dc + 1) * 128],
                                                        HBS[:, lf * TG:(lf + 1) * TG],
                                                        start=(lf == 0), stop=(lf == 3))
                                    return inst
                                P.op("t", dmm, incs=(("pe", 1),))
                                P.wait("v", "pe", P.val("pe"))
                                xs = self.xr_ap(dc, tg)
                                P.op("v", lambda e, xs=xs, sl=sl:
                                     e.tensor_add(xs, xs, PD[sl][:, :]), incs=(("dve", 1),))
                                zadd[sl] = P.val("dve")
                        wg_pe[blk] = P.val("pe")
                        blk += 1
            P.join()

    def l1_attn(self):
        P, nc = self.P, self.nc
        GH, NHG, NCH = 4, 4, 2
        scale = 1.0 / float(np.sqrt(1040.0))
        Wd_q = self.hbm_in("L1_attn_Wd_q_w", (D, 512))
        Wu_q = self.hbm_in("L1_attn_Wu_q_w", (512, D))
        Wq_r = self.hbm_in("L1_attn_Wq_r_w", (512, D))
        Wk_r = self.hbm_in("L1_attn_Wk_r_w", (D, D))
        Wd_kv = self.hbm_in("L1_attn_Wd_kv_w", (D, 512))
        Wu_k = self.hbm_in("L1_attn_Wu_k_w", (512, D))
        Wu_v = self.hbm_in("L1_attn_Wu_v_w", (512, D))
        Wo = self.hbm_in("L1_attn_Wo_w", (D, D))
        cos1_h = self.hbm_in("c_cos1", (T, 512))
        sin1_h = self.hbm_in("c_sin1", (T, 512))
        ind4_h = self.hbm_in("c_ind4", (128, 8))

        with contextlib.ExitStack() as sc:
            sbl = lambda n_, shp, dt=F32: sc.enter_context(nc.sbuf_tensor(self.nm(n_), list(shp), dt))
            CTQ = sbl("ctq", [128, 4 * T], BF16)
            CTKV = sbl("ctkv", [128, 4 * T], BF16)
            IND4 = sbl("ind4", [128, 8], BF16)
            CS = sbl("cs", [128, 2 * 2 * 128])
            RS = sbl("rs", [128, 256])
            RB = sbl("rb", [128, 256], BF16)
            dma(P, "g", lambda: IND4[:], lambda: ind4_h[:, :], "wg")
            bdq, _ = self.load_col_f32(sc, "L1_attn_Wd_q_b", 512, "bdq")
            bdkv, _ = self.load_col_f32(sc, "L1_attn_Wd_kv_b", 512, "bdkv")
            buq, _ = self.load_col_f32(sc, "L1_attn_Wu_q_b", D, "buq")
            buk, _ = self.load_col_f32(sc, "L1_attn_Wu_k_b", D, "buk")
            bo, _ = self.load_col_f32(sc, "L1_attn_Wo_b", D, "bo")
            buv_r, _ = self.load_row_bf16(sc, self.hbm_in("L1_attn_Wu_v_b", (D,)), D, "buvr")
            bqr_r, _ = self.load_row_bf16(sc, self.hbm_in("L1_attn_Wq_r_b", (D,)), D, "bqrr")
            bkr_r, wv0 = self.load_row_bf16(sc, self.hbm_in("L1_attn_Wk_r_b", (D,)), D, "bkrr")

            with contextlib.ExitStack() as s2:
                PSC = [s2.enter_context(nc.psum_tensor(self.nm("psc"), [128, 512], F32))
                       for _ in range(2)]
                pool4 = [self.PSB[0][:, :], self.PSB[1][:, :], PSC[0][:, :], PSC[1][:, :]]
                wdq, wdq_t, _ = self.load_w_bf16(s2, Wd_q, D, 512, name="wdq")
                wdkv, wdkv_t, wv1 = self.load_w_bf16(s2, Wd_kv, D, 512, name="wdkv")
                P.wait("t", "wg", wv1)
                self.gemm_fm_to(nkc=NDC, nmc=4, xin_fn=self.xn_ap,
                                wtile_fn=lambda kc, mc: wdq_t(kc, mc * 128, 128),
                                dst_fn=lambda mc, tg: CTQ[:, mc * T + tg * TG: mc * T + (tg + 1) * TG],
                                bias_fn=lambda mc: bdq[:, mc:mc + 1], pool=pool4)
                self.gemm_fm_to(nkc=NDC, nmc=4, xin_fn=self.xn_ap,
                                wtile_fn=lambda kc, mc: wdkv_t(kc, mc * 128, 128),
                                dst_fn=lambda mc, tg: CTKV[:, mc * T + tg * TG: mc * T + (tg + 1) * TG],
                                bias_fn=lambda mc: bdkv[:, mc:mc + 1], pool=pool4)
                P.join()

            for hg in range(NHG):
                with contextlib.ExitStack() as s2:
                    sb2 = lambda n_, shp, dt=F32: s2.enter_context(nc.sbuf_tensor(self.nm(n_), list(shp), dt))
                    KF = sb2("kf", [128, NCH * T], BF16)
                    VT = sb2("vt", [128, NTT * GH * 65], BF16)
                    QF = sb2("qf", [128, NCH * TG], BF16)
                    SQB = self.scr_bf16(0, 2 * NCH * TG)
                    ARAW = self.scr_f32(3072, 512)
                    AOUT = self.scr_bf16(3584, NCH * TG)
                    KMAX = sb2("kmax", [128, 4])
                    NEGMB = sb2("negmb", [128, 1], BF16)
                    NEGM = sb2("negm", [128, GH])
                    NROW = sb2("nrow", [1, GH], BF16)
                    RECR = sb2("recr", [1, 512])
                    ST = [s2.enter_context(nc.psum_tensor(self.nm("st"), [128, 1024], F32))
                          for _ in range(2)]
                    PVP = s2.enter_context(nc.psum_tensor(self.nm("pv"), [128, 512], F32))
                    PTR = [s2.enter_context(nc.psum_tensor(self.nm("ptr"), [128, 128], BF16))
                           for _ in range(1)]
                    PT = [sb2("pt", [128, 1024], BF16) for _ in range(3)]
                    hc = hg * GH * 64

                    ph_a = contextlib.ExitStack()
                    wuk, wuk_t, _ = self.load_w_bf16(ph_a, Wu_k, 512, 256, col0=hc, name="wuk")
                    wuv, wuv_t, _ = self.load_w_bf16(ph_a, Wu_v, 512, 256, col0=hc, name="wuv")
                    wkr, wkr_t, wv2 = self.load_w_bf16(ph_a, Wk_r, D, 256, col0=hc, name="wkr")
                    P.wait("t", "wg", wv2)

                    stpool = [ST[0][:, 0:512], ST[0][:, 512:1024],
                              ST[1][:, 0:512], ST[1][:, 512:1024]]
                    self.gemm_fm_to(nkc=4, nmc=NCH,
                                    xin_fn=lambda kc, tg: CTKV[:, kc * T + tg * TG: kc * T + (tg + 1) * TG],
                                    wtile_fn=lambda kc, mc: wuk_t(kc, mc * 128, 128),
                                    dst_fn=lambda mc, tg: KF[:, mc * T + tg * TG: mc * T + (tg + 1) * TG],
                                    bias_fn=lambda mc: buk[:, hg * NCH + mc: hg * NCH + mc + 1],
                                    pool=stpool)

                    P.op("v", lambda e: e.memset(VT[:], 1.0), incs=(("dve", 1),))
                    P.wait("s", "dve", P.val("dve"))
                    for t2 in range(0, NTT, 4):
                        if t2 > 0:
                            P.wait("t", "act", P.val("act"))
                        bks = [stpool[(t2 + k) % 4] for k in range(4)]
                        def vmm(e, t2=t2, hc=hc, bks=bks):
                            inst = None
                            for k in range(4):
                                tt = t2 + k
                                for kc in range(4):
                                    e.matmul(bks[k][:, 0:GH * 64],
                                             CTKV[:, kc * T + tt * 128: kc * T + (tt + 1) * 128],
                                             wuv_t(kc), start=(kc == 0), stop=False)
                                inst = e.matmul(bks[k][:, 0:GH * 64], self.ONES1B[:],
                                                buv_r[0:1, hc:hc + GH * 64], start=False, stop=True)
                            return inst
                        P.op("t", vmm, incs=(("pe", 1),))
                        P.wait("s", "pe", P.val("pe"))
                        def vevac(e, t2=t2, bks=bks):
                            inst = None
                            for k in range(4):
                                tt = t2 + k
                                vdst = VT[:, tt * GH * 65:(tt + 1) * GH * 65].rearrange(
                                    "p (h f) -> p h f", f=65)[:, :, 0:64]
                                inst = e.copy(vdst, bks[k][:, 0:GH * 64]
                                              .rearrange("p (h f) -> p h f", f=64))
                            return inst
                        P.op("s", vevac, incs=(("act", 1),))
                    P.join(("t", "v", "s"))

                    for tt in range(NTT):
                        sl = tt % 2
                        cslot = tt % 2
                        P.wait("y", "dve", P.val("dve"))
                        dma(P, "y", lambda cslot=cslot: CS[:, cslot * 256: cslot * 256 + 128],
                            lambda tt=tt, hg=hg: cos1_h[tt * 128:(tt + 1) * 128,
                                                        hg * 128:(hg + 1) * 128], "ws")
                        dma(P, "y", lambda cslot=cslot: CS[:, cslot * 256 + 128: cslot * 256 + 256],
                            lambda tt=tt, hg=hg: sin1_h[tt * 128:(tt + 1) * 128,
                                                        hg * 128:(hg + 1) * 128], "ws")
                        ws_v = P.val("ws")
                        P.wait("t", "dve", P.val("dve"))
                        def krmm(e, tt=tt, sl=sl, hc=hc):
                            for kc in range(NDC):
                                e.matmul(self.PSB[sl][:, 0:256],
                                         self.XN[:, kc * T + tt * 128: kc * T + (tt + 1) * 128],
                                         wkr_t(kc), start=(kc == 0), stop=False)
                            return e.matmul(self.PSB[sl][:, 0:256], self.ONES1B[:],
                                            bkr_r[0:1, hc:hc + 256], start=False, stop=True)
                        P.op("t", krmm, incs=(("pe", 1),))
                        P.wait("s", "pe", P.val("pe"))
                        P.op("s", lambda e, sl=sl: e.copy(RS[:, :], self.PSB[sl][:, 0:256]),
                             incs=(("act", 1),))
                        P.wait("v", "act", P.val("act"))
                        P.wait("v", "ws", ws_v)
                        self.rope_tm_l1(RS[:, :], RB[:, :],
                                        CS[:, cslot * 256: cslot * 256 + 128],
                                        CS[:, cslot * 256 + 128: cslot * 256 + 256], npairs=128)
                        for bp in range(2):
                            psl = 0
                            P.wait("t", "dve", P.val("dve"))
                            P.op("t", lambda e, bp=bp, psl=psl:
                                 e.transpose(PTR[0][:, :], RB[:, bp * 128:(bp + 1) * 128],
                                             self.IDENT[:]), incs=(("pe", 1),))
                            P.wait("v", "pe", P.val("pe"))
                            dst = KF[:, bp * T + tt * 128: bp * T + (tt + 1) * 128]
                            P.op("v", lambda e, dst=dst, psl=psl:
                                 e.tensor_tensor(dst, dst, PTR[0][:, :], op=OP.add),
                                 incs=(("dve", 1),))
                    P.join()
                    ph_a.close()
                    wuq, wuq_t, _ = self.load_w_bf16(s2, Wu_q, 512, 256, col0=hc, name="wuq")
                    wqr, wqr_t, _ = self.load_w_bf16(s2, Wq_r, 512, 256, col0=hc, name="wqr")
                    wob, wob_t, wv3 = self.load_w_bf16(s2, Wo, 256, D, row0=hc, name="wo")
                    P.wait("t", "wg", wv3)

                    for qg in range(NTG):
                        P.wait("t", "act", P.val("act"))
                        self.gemm_fm_to(nkc=4, nmc=NCH,
                                        xin_fn=lambda kc, tg, qg=qg: CTQ[:, kc * T + qg * TG: kc * T + (qg + 1) * TG],
                                        wtile_fn=lambda kc, mc: wuq_t(kc, mc * 128, 128),
                                        dst_fn=lambda mc, tg: QF[:, mc * TG:(mc + 1) * TG],
                                        bias_fn=lambda mc: buq[:, hg * NCH + mc: hg * NCH + mc + 1],
                                        tgs=(0,), pool=stpool)
                        for lt in range(4):
                            tt = qg * 4 + lt
                            sl = lt % 2
                            cslot = lt % 2
                            P.wait("y", "dve", P.val("dve"))
                            dma(P, "y", lambda cslot=cslot: CS[:, cslot * 256: cslot * 256 + 128],
                                lambda tt=tt, hg=hg: cos1_h[tt * 128:(tt + 1) * 128,
                                                            hg * 128:(hg + 1) * 128], "ws")
                            dma(P, "y", lambda cslot=cslot: CS[:, cslot * 256 + 128: cslot * 256 + 256],
                                lambda tt=tt, hg=hg: sin1_h[tt * 128:(tt + 1) * 128,
                                                            hg * 128:(hg + 1) * 128], "ws")
                            ws_v = P.val("ws")
                            P.wait("t", "dve", P.val("dve"))
                            def qrmm(e, tt=tt, sl=sl, hc=hc):
                                for kc in range(4):
                                    e.matmul(self.PSB[sl][:, 0:256],
                                             CTQ[:, kc * T + tt * 128: kc * T + (tt + 1) * 128],
                                             wqr_t(kc), start=(kc == 0), stop=False)
                                return e.matmul(self.PSB[sl][:, 0:256], self.ONES1B[:],
                                                bqr_r[0:1, hc:hc + 256], start=False, stop=True)
                            P.op("t", qrmm, incs=(("pe", 1),))
                            P.wait("s", "pe", P.val("pe"))
                            P.op("s", lambda e, sl=sl: e.copy(RS[:, :], self.PSB[sl][:, 0:256]),
                                 incs=(("act", 1),))
                            P.wait("v", "act", P.val("act"))
                            P.wait("v", "ws", ws_v)
                            self.rope_tm_l1(RS[:, :], RB[:, :],
                                            CS[:, cslot * 256: cslot * 256 + 128],
                                            CS[:, cslot * 256 + 128: cslot * 256 + 256], npairs=128)
                            for bp in range(2):
                                psl = 0
                                P.wait("t", "dve", P.val("dve"))
                                P.op("t", lambda e, bp=bp, psl=psl:
                                     e.transpose(PTR[0][:, :], RB[:, bp * 128:(bp + 1) * 128],
                                                 self.IDENT[:]), incs=(("pe", 1),))
                                P.wait("v", "pe", P.val("pe"))
                                dst = QF[:, bp * TG + lt * 128: bp * TG + (lt + 1) * 128]
                                P.op("v", lambda e, dst=dst, psl=psl:
                                     e.tensor_tensor(dst, dst, PTR[0][:, :], op=OP.add),
                                     incs=(("dve", 1),))
                        P.join(("t", "v", "s"))

                        k3_fn = lambda tg: KF[:].rearrange("p (c t) -> p c t", c=NCH)[:, :, tg * TG:(tg + 1) * TG]
                        q3 = QF[:].rearrange("p (c t) -> p c t", c=NCH)
                        self.negm_content(GH, NCH, k3_fn, q3, IND4, KMAX, NEGM, NROW,
                                          NEGMB, ST[1], PTR, SQB, scale, first=(qg == 0))

                        def kf_fn(h, kt):
                            ch, ro = h // 2, 64 * (h % 2)
                            return KF[ro:ro + 64, ch * T + kt * 128: ch * T + (kt + 1) * 128]

                        def qf_fn(h, qg_):
                            ch, ro = h // 2, 64 * (h % 2)
                            return QF[ro:ro + 64, ch * TG:(ch + 1) * TG]

                        def v_fn(h, kt):
                            return VT[:, kt * GH * 65 + h * 65: kt * GH * 65 + (h + 1) * 65]

                        def out_fn(h, qg_):
                            ch, ro = h // 2, 64 * (h % 2)
                            return AOUT[ro:ro + 64, ch * TG:(ch + 1) * TG]

                        self.attn_core(heads=range(GH), parts=[(kf_fn, qf_fn, 64)],
                                       v_slot_fn=v_fn,
                                       negm_ap_fn=lambda h: NEGM[:, h:h + 1],
                                       scale=scale, causal=False,
                                       out_ap_fn=out_fn, qgs=(qg,),
                                       ST=ST, PV=PVP, PT=PT, araw=ARAW, recr=RECR)

                        epi0 = P.val("epi")
                        for mc in range(NDC):
                            sl = mc % 2
                            if mc >= 2:
                                P.wait("t", "epi", epi0 + mc - 1)
                            def wmm(e, mc=mc, sl=sl):
                                inst = None
                                for kc in range(NCH):
                                    inst = e.matmul(self.PSB[sl][:, :], wob_t(kc, mc * 128, 128),
                                                    AOUT[:, kc * TG:(kc + 1) * TG],
                                                    start=(kc == 0), stop=(kc == NCH - 1))
                                return inst
                            P.op("t", wmm, incs=(("pe", 1),))
                            P.wait("v", "pe", P.val("pe"))
                            xs = self.xr_ap(mc, qg)
                            if hg == 0:
                                P.op("v", lambda e, xs=xs, sl=sl:
                                     e.tensor_add(xs, xs, self.PSB[sl][:, :]))
                                P.op("v", lambda e, xs=xs, mc=mc:
                                     e.tensor_scalar_add(xs, xs, bo[:, mc:mc + 1]),
                                     incs=(("epi", 1),))
                            else:
                                P.op("v", lambda e, xs=xs, sl=sl:
                                     e.tensor_add(xs, xs, self.PSB[sl][:, :]),
                                     incs=(("epi", 1),))
                        P.wait("t", "epi", epi0 + NDC)
                    P.join()
            P.join()

    def l1_ffn(self):
        P, nc = self.P, self.nc
        E, FP = 16, 256
        wg_h = self.hbm_in("L1_ffn_wg", (D, E))
        w1_h = self.hbm_in("L1_ffn_w1", (E, D, FP))
        w2_h = self.hbm_in("L1_ffn_w2", (E, FP, D))
        b2_h = self.hbm_in("L1_ffn_b2", (E, D))
        tri_h = self.hbm_in("c_tri", (128, 128))
        gidx_h = self.hbm_in("c_gidx", (128, 16))
        esel_h = self.hbm_in("c_esel16", (16, 16 * 128))
        with contextlib.ExitStack() as sc:
            sbl = lambda n_, shp, dt=F32: sc.enter_context(nc.sbuf_tensor(self.nm(n_), list(shp), dt))
            TRI = sbl("tri", [128, 128], BF16)
            GIDX = sbl("gidx", [128, 16])
            SC_ = sbl("sc", [128, 256])
            POSB = sbl("posb", [128, 256], BF16)
            CUMS = sbl("cums", [128, 256])
            SEL = sbl("sel", [128, 256])
            SELB = sbl("selb", [128, 256], BF16)
            TMP = sbl("tmp", [128, 256])
            V12 = sbl("v12", [128, 4])
            ROWS = sbl("rows", [1, 3 * 256])
            MSEL = sbl("msel", [16, T], BF16)
            ESEL = sbl("esel", [16, 16 * 128], BF16)
            MR2 = sbl("mr2", [65, 512])
            ZB = sbl("zb", [128, NDC * TG], BF16)
            ZQ = sbl("zq", [128, NDC * TG], BF16)
            HALL = sbl("hall", [128, E * 2 * TG], BF16)
            GAM, _ = self.load_col_f32(sc, "L1_ffn_gamma", D, "gam")
            BET, _ = self.load_col_f32(sc, "L1_ffn_beta", D, "bet")
            B1C, _ = self.load_col_f32_shaped(sc, "L1_ffn_b1_pf", (128, 2 * E), "b1c")
            wgt, wgt_t, _ = self.load_w_bf16(sc, wg_h, D, E, name="wg")
            bgr, _ = self.load_row_bf16(sc, self.hbm_in("L1_ffn_bg", (E,)), E, "bgr")
            b2sb = sc.enter_context(nc.sbuf_tensor(self.nm("b2sb"), [16, D], BF16))
            dma(P, "g", lambda: b2sb[:, :], lambda: b2_h[:, :], "wg")
            dma(P, "g", lambda: TRI[:], lambda: tri_h[:, :], "wg")
            dma(P, "g", lambda: ESEL[:, :], lambda: esel_h[:, :], "wg")
            dma(P, "g", lambda: GIDX[:], lambda: gidx_h[:, :], "wg")
            wv = P.val("wg")
            P.wait("t", "wg", wv)
            PH = [sc.enter_context(nc.psum_tensor(self.nm("ph"), [128, 512], F32))
                  for _ in range(2)]
            PD = [sc.enter_context(nc.psum_tensor(self.nm("pd"), [128, 512], F32))
                  for _ in range(2)]
            PTB = sc.enter_context(nc.psum_tensor(self.nm("ptb"), [128, 128], BF16))

            for tt in range(NTT):
                sl = tt % 2
                if tt >= 2:
                    P.wait("t", "dve", P.val("dve"))
                def gmm(e, tt=tt, sl=sl):
                    for kc in range(NDC):
                        e.matmul(self.PSB[sl][:, 0:E],
                                 self.XN[:, kc * T + tt * 128: kc * T + (tt + 1) * 128],
                                 wgt_t(kc), start=(kc == 0), stop=False)
                    return e.matmul(self.PSB[sl][:, 0:E], self.ONES1B[:],
                                    bgr[0:1, :], start=False, stop=True)
                P.op("t", gmm, incs=(("pe", 1),))
                P.wait("v", "pe", P.val("pe"))
                P.op("v", lambda e, sl=sl: e.reduce_max(V12[:, 0:1], self.PSB[sl][:, 0:E],
                                                        axis=mybir.AxisListType.X),
                     incs=(("dve", 1),))
                P.wait("v", "dve", P.val("dve"))
                P.op("v", lambda e: e.tensor_scalar_mul(V12[:, 1:2], V12[:, 0:1], -1.0),
                     incs=(("dve", 1),))
                P.wait("s", "dve", P.val("dve"))
                P.op("s", lambda e, sl=sl, tt=tt:
                     e.activation(SC_[:, tt * E:(tt + 1) * E], self.PSB[sl][:, 0:E], AF.Exp,
                                  bias=V12[:, 1:2], accum_out=V12[:, 2:3]),
                     incs=(("act", 1),))
                P.wait("v", "act", P.val("act"))
                P.op("v", lambda e: e.reciprocal(V12[:, 3:4], V12[:, 2:3]), incs=(("dve", 1),))
                P.wait("v", "dve", P.val("dve"))
                P.op("v", lambda e, tt=tt:
                     e.tensor_scalar(SC_[:, tt * E:(tt + 1) * E], SC_[:, tt * E:(tt + 1) * E],
                                     V12[:, 3:4], None, op0=OP.mult), incs=(("dve", 1),))
            sc3 = SC_[:].rearrange("p (t e) -> p t e", e=E)
            P.wait("v", "dve", P.val("dve"))
            P.op("v", lambda e: e.reduce_max(CUMS[:, 0:16], sc3, axis=mybir.AxisListType.X),
                 incs=(("dve", 1),))
            P.wait("v", "dve", P.val("dve"))
            v1b = CUMS[:, 0:16].rearrange("p t -> p t ()").broadcast_to([128, 16, E])
            P.op("v", lambda e: e.tensor_tensor(SEL[:].rearrange("p (t e) -> p t e", e=E),
                                                sc3, v1b, op=OP.is_ge))
            P.op("v", lambda e: e.scalar_tensor_tensor(TMP[:], SEL[:], -1e9, SC_[:],
                                                       op0=OP.mult, op1=OP.add),
                 incs=(("dve", 1),))
            P.wait("v", "dve", P.val("dve"))
            P.op("v", lambda e: e.reduce_max(CUMS[:, 16:32],
                                             TMP[:].rearrange("p (t e) -> p t e", e=E),
                                             axis=mybir.AxisListType.X), incs=(("dve", 1),))
            P.wait("v", "dve", P.val("dve"))
            v2b = CUMS[:, 16:32].rearrange("p t -> p t ()").broadcast_to([128, 16, E])
            P.op("v", lambda e: e.tensor_tensor(POSB[:].rearrange("p (t e) -> p t e", e=E),
                                                sc3, v2b, op=OP.is_ge), incs=(("dve", 1),))
            P.wait("t", "dve", P.val("dve"))
            P.op("t", lambda e: e.matmul(self.PSB[0][0:1, 0:256], self.ONESC[:], POSB[:],
                                         start=True, stop=True), incs=(("pe", 1),))
            P.wait("s", "pe", P.val("pe"))
            P.op("s", lambda e: e.copy(ROWS[0:1, 0:256], self.PSB[0][0:1, 0:256]),
                 incs=(("act", 1),))
            P.wait("v", "act", P.val("act"))
            P.op("v", lambda e: e.memset(ROWS[0:1, 256:272], 0.0), incs=(("dve", 1),))
            for tt in range(1, 16):
                P.op("v", lambda e, tt=tt:
                     e.tensor_add(ROWS[0:1, 256 + tt * 16:256 + (tt + 1) * 16],
                                  ROWS[0:1, 256 + (tt - 1) * 16:256 + tt * 16],
                                  ROWS[0:1, (tt - 1) * 16: tt * 16]), incs=(("dve", 1),))
            P.op("v", lambda e: e.tensor_add(ROWS[0:1, 512:528], ROWS[0:1, 496:512],
                                             ROWS[0:1, 240:256]))
            P.op("v", lambda e: e.tensor_scalar(ROWS[0:1, 512:528], ROWS[0:1, 512:528],
                                                -1.0, 1024.0, op0=OP.mult, op1=OP.add),
                 incs=(("dve", 1),))
            P.wait("v", "dve", P.val("dve"))
            rbud = ROWS[0:1, 512:528].rearrange("o e -> o () e").broadcast_to([1, 16, E])
            P.op("v", lambda e: e.tensor_tensor(ROWS[0:1, 256:512].rearrange("o (t e) -> o t e", e=E),
                                                ROWS[0:1, 256:512].rearrange("o (t e) -> o t e", e=E),
                                                rbud, op=OP.add), incs=(("dve", 1),))
            P.wait("t", "dve", P.val("dve"))
            def cmm(e):
                e.matmul(self.PSB[1][:, 0:256], TRI[:], POSB[:], start=True, stop=True)
                return e.matmul(self.PSB[0][:, 0:256], self.ONESP[0:1, :],
                                ROWS[0:1, 256:512], start=True, stop=True)
            P.op("t", cmm, incs=(("pe", 1),))
            P.wait("s", "pe", P.val("pe"))
            P.op("s", lambda e: e.copy(CUMS[:, :], self.PSB[1][:, 0:256]), incs=(("act", 1),))
            P.wait("v", "act", P.val("act"))
            gb = GIDX[:].rearrange("p t -> p t ()").broadcast_to([128, 16, E])
            P.op("v", lambda e: e.tensor_tensor(TMP[:].rearrange("p (t e) -> p t e", e=E), gb,
                                                CUMS[:].rearrange("p (t e) -> p t e", e=E),
                                                op=OP.subtract))
            P.op("v", lambda e: e.tensor_add(TMP[:], TMP[:], POSB[:]))
            P.op("v", lambda e: e.tensor_tensor(TMP[:], TMP[:], self.PSB[0][:, 0:256],
                                                op=OP.is_lt))
            P.op("v", lambda e: e.tensor_scalar(SEL[:], POSB[:], -1.0, 1.0,
                                                op0=OP.mult, op1=OP.add))
            P.op("v", lambda e: e.tensor_mul(SEL[:], SEL[:], TMP[:]))
            P.op("v", lambda e: e.tensor_tensor(SELB[:], SEL[:], POSB[:], op=OP.add),
                 incs=(("dve", 1),))
            P.wait("t", "dve", P.val("dve"))
            for tt in range(NTT):
                P.wait("t", "act", P.val("act"))
                P.op("t", lambda e, tt=tt:
                     e.transpose(PTB[0:16, 0:128],
                                 SELB[:, tt * 16:(tt + 1) * 16], self.IDENT[:]),
                     incs=(("pe", 1),))
                P.wait("s", "pe", P.val("pe"))
                P.op("s", lambda e, tt=tt:
                     e.copy(MSEL[0:16, tt * 128:(tt + 1) * 128],
                            PTB[0:16, 0:128]), incs=(("act", 1),))
            P.join()

            for tg in range(NTG):
                self.bcast_tg(tg)
                gelu_done = {}
                mask_done = 0
                for ei in range(E):
                    with contextlib.ExitStack() as s3:
                        w1e, w1e_t, wv1 = self.load_w_bf16(s3, w1_h, D, FP, row0=0, name="w1e",
                                                           hbm_pre=lambda a, ei=ei: a[ei])
                        P.wait("t", "wg", wv1)
                        for fc in range(2):
                            sl = fc
                            if sl in gelu_done:
                                P.wait("t", "act", gelu_done[sl])
                            def hmm(e, fc=fc, sl=sl, tg=tg):
                                inst = None
                                for kc in range(NDC):
                                    inst = e.matmul(PH[sl][:, :], w1e_t(kc, fc * 128, 128),
                                                    self.xn_ap(kc, tg),
                                                    start=(kc == 0), stop=(kc == NDC - 1))
                                return inst
                            P.op("t", hmm, incs=(("pe", 1),))
                            P.wait("s", "pe", P.val("pe"))
                            hsl = HALL[:, (ei * 2 + fc) * TG:(ei * 2 + fc + 1) * TG]
                            P.op("s", lambda e, hsl=hsl, sl=sl, fc=fc, ei=ei:
                                 e.activation(hsl, PH[sl][:, :], AF.Gelu,
                                              bias=B1C[:, fc * E + ei: fc * E + ei + 1]),
                                 incs=(("act", 1),))
                            gelu_done[sl] = P.val("act")
                        P.wait("t", "dve", mask_done)
                        P.op("t", lambda e, ei=ei, tg=tg:
                             e.matmul(PD[0][:, :], ESEL[:, ei * 128:(ei + 1) * 128],
                                      MSEL[0:16, tg * TG:(tg + 1) * TG],
                                      start=True, stop=True), incs=(("pe", 1),))
                        P.wait("v", "pe", P.val("pe"))
                        P.wait("v", "act", P.val("act"))
                        h2 = HALL[:, ei * 2 * TG:(ei + 1) * 2 * TG].rearrange(
                            "p (c t) -> p c t", c=2)
                        mb = PD[0][:, :].rearrange("p t -> p () t").broadcast_to([128, 2, TG])
                        P.op("v", lambda e, h2=h2, mb=mb:
                             e.tensor_tensor(h2, h2, mb, op=OP.mult), incs=(("dve", 1),))
                        mask_done = P.val("dve")
                h_ready = P.val("dve")
                zadd_done = {}
                for dc in range(NDC):
                    sl = dc % 2
                    with contextlib.ExitStack() as s3:
                        w2c, w2c_t, wv2 = self.load_w_bf16(s3, w2_h, E * FP, 128,
                                                           col0=dc * 128, name="w2c",
                                                           hbm_pre=lambda a: a.rearrange("e f d -> (e f) d"))
                        P.wait("t", "wg", wv2)
                        if dc == 0:
                            P.wait("t", "dve", h_ready)
                        if sl in zadd_done:
                            P.wait("t", "dve", zadd_done[sl])
                        def dmm(e, dc=dc, sl=sl, tg=tg):
                            for ef in range(E * 2):
                                e.matmul(PD[sl][:, :], w2c_t(ef),
                                         HALL[:, ef * TG:(ef + 1) * TG],
                                         start=(ef == 0), stop=False)
                            return e.matmul(PD[sl][:, :], b2sb[:, dc * 128:(dc + 1) * 128],
                                            MSEL[0:16, tg * TG:(tg + 1) * TG],
                                            start=False, stop=True)
                        P.op("t", dmm, incs=(("pe", 1),))
                        P.wait("v", "pe", P.val("pe"))
                        zdst = self.scr_f32(dc * TG, TG)
                        self.xn32_into(zdst, dc, tg)
                        P.op("v", lambda e, zdst=zdst, sl=sl:
                             e.tensor_add(zdst, zdst, PD[sl][:, :]), incs=(("dve", 1),))
                        zadd_done[sl] = P.val("dve")
                P.wait("v", "dve", P.val("dve"))
                P.op("v", lambda e: e.tensor_copy(ZB[:], self.scr_f32(0, NDC * TG)),
                     incs=(("dve", 1),))
                P.op("s", lambda e: e.activation(ZQ[:], self.scr_f32(0, NDC * TG), AF.Square),
                     incs=(("act", 1),))
                P.wait("t", "dve", P.val("dve"))
                P.wait("t", "act", P.val("act"))
                for which, src in ((0, ZB), (1, ZQ)):
                    def mms(e, src=src, which=which):
                        inst = None
                        for c in range(NDC):
                            inst = e.matmul(self.PSB[which][0:1, 0:512], self.ONESC[:],
                                            src[:, c * TG:(c + 1) * TG],
                                            start=(c == 0), stop=(c == NDC - 1))
                        return inst
                    P.op("t", mms, incs=(("pe", 1),))
                P.wait("s", "pe", P.val("pe"))
                P.op("s", lambda e: e.activation(MR2[0:1, :], self.PSB[0][0:1, 0:512],
                                                 AF.Identity, scale=1.0 / 1024.0))
                P.op("s", lambda e: e.activation(MR2[32:33, :], self.PSB[1][0:1, 0:512],
                                                 AF.Identity, scale=1.0 / 1024.0),
                     incs=(("act", 1),))
                P.wait("v", "act", P.val("act"))
                musq2 = self.SCR[32:33, 4096:4608]
                P.op("v", lambda e, m=musq2: e.tensor_mul(m, MR2[0:1, :], MR2[0:1, :]),
                     incs=(("dve", 1),))
                P.wait("v", "dve", P.val("dve"))
                P.op("v", lambda e, m=musq2: e.tensor_sub(MR2[32:33, :], MR2[32:33, :], m),
                     incs=(("dve", 1),))
                P.wait("s", "dve", P.val("dve"))
                P.op("s", lambda e: e.activation(MR2[32:33, :], MR2[32:33, :], AF.Ln,
                                                 bias=self.CPOOL[32:33, self._consts[LN_EPS]:self._consts[LN_EPS] + 1]))
                P.op("s", lambda e: e.activation(MR2[64:65, :], MR2[32:33, :], AF.Exp,
                                                 scale=-0.5), incs=(("act", 1),))
                P.wait("t", "act", P.val("act"))
                P.wait("t", "dve", P.val("dve"))
                def k1(e):
                    e.matmul(self.PSB[0][:, :], self.ONESP[0:1, :], MR2[0:1, :],
                             start=True, stop=True)
                    return e.matmul(self.PSB[1][:, :], self.ONESP[64:65, :], MR2[64:65, :],
                                    start=True, stop=True)
                P.op("t", k1, incs=(("pe", 1),))
                P.wait("v", "pe", P.val("pe"))
                for dc in range(NDC):
                    z = self.scr_f32(dc * TG, TG)
                    xs = self.xr_ap(dc, tg)
                    P.op("v", lambda e, z=z: e.tensor_tensor(z, z, self.PSB[0][:, :],
                                                             op=OP.subtract))
                    P.op("v", lambda e, z=z: e.tensor_tensor(z, z, self.PSB[1][:, :],
                                                             op=OP.mult))
                    P.op("v", lambda e, z=z, dc=dc:
                         e.tensor_scalar(z, z, GAM[:, dc:dc + 1], BET[:, dc:dc + 1],
                                         op0=OP.mult, op1=OP.add))
                    P.op("v", lambda e, z=z, xs=xs: e.tensor_add(xs, xs, z),
                         incs=(("dve", 1),))
                P.wait("t", "dve", P.val("dve"))
            P.join()

    def mha16(self, *, wq_src, wk_src, wv_src, bq_col=None, bk_col=None, bv_row=None,
              causal, scale, AOUTF, name="mha"):
        P, nc = self.P, self.nc
        GH, NHG, NCH = 4, 4, 2
        ind4_h = self.hbm_in("c_ind4", (128, 8))
        with contextlib.ExitStack() as sc:
            sb0 = lambda n_, shp, dt=F32: sc.enter_context(nc.sbuf_tensor(self.nm(n_), list(shp), dt))
            IND4 = sb0("ind4", [128, 8], BF16)
            dma(P, "g", lambda: IND4[:], lambda: ind4_h[:, :], "wg")
            for hg in range(NHG):
                hc = hg * 256
                with contextlib.ExitStack() as s2:
                    sb2 = lambda n_, shp, dt=F32: s2.enter_context(nc.sbuf_tensor(self.nm(n_), list(shp), dt))
                    KF = sb2("kf", [128, NCH * T], BF16)
                    VT = sb2("vt", [128, NTT * GH * 65], BF16)
                    QF = sb2("qf", [128, NCH * TG], BF16)
                    SQB = self.scr_bf16(0, 2 * NCH * TG)
                    ARAW = self.scr_f32(3072, 512)
                    ARAW2 = self.scr_f32(2048, 512)
                    KMAX = sb2("kmax", [128, 4])
                    NEGMB = sb2("negmb", [128, 32], BF16)
                    NEGM = sb2("negm", [128, GH])
                    NEGMP = sb2("negmp", [128, 2])
                    NROW = sb2("nrow", [1, GH], BF16)
                    RECR = self.scr_f32(3584, 512, parts=1)
                    RECR2 = self.scr_f32(1024, 512, parts=1)
                    ST = [s2.enter_context(nc.psum_tensor(self.nm("st"), [128, 1024], F32))
                          for _ in range(2)]
                    PVE = s2.enter_context(nc.psum_tensor(self.nm("pve"), [128, 512], F32))
                    PVO = s2.enter_context(nc.psum_tensor(self.nm("pvo"), [128, 512], F32))
                    PT = [sb2("pt", [128, 1024], BF16) for _ in range(3)]

                    wq, wq_t, _ = self.load_w_bf16(s2, wq_src[0], D, 256,
                                                   col0=wq_src[1] + hc, name="wq")
                    wk, wk_t, _ = self.load_w_bf16(s2, wk_src[0], D, 256,
                                                   col0=wk_src[1] + hc, name="wk")
                    wv, wv_t, wva = self.load_w_bf16(s2, wv_src[0], D, 256,
                                                    col0=wv_src[1] + hc, name="wv")
                    P.wait("t", "wg", wva)

                    stpool = [ST[0][:, 0:512], ST[0][:, 512:1024],
                              ST[1][:, 0:512], ST[1][:, 512:1024]]
                    self.gemm_fm_to(nkc=NDC, nmc=NCH, xin_fn=self.xn_src,
                                    wtile_fn=lambda kc, mc: wk_t(kc, mc * 128, 128),
                                    dst_fn=lambda mc, tg: KF[:, mc * T + tg * TG: mc * T + (tg + 1) * TG],
                                    bias_fn=(None if bk_col is None else
                                             (lambda mc: bk_col[:, hg * NCH + mc: hg * NCH + mc + 1])),
                                    pool=stpool)
                    P.op("v", lambda e: e.memset(VT[:], 1.0), incs=(("dve", 1),))
                    P.wait("s", "dve", P.val("dve"))
                    for t2 in range(0, NTT, 2):
                        if t2 >= 4:
                            P.wait("t", "act", self._vt_done.get((t2 // 2) % 2, 0))
                        if t2 == 0:
                            self._vt_done = {}
                        bks = [stpool[(t2 + 0) % 4], stpool[(t2 + 1) % 4]]
                        def vmm(e, t2=t2, hc=hc, bks=bks):
                            inst = None
                            for k in range(2):
                                tt = t2 + k
                                for kc in range(NDC):
                                    inst = e.matmul(bks[k][:, 0:GH * 64],
                                                    self.xn_tile(kc, tt),
                                                    wv_t(kc), start=(kc == 0),
                                                    stop=(kc == NDC - 1 and bv_row is None))
                                if bv_row is not None:
                                    inst = e.matmul(bks[k][:, 0:GH * 64], self.ONES1B[:],
                                                    bv_row[0:1, hc:hc + GH * 64],
                                                    start=False, stop=True)
                            return inst
                        P.op("t", vmm, incs=(("pe", 1),))
                        P.wait("s", "pe", P.val("pe"))
                        def vevac(e, t2=t2, bks=bks):
                            inst = None
                            for k in range(2):
                                tt = t2 + k
                                vdst = VT[:, tt * GH * 65:(tt + 1) * GH * 65].rearrange(
                                    "p (h f) -> p h f", f=65)[:, :, 0:64]
                                inst = e.copy(vdst, bks[k][:, 0:GH * 64]
                                              .rearrange("p (h f) -> p h f", f=64))
                            return inst
                        P.op("s", vevac, incs=(("act", 1),))
                        self._vt_done[(t2 // 2) % 2] = P.val("act")
                    P.join(("t", "v", "s"))

                    for qg in range(NTG):
                        P.wait("t", "act", P.val("act"))
                        self.gemm_fm_to(nkc=NDC, nmc=NCH,
                                        xin_fn=lambda kc, tg, qg=qg: self.xn_src(kc, qg),
                                        wtile_fn=lambda kc, mc: wq_t(kc, mc * 128, 128),
                                        dst_fn=lambda mc, tg: QF[:, mc * TG:(mc + 1) * TG],
                                        bias_fn=(None if bq_col is None else
                                                 (lambda mc: bq_col[:, hg * NCH + mc: hg * NCH + mc + 1])),
                                        tgs=(0,), pool=stpool)
                        k3_fn = lambda tg: KF[:].rearrange("p (c t) -> p c t", c=NCH)[:, :, tg * TG:(tg + 1) * TG]
                        q3 = QF[:].rearrange("p (c t) -> p c t", c=NCH)
                        self.negm_content(GH, NCH, k3_fn, q3, IND4, KMAX, NEGM, NROW,
                                          NEGMB, ST[1][:, 0:512], None, SQB, scale,
                                          first=(qg == 0))
                        P.wait("v", "act", P.val("act"))
                        nm3 = NEGM[:].rearrange("p (c two) -> p c two", two=2)
                        P.op("v", lambda e, nm3=nm3:
                             e.tensor_tensor(NEGMP[:, 0:2], nm3[:, :, 0:1].rearrange("p c o -> p (c o)"),
                                             nm3[:, :, 1:2].rearrange("p c o -> p (c o)"),
                                             op=OP.min), incs=(("dve", 1),))
                        P.wait("s", "dve", P.val("dve"))

                        def v_fn(h, kt):
                            return VT[:, kt * GH * 65 + h * 65: kt * GH * 65 + (h + 1) * 65]

                        def out_fn(h, qg_, hg=hg, qg2=qg):
                            gch = hg * NCH + h // 2
                            ro = 64 * (h % 2)
                            return AOUTF[ro:ro + 64, gch * T + qg2 * TG:(gch * T + (qg2 + 1) * TG)]

                        self.attn_core_paired(
                            npairs=NCH,
                            kf_even=lambda hp, kt: KF[0:64, hp * T + kt * 128: hp * T + (kt + 1) * 128],
                            kf_odd=lambda hp, kt: KF[64:128, hp * T + kt * 128: hp * T + (kt + 1) * 128],
                            qf_even=lambda hp, qg2=qg: QF[0:64, hp * TG:(hp + 1) * TG],
                            qf_odd=lambda hp, qg2=qg: QF[64:128, hp * TG:(hp + 1) * TG],
                            v_slot_fn=v_fn,
                            negmp_ap_fn=lambda hp: NEGMP[:, hp:hp + 1],
                            scale=scale, causal=causal, out_ap_fn=out_fn, qg=qg,
                            ST=ST, PVE=PVE, PVO=PVO, PT=PT,
                            araw_e=ARAW, araw_o=ARAW2, recr_e=RECR, recr_o=RECR2)
                    P.join()
            P.join()

    def xn_src(self, kc, tg):
        return self._attn_in[:, kc * T + tg * TG: kc * T + (tg + 1) * TG]

    def xn_tile(self, kc, tt):
        return self._attn_in[:, kc * T + tt * 128: kc * T + (tt + 1) * 128]

    def gemm_fm_to(self, *, nkc, nmc, xin_fn, wtile_fn, dst_fn, tgs=None,
                   bias_fn=None, n=TG, pool=None):
        P = self.P
        tgs = list(range(NTG)) if tgs is None else list(tgs)
        tiles = [(mc, tg) for mc in range(nmc) for tg in tgs]
        if pool is None:
            pool = [self.PSB[0][:, 0:n], self.PSB[1][:, 0:n]]
        npool = len(pool)
        per = max(1, npool // 2)
        done = {}
        slot = 0
        for g0 in range(0, len(tiles), per):
            grp = tiles[g0:g0 + per]
            banks = []
            for k in range(len(grp)):
                b = (slot + k) % npool
                banks.append(pool[b])
                if b in done:
                    P.wait("t", "act", done[b])

            def mms(e, grp=grp, banks=banks):
                inst = None
                for (mc, tg), bk in zip(grp, banks):
                    for kc in range(nkc):
                        inst = e.matmul(bk, wtile_fn(kc, mc), xin_fn(kc, tg),
                                        start=(kc == 0), stop=(kc == nkc - 1))
                return inst
            P.op("t", mms, incs=(("pe", 1),))
            P.wait("s", "pe", P.val("pe"))

            def evacs(e, grp=grp, banks=banks):
                inst = None
                for (mc, tg), bk in zip(grp, banks):
                    if bias_fn is None:
                        inst = e.copy(dst_fn(mc, tg), bk)
                    else:
                        inst = e.activation(dst_fn(mc, tg), bk, AF.Identity,
                                            bias=bias_fn(mc))
                return inst
            P.op("s", evacs, incs=(("act", 1),))
            for k in range(len(grp)):
                done[(slot + k) % npool] = P.val("act")
            slot = (slot + len(grp)) % npool
        P.wait("t", "act", P.val("act"))


def _stage_inputs(inputs):
    d = dict(_host_constants())
    d["L0_ffn_w_gate_t"] = np.ascontiguousarray(inputs["L0_ffn_w_gate"].T)
    d["L0_ffn_w_up_t"] = np.ascontiguousarray(inputs["L0_ffn_w_up"].T)
    d["L0_ffn_w_down_t"] = np.ascontiguousarray(inputs["L0_ffn_w_down"].T)
    d["L2_attn_in_proj_t"] = np.ascontiguousarray(inputs["L2_attn_in_proj_weight"].T)
    d["L2_attn_out_proj_t"] = np.ascontiguousarray(inputs["L2_attn_out_proj_weight"].T)
    b1l2 = inputs["L2_ffn_b1"]
    d["L2_ffn_b1_pf"] = np.ascontiguousarray(
        b1l2.T.reshape(16, 128, 8).transpose(1, 0, 2).reshape(128, 128))
    b1 = inputs["L1_ffn_b1"]
    d["L1_ffn_b1_pf"] = np.ascontiguousarray(
        b1.T.reshape(2, 128, 16).transpose(1, 0, 2).reshape(128, 32))
    d["L3_ffn_gate_W_t"] = np.ascontiguousarray(inputs["L3_ffn_gate_W"].T)
    d["L3_ffn_W_gate_t"] = np.ascontiguousarray(inputs["L3_ffn_W_gate"].transpose(0, 2, 1))
    d["L3_ffn_W_up_t"] = np.ascontiguousarray(inputs["L3_ffn_W_up"].transpose(0, 2, 1))
    d["L3_ffn_W_down_t"] = np.ascontiguousarray(inputs["L3_ffn_W_down"].transpose(0, 2, 1))
    skip = {"L0_ffn_w_gate", "L0_ffn_w_up", "L0_ffn_w_down", "L0_attn_x",
            "L2_attn_key", "L2_attn_value", "L2_attn_in_proj_weight",
            "L2_attn_out_proj_weight", "L3_ffn_gate_W", "L3_ffn_W_gate",
            "L3_ffn_W_up", "L3_ffn_W_down"}
    for k, v in inputs.items():
        if k not in skip:
            d[k] = np.ascontiguousarray(v)
        if v.ndim == 1 and v.size % 128 == 0:
            d[k + "_pf"] = np.ascontiguousarray(v.reshape(-1, 128).T)
    return d


def kernel(**inputs):
    x = inputs["L0_attn_x"]
    staged = _stage_inputs(inputs)
    xT = [np.ascontiguousarray(x[c].T) for c in range(8)]

    groups = LAUNCH_GROUPS if LAUNCH_GROUPS is not None else [DO_LAYERS]
    cur = xT
    for g in groups:
        b = Build(g)
        nc = b.build()
        in_maps = []
        for c in range(8):
            m = {}
            for name in b.inputs:
                m[name] = cur[c] if name == "xin" else staged[name]
            in_maps.append(m)
        res = run_bass_kernel_spmd(nc, in_maps, core_ids=list(range(8)))
        cur = [np.asarray(res.results[c]["xout"]) for c in range(8)]
        for name in b.outputs:
            if name.startswith("tap_"):
                DEBUG_RESULTS[name[4:]] = [np.asarray(res.results[c][name]) for c in range(8)]
    out = np.stack([np.ascontiguousarray(cur[c].T) for c in range(8)], axis=0)
    return out.astype(np.float32)
